# version 05
# baseline (speedup 1.0000x reference)
import numpy as np
from contextlib import ExitStack

import concourse.bass as bass
import concourse.mybir as mybir
from concourse.bass_utils import run_bass_kernel_spmd
from concourse.alu_op_type import AluOpType as ALU

F32 = mybir.dt.float32
BF16 = mybir.dt.bfloat16
AF = mybir.ActivationFunctionType
AX = mybir.AxisListType.X

B, S, H, NH, DH = 2, 4096, 2048, 16, 128
HPC = 4
TOK = 1024
KC = H // 128
SCALE = float(1.0 / np.sqrt(DH))
NEG = -1.0e30

ENGINES = ("sync", "tensor", "vector", "scalar", "gpsimd")


class Ctr:
    __slots__ = ("h", "n")

    def __init__(self, h):
        self.h = h
        self.n = 0

    def inc(self, k=1):
        self.n += k
        return self.n


def _w(inst, h, val):
    if val > 0:
        inst.wait_op(h, val, "sem-ge", check=False)
    return inst


def _collapse(deps):
    best = {}
    order = []
    for h, v in deps:
        if v is None or v <= 0:
            continue
        key = id(h)
        if key not in best:
            best[key] = [h, v]
            order.append(key)
        elif v > best[key][1]:
            best[key][1] = v
    return [tuple(best[k]) for k in order]


def build_nc():
    nc = bass.Bass("TRN2", target_bir_lowering=False)

    xb = nc.dram_tensor("xb", [S, H], F32, kind="ExternalInput").ap()
    xown = nc.dram_tensor("xown", [TOK, H], F32, kind="ExternalInput").ap()
    wqkv = nc.dram_tensor("wqkv", [H, 1536], F32, kind="ExternalInput").ap()
    bq = nc.dram_tensor("bq", [128, 4], F32, kind="ExternalInput").ap()
    bk = nc.dram_tensor("bk", [128, 4], F32, kind="ExternalInput").ap()
    bv = nc.dram_tensor("bv", [128, 512], F32, kind="ExternalInput").ap()
    jrow = nc.dram_tensor("jrow", [HPC, 128, S], F32, kind="ExternalInput").ap()
    mtri = nc.dram_tensor("mtri", [128, 128], F32, kind="ExternalInput").ap()
    iden = nc.dram_tensor("iden", [128, 128], BF16, kind="ExternalInput").ap()
    wo = nc.dram_tensor("wo", [H, H], F32, kind="ExternalInput").ap()
    bob = nc.dram_tensor("bob", [128, H], F32, kind="ExternalInput").ap()
    gwt = nc.dram_tensor("gwt", [H, 16], F32, kind="ExternalInput").ap()
    wgu = nc.dram_tensor("wgu", [16, H, 11, 256], F32, kind="ExternalInput").ap()
    wdn = nc.dram_tensor("wdn", [16, 1408, H], F32, kind="ExternalInput").ap()
    sgu = nc.dram_tensor("sgu", [H, 22, 256], F32, kind="ExternalInput").ap()
    sdn = nc.dram_tensor("sdn", [2816, H], F32, kind="ExternalInput").ap()
    out = nc.dram_tensor("out", [TOK, H], F32, kind="ExternalOutput").ap()

    dq = nc.dram_tensor("dq", [512, S], BF16, kind="Internal").ap()
    dk = nc.dram_tensor("dk", [512, S], BF16, kind="Internal").ap()
    dv = nc.dram_tensor("dv", [S, 512], BF16, kind="Internal").ap()
    cin = nc.dram_tensor("cin", [512, S], BF16, kind="Internal").ap()
    cout = nc.dram_tensor("cout", [4, 512, S], BF16, kind="Internal").ap()
    cin3 = nc.dram_tensor("cin3", [4, H, TOK], BF16, kind="Internal").ap()
    cfin = nc.dram_tensor("cfin", [H, TOK], BF16, kind="Internal").ap()
    ydr = nc.dram_tensor("ydr", [TOK, H], F32, kind="Internal").ap()

    with ExitStack() as top:
        def SEM(name):
            return Ctr(top.enter_context(nc.semaphore(name)))

        vO, sO, gO, tP, cc = SEM("vO"), SEM("sO"), SEM("gO"), SEM("tP"), SEM("ccs")
        fV, fS, fG, fT, fSY = SEM("fV"), SEM("fS"), SEM("fG"), SEM("fT"), SEM("fSY")
        dX = [SEM(f"dX{i}") for i in range(4)]
        dW = SEM("dW")
        dB = SEM("dB")
        dI = SEM("dI")
        dQs, dKs, dVs = SEM("dQs"), SEM("dKs"), SEM("dVs")
        dH = [SEM("dH0"), SEM("dH1")]
        dJ = SEM("dJ")
        dO = [SEM("dO0"), SEM("dO1")]
        dPsl = [SEM("dP0"), SEM("dP1")]
        dD = SEM("dD")
        dY = SEM("dY")
        dR = SEM("dR")
        dWo = SEM("dWo")
        dOr = SEM("dOr")
        dU = [SEM("dU0"), SEM("dU1")]

        epsT = top.enter_context(nc.sbuf_tensor("epsT", [128, 1], F32))
        finscr = top.enter_context(nc.sbuf_tensor("finscr", [128, 1], F32))
        idsb = top.enter_context(nc.sbuf_tensor("idsb", [128, 128], BF16))

        ops = {name: [] for name in ENGINES}

        def E(name, fn):
            ops[name].append(fn)

        def run_block():
            with nc.Block() as blk:
                for name in ENGINES:
                    if ops[name]:
                        def runner(eng, fns=tuple(ops[name])):
                            for f in fns:
                                f(eng)
                        getattr(blk, name)(runner)
            for name in ENGINES:
                ops[name] = []

        def mkchain(ordc, fanc):
            def plan(*deps):
                evs = _collapse([(ordc.h, ordc.n)] + list(deps))
                if len(evs) > 1:
                    fanc.inc(len(evs))
                    P = (tuple(evs), (fanc.h, fanc.n))
                else:
                    P = ((), evs[0] if evs else None)
                ordc.inc(1)
                return P

            def emit(eng, P, builder):
                for hh, vv in P[0]:
                    _w(eng.sem_inc(fanc.h, 1), hh, vv)
                i = builder()
                if P[1] is not None:
                    _w(i, P[1][0], P[1][1])
                i.then_inc(ordc.h, 1)
                return i
            return plan, emit

        pV, eV = mkchain(vO, fV)
        pS, eS = mkchain(sO, fS)
        pG, eG = mkchain(gO, fG)

        def mkgrp(fanc):
            def plan(*deps):
                evs = _collapse(list(deps))
                if len(evs) > 1:
                    fanc.inc(len(evs))
                    return (tuple(evs), (fanc.h, fanc.n))
                return ((), evs[0] if evs else None)

            def pre(eng, P):
                for hh, vv in P[0]:
                    _w(eng.sem_inc(fanc.h, 1), hh, vv)

            def wt(inst, P):
                if P[1] is not None:
                    _w(inst, P[1][0], P[1][1])
                return inst
            return plan, pre, wt
        pT_, eTpre, eTw = mkgrp(fT)
        pSY, eSYpre, eSYw = mkgrp(fSY)
        pGD, eGDpre, eGDw = mkgrp(fG)

        def ldid(eng):
            eng.dma_start(out=idsb[:], in_=iden[:, :]).then_inc(dI.h, 16)
        idv = dI.inc(16)
        E("gpsimd", ldid)

        Pe0 = pV()

        def setup(eng, Pe0=Pe0):
            eV(eng, Pe0, lambda: eng.memset(epsT[:], 1e-5))
        epsv = vO.n
        E("vector", setup)

        with ExitStack() as ph:
            pb = [ph.enter_context(nc.psum_tensor(f"pb{i}", [128, 512], F32))
                  for i in range(6)]
            tps1 = ph.enter_context(nc.psum_tensor("tps1", [128, 8, 128], BF16))
            xt1 = ph.enter_context(nc.sbuf_tensor("xt1", [128, 4, H], F32))
            scr1 = ph.enter_context(nc.sbuf_tensor("scr1", [128, H], F32))
            stats1 = ph.enter_context(nc.sbuf_tensor("stats1", [128, 4, 8], F32))
            xnb1 = ph.enter_context(nc.sbuf_tensor("xnb1", [128, 2, H], BF16))
            xnT1 = ph.enter_context(nc.sbuf_tensor("xnT1", [128, KC, 1024], BF16))
            wqs = ph.enter_context(nc.sbuf_tensor("wqs", [128, KC, 1536], BF16))
            bqT = ph.enter_context(nc.sbuf_tensor("bqT", [128, 4], F32))
            bkT = ph.enter_context(nc.sbuf_tensor("bkT", [128, 4], F32))
            bvT = ph.enter_context(nc.sbuf_tensor("bvT", [128, 512], F32))
            qev = ph.enter_context(nc.sbuf_tensor("qev", [128, 4, 512], BF16))
            kev = ph.enter_context(nc.sbuf_tensor("kev", [128, 4, 512], BF16))
            vev = ph.enter_context(nc.sbuf_tensor("vev", [128, 4, 512], BF16))

            def ldw(eng):
                eng.dma_start(out=wqs[:],
                              in_=wqkv[:, :].rearrange("(c p) n -> p c n", p=128)
                              ).then_inc(dW.h, 16)
            wqv = dW.inc(16)
            E("gpsimd", ldw)

            def ldb(eng):
                eng.dma_start(out=bqT[:], in_=bq[:, :]).then_inc(dB.h, 16)
                eng.dma_start(out=bkT[:], in_=bk[:, :]).then_inc(dB.h, 16)
                eng.dma_start(out=bvT[:], in_=bv[:, :]).then_inc(dB.h, 16)
            blv = dB.inc(48)
            E("scalar", ldb)

            vmm_l = []
            apv_l, stv_l1 = [], []
            thv_l = []
            evv_l = []
            st_l = []
            vmm_evac_l = []

            for g in range(8):
                for t in range(4 * g, 4 * g + 4):
                    sl = t % 4
                    bsl = t % 2
                    Pld = pSY((vO.h, apv_l[t - 4]) if t >= 4 else (vO.h, 0))

                    def ld(eng, t=t, sl=sl, Pld=Pld):
                        eSYpre(eng, Pld)
                        i = eng.dma_start(out=xt1[:, sl, :],
                                          in_=xb[t * 128:(t + 1) * 128, :])
                        eSYw(i, Pld)
                        i.then_inc(dX[sl].h, 16)
                    ldv = dX[sl].inc(16)
                    E("sync", ld)

                    ldep = (dX[sl].h, ldv)
                    sdep = (sO.h, stv_l1[t - 4]) if t >= 4 else (sO.h, 0)
                    P1 = pV(ldep, sdep)
                    P2 = pV(ldep)
                    P3, P4, P5, P6 = pV(), pV(), pV(), pV()
                    lnv = vO.n

                    def stats_fn(eng, sl=sl, Ps=(P1, P2, P3, P4, P5, P6)):
                        eV(eng, Ps[0], lambda: eng.tensor_reduce(
                            stats1[:, sl, 0:1], xt1[:, sl, :], axis=AX, op=ALU.add))
                        eV(eng, Ps[1], lambda: eng.scalar_tensor_tensor(
                            scr1[:], xt1[:, sl, :], 1.0, xt1[:, sl, :],
                            op0=ALU.mult, op1=ALU.mult,
                            accum_out=stats1[:, sl, 1:2]))
                        eV(eng, Ps[2], lambda: eng.tensor_scalar(
                            stats1[:, sl, 2:3], stats1[:, sl, 0:1], 1.0 / H, None,
                            op0=ALU.mult))
                        eV(eng, Ps[3], lambda: eng.tensor_tensor(
                            stats1[:, sl, 3:4], stats1[:, sl, 2:3],
                            stats1[:, sl, 2:3], op=ALU.mult))
                        eV(eng, Ps[4], lambda: eng.tensor_scalar(
                            stats1[:, sl, 4:5], stats1[:, sl, 1:2], 1.0 / H, None,
                            op0=ALU.mult))
                        eV(eng, Ps[5], lambda: eng.tensor_tensor(
                            stats1[:, sl, 4:5], stats1[:, sl, 4:5],
                            stats1[:, sl, 3:4], op=ALU.subtract))
                    E("vector", stats_fn)

                    Pst = pS((vO.h, lnv), (vO.h, epsv))

                    def stdf(eng, sl=sl, Pst=Pst):
                        eS(eng, Pst, lambda: eng.activation(
                            stats1[:, sl, 5:6], stats1[:, sl, 4:5], AF.Sqrt,
                            bias=epsT[:, 0:1]))
                    stv = sO.n
                    stv_l1.append(stv)
                    E("scalar", stdf)

                    Pa1 = pV((sO.h, stv),
                             (tP.h, thv_l[2 * (t - 2) + 1]) if t >= 2 else (tP.h, 0))
                    Pa2 = pV()

                    def app(eng, sl=sl, bsl=bsl, Pa1=Pa1, Pa2=Pa2):
                        eV(eng, Pa1, lambda: eng.reciprocal(
                            stats1[:, sl, 6:7], stats1[:, sl, 5:6]))
                        eV(eng, Pa2, lambda: eng.tensor_scalar(
                            xnb1[:, bsl, :], xt1[:, sl, :],
                            stats1[:, sl, 2:3], stats1[:, sl, 6:7],
                            op0=ALU.subtract, op1=ALU.mult))
                    apv = vO.n
                    apv_l.append(apv)
                    E("vector", app)

                    gidx = g % 2
                    for half in range(2):
                        hidx = 2 * t + half
                        tdeps = [(vO.h, apv), (dI.h, idv)]
                        if g >= 2 and t % 4 == 0 and half == 0:
                            tdeps.append((tP.h, vmm_l[g - 2]))
                        if hidx >= 1:
                            tdeps.append((vO.h, evv_l[hidx - 1]))
                        Ptr = pT_(*tdeps)

                        def trf(eng, bsl=bsl, half=half, Ptr=Ptr):
                            eTpre(eng, Ptr)
                            for s8 in range(8):
                                kch = half * 8 + s8
                                i = eng.transpose(tps1[:, s8, :],
                                                  xnb1[:, bsl, kch * 128:(kch + 1) * 128],
                                                  idsb[:])
                                eTw(i, Ptr)
                            i.then_inc(tP.h, 1)
                        thv = tP.inc()
                        thv_l.append(thv)
                        E("tensor", trf)

                        base = gidx * 512 + (t % 4) * 128
                        Pev = [pV((tP.h, thv)) if s8 == 0 else pV()
                               for s8 in range(8)]

                        def evf(eng, half=half, base=base, Pev=tuple(Pev)):
                            for s8 in range(8):
                                kch = half * 8 + s8
                                eV(eng, Pev[s8], lambda kch=kch, s8=s8: eng.tensor_copy(
                                    xnT1[:, kch, base:base + 128], tps1[:, s8, :]))
                        evv_l.append(vO.n)
                        E("vector", evf)

                gs = (g % 2) * 512
                xnrdy = evv_l[2 * (4 * g + 3) + 1]
                qdeps = [(vO.h, xnrdy)]
                if g == 0:
                    qdeps.append((dW.h, wqv))
                else:
                    qdeps.append((vO.h, vmm_evac_l[g - 1][2]))
                Pq = pT_(*qdeps)

                def q_mms(eng, gs=gs, Pq=Pq):
                    eTpre(eng, Pq)
                    for oc in range(4):
                        for kc in range(KC):
                            i = eng.matmul(pb[oc][:],
                                           wqs[:, kc, oc * 128:(oc + 1) * 128],
                                           xnT1[:, kc, gs:gs + 512],
                                           start=(kc == 0), stop=(kc == KC - 1),
                                           skip_group_check=True)
                            eTw(i, Pq)
                    i.then_inc(tP.h, 1)
                mq = tP.inc()
                E("tensor", q_mms)

                k1deps = [(vO.h, xnrdy)]
                if g == 0:
                    k1deps.append((dW.h, wqv))
                else:
                    k1deps.append((vO.h, vmm_evac_l[g - 1][1]))
                Pk1 = pT_(*k1deps)

                def k1_mms(eng, gs=gs, Pk1=Pk1):
                    eTpre(eng, Pk1)
                    for oc in range(2):
                        for kc in range(KC):
                            i = eng.matmul(pb[4 + oc][:],
                                           wqs[:, kc, 512 + oc * 128:512 + (oc + 1) * 128],
                                           xnT1[:, kc, gs:gs + 512],
                                           start=(kc == 0), stop=(kc == KC - 1),
                                           skip_group_check=True)
                            eTw(i, Pk1)
                    i.then_inc(tP.h, 1)
                mk1 = tP.inc()
                E("tensor", k1_mms)

                Pqe = [None] * 6
                Pqe[0] = pV((tP.h, mq), (dB.h, blv),
                            (dQs.h, st_l[g - 1][0]) if g >= 1 else (dQs.h, 0))
                Pqe[1] = pV()
                Pqe[2] = pV()
                Pqe[3] = pV()
                Pqe[4] = pV((tP.h, mk1),
                            (dKs.h, st_l[g - 1][1]) if g >= 1 else (dKs.h, 0))
                Pqe[5] = pV()
                qevv = vO.n - 2
                k1evv = vO.n

                def qk1_ev(eng, Pqe=tuple(Pqe)):
                    for oc in range(4):
                        eV(eng, Pqe[oc], lambda oc=oc: eng.tensor_scalar(
                            qev[:, oc, :], pb[oc][:], bqT[:, oc:oc + 1], None,
                            op0=ALU.add))
                    for oc in range(2):
                        eV(eng, Pqe[4 + oc], lambda oc=oc: eng.tensor_scalar(
                            kev[:, oc, :], pb[4 + oc][:], bkT[:, oc:oc + 1], None,
                            op0=ALU.add))
                E("vector", qk1_ev)

                Pk2 = pT_((vO.h, k1evv))

                def k2_mms(eng, gs=gs, Pk2=Pk2):
                    eTpre(eng, Pk2)
                    for oc in range(2, 4):
                        for kc in range(KC):
                            i = eng.matmul(pb[2 + oc][:],
                                           wqs[:, kc, 512 + oc * 128:512 + (oc + 1) * 128],
                                           xnT1[:, kc, gs:gs + 512],
                                           start=(kc == 0), stop=(kc == KC - 1),
                                           skip_group_check=True)
                            eTw(i, Pk2)
                    i.then_inc(tP.h, 1)
                mk2 = tP.inc()
                E("tensor", k2_mms)

                Pvm = pT_((vO.h, qevv))

                def v_mms(eng, gs=gs, Pvm=Pvm):
                    eTpre(eng, Pvm)
                    for tt in range(4):
                        for kc in range(KC):
                            i = eng.matmul(pb[tt][:],
                                           xnT1[:, kc, gs + tt * 128:gs + (tt + 1) * 128],
                                           wqs[:, kc, 1024:1536],
                                           start=(kc == 0), stop=(kc == KC - 1),
                                           skip_group_check=True)
                            eTw(i, Pvm)
                    i.then_inc(tP.h, 1)
                mv = tP.inc()
                vmm_l.append(mv)
                E("tensor", v_mms)

                Pk2e = [None] * 6
                Pk2e[0] = pV((tP.h, mk2))
                Pk2e[1] = pV()
                Pk2e[2] = pV((tP.h, mv), (dB.h, blv),
                             (dVs.h, st_l[g - 1][2]) if g >= 1 else (dVs.h, 0))
                Pk2e[3] = pV()
                Pk2e[4] = pV()
                Pk2e[5] = pV()
                k2evv = vO.n - 4
                vevv = vO.n

                def k2v_ev(eng, Pk2e=tuple(Pk2e)):
                    for j, oc in enumerate(range(2, 4)):
                        eV(eng, Pk2e[j], lambda oc=oc: eng.tensor_scalar(
                            kev[:, oc, :], pb[2 + oc][:], bkT[:, oc:oc + 1], None,
                            op0=ALU.add))
                    for tt in range(4):
                        eV(eng, Pk2e[2 + tt], lambda tt=tt: eng.tensor_tensor(
                            vev[:, tt, :], pb[tt][:], bvT[:], op=ALU.add))
                vmm_evac_l.append((qevv, k2evv, vevv))
                E("vector", k2v_ev)

                Pst1 = pGD((vO.h, qevv))
                Pst2 = pGD((vO.h, k2evv))
                Pst3 = pGD((vO.h, vevv))

                def qkv_st(eng, g=g, Pst1=Pst1, Pst2=Pst2, Pst3=Pst3):
                    eGDpre(eng, Pst1)
                    i = eng.dma_start(
                        out=dq[:, :].rearrange("(oc p) t -> p oc t", p=128)[:, :, g * 512:(g + 1) * 512],
                        in_=qev[:])
                    eGDw(i, Pst1)
                    i.then_inc(dQs.h, 16)
                    eGDpre(eng, Pst2)
                    i = eng.dma_start(
                        out=dk[:, :].rearrange("(oc p) t -> p oc t", p=128)[:, :, g * 512:(g + 1) * 512],
                        in_=kev[:])
                    eGDw(i, Pst2)
                    i.then_inc(dKs.h, 16)
                    eGDpre(eng, Pst3)
                    i = eng.dma_start(
                        out=dv[:, :].rearrange("(tt p) c -> p tt c", p=128)[:, g * 4:(g + 1) * 4, :],
                        in_=vev[:])
                    eGDw(i, Pst3)
                    i.then_inc(dVs.h, 16)
                dQs.inc(16); dKs.inc(16); dVs.inc(16)
                st_l.append((dQs.n, dKs.n, dVs.n))
                E("gpsimd", qkv_st)

            Pf = pG((dQs.h, dQs.n), (dKs.h, dKs.n), (dVs.h, dVs.n))

            def fin1a(eng, Pf=Pf):
                eG(eng, Pf, lambda: eng.memset(finscr[0:1, 0:1], 0.0))
            E("gpsimd", fin1a)
            run_block()

        with ExitStack() as ph:
            scb = [ph.enter_context(nc.psum_tensor(f"sb{i}", [128, 512], F32))
                   for i in range(3)]
            avb = ph.enter_context(nc.psum_tensor("avb", [128, 2, 256], F32))
            tps2 = ph.enter_context(nc.psum_tensor("tps2", [128, 8, 128], BF16))
            qh = ph.enter_context(nc.sbuf_tensor("qh", [128, 2, S], BF16))
            kh = ph.enter_context(nc.sbuf_tensor("kh", [128, 2, S], BF16))
            vh = ph.enter_context(nc.sbuf_tensor("vh", [128, 2, 32, 128], BF16))
            jbc = ph.enter_context(nc.sbuf_tensor("jbc", [128, S], F32))
            sc = ph.enter_context(nc.sbuf_tensor("sc", [128, 3, S], F32))
            pg = ph.enter_context(nc.sbuf_tensor("pg", [128, 3, S], BF16))
            pn = ph.enter_context(nc.sbuf_tensor("pn", [128, 3, S], BF16))
            pt = ph.enter_context(nc.sbuf_tensor("pt", [128, 32, 256], BF16))
            oev = ph.enter_context(nc.sbuf_tensor("oev", [128, 2, 256], BF16))
            acols = ph.enter_context(nc.sbuf_tensor("acols", [128, 3, 4], F32))
            mtri_sb = ph.enter_context(nc.sbuf_tensor("mtri_sb", [128, 128], F32))

            def ldmtri(eng):
                eng.dma_start(out=mtri_sb[:], in_=mtri[:, :]).then_inc(dB.h, 16)
            mtv = dB.inc(16)
            E("scalar", ldmtri)

            sev_l, exp_l, pnv_l, ptev_l = [], [], [], []
            avv_l, oevv_l, ost_l = [], [], []
            ck_cnt = 0
            qt_cnt = 0
            qg_cnt = 0

            for h in range(HPC):
                hs = h % 2
                hdeps = []
                if h >= 2:
                    hdeps = [(vO.h, sev_l[144 * (h - 1) - 1]),
                             (tP.h, avv_l[16 * (h - 1) - 1])]
                Plh = pSY(*hdeps)

                def ldh(eng, h=h, hs=hs, Plh=Plh):
                    eSYpre(eng, Plh)
                    i = eng.dma_start(out=qh[:, hs, :], in_=dq[h * 128:(h + 1) * 128, :])
                    eSYw(i, Plh)
                    i.then_inc(dH[hs].h, 16)
                    i = eng.dma_start(out=kh[:, hs, :], in_=dk[h * 128:(h + 1) * 128, :])
                    eSYw(i, Plh)
                    i.then_inc(dH[hs].h, 16)
                    i = eng.dma_start(
                        out=vh[:, hs, :, :],
                        in_=dv[:, :].rearrange("(kt p) c -> p kt c", p=128)[:, :, h * 128:(h + 1) * 128])
                    eSYw(i, Plh)
                    i.then_inc(dH[hs].h, 16)
                qlv = dH[hs].inc(48)
                E("sync", ldh)

                Plj = pGD((vO.h, sev_l[144 * h - 1]) if h >= 1 else (vO.h, 0))

                def ldj(eng, h=h, Plj=Plj):
                    eGDpre(eng, Plj)
                    i = eng.dma_start(out=jbc[:], in_=jrow[h, :, :])
                    eGDw(i, Plj)
                    i.then_inc(dJ.h, 16)
                jlv = dJ.inc(16)
                E("gpsimd", ldj)

                for qt in range(32):
                    Eext = (qt + 1) * 128
                    ssl = qt % 3
                    nck = (Eext + 511) // 512

                    for ck in range(nck):
                        k0 = ck * 512
                        k1 = min(Eext, k0 + 512)
                        bank = ck_cnt % 2
                        mdeps = [(dH[hs].h, qlv)]
                        if ck_cnt >= 2:
                            mdeps.append((vO.h, sev_l[ck_cnt - 2]))
                        Pm = pT_(*mdeps)

                        def smm(eng, hs=hs, qt=qt, k0=k0, k1=k1, bank=bank, Pm=Pm):
                            eTpre(eng, Pm)
                            i = eng.matmul(scb[bank][:, 0:k1 - k0],
                                           qh[:, hs, qt * 128:(qt + 1) * 128],
                                           kh[:, hs, k0:k1], start=True, stop=True,
                                           skip_group_check=True)
                            eTw(i, Pm)
                            i.then_inc(tP.h, 1)
                        mmv = tP.inc()
                        E("tensor", smm)

                        edeps = [(tP.h, mmv), (dJ.h, jlv)]
                        if qt_cnt >= 3 and ck == 0:
                            edeps.append((sO.h, exp_l[qt_cnt - 3]))
                        Pe_ = pV(*edeps)

                        def sev(eng, ssl=ssl, k0=k0, k1=k1, bank=bank, Pe_=Pe_):
                            eV(eng, Pe_, lambda: eng.scalar_tensor_tensor(
                                sc[:, ssl, k0:k1], scb[bank][:, 0:k1 - k0],
                                SCALE, jbc[:, k0:k1], op0=ALU.mult, op1=ALU.add))
                        sev_l.append(vO.n)
                        E("vector", sev)
                        ck_cnt += 1

                    Pn1 = pV((dB.h, mtv))
                    nmv = vO.n

                    def nmx(eng, ssl=ssl, qt=qt, Pn1=Pn1):
                        eV(eng, Pn1, lambda: eng.tensor_tensor(
                            sc[:, ssl, qt * 128:(qt + 1) * 128],
                            sc[:, ssl, qt * 128:(qt + 1) * 128],
                            mtri_sb[:], op=ALU.add))
                    E("vector", nmx)

                    Pgr = pV()
                    grv = vO.n

                    def gmax(eng, ssl=ssl, Eext=Eext, Pgr=Pgr):
                        eV(eng, Pgr, lambda: eng.tensor_reduce(
                            acols[:, ssl, 0:1], sc[:, ssl, 0:Eext],
                            axis=AX, op=ALU.max, negate=True))
                    E("vector", gmax)

                    xdeps = [(vO.h, grv)]
                    Px = pS(*xdeps)
                    exv = sO.n

                    def expf(eng, ssl=ssl, Eext=Eext, Px=Px):
                        eS(eng, Px, lambda: eng.activation(
                            pg[:, ssl, 0:Eext], sc[:, ssl, 0:Eext], AF.Exp,
                            bias=acols[:, ssl, 0:1], accum_out=acols[:, ssl, 1:2]))
                    exp_l.append(exv)
                    E("scalar", expf)

                    if qt % 2 == 1:
                        Pz = pV((tP.h, avv_l[qg_cnt - 1]) if qg_cnt >= 1 else (tP.h, 0))

                        def ptz(eng, nkt=qt + 1, Pz=Pz):
                            eV(eng, Pz, lambda: eng.memset(pt[:, nkt - 1, 0:128], 0.0))
                        E("vector", ptz)

                    Pr1 = pV((sO.h, exv))
                    rcv = vO.n

                    def recf(eng, ssl=ssl, Pr1=Pr1):
                        eV(eng, Pr1, lambda: eng.reciprocal(
                            acols[:, ssl, 2:3], acols[:, ssl, 1:2]))
                    E("vector", recf)

                    mdeps2 = [(vO.h, rcv)]
                    if qt_cnt >= 3:
                        mdeps2.append((vO.h, ptev_l[(qt_cnt - 3) // 2]))
                    Pmu = pS(*mdeps2)
                    pnv = sO.n

                    def pnorm(eng, ssl=ssl, Eext=Eext, Pmu=Pmu):
                        eS(eng, Pmu, lambda: eng.mul(
                            pn[:, ssl, 0:Eext], pg[:, ssl, 0:Eext],
                            acols[:, ssl, 2:3]))
                    pnv_l.append(pnv)
                    E("scalar", pnorm)
                    qt_cnt += 1

                    if qt % 2 == 1:
                        qg = qt // 2
                        nkt = qt + 1
                        osl = qg % 2
                        jobs = [((qt - 1) % 3, 0, kt) for kt in range(qt)] + \
                               [(qt % 3, 1, kt) for kt in range(nkt)]
                        nwav = (len(jobs) + 7) // 8
                        wave_ev = []
                        for w in range(nwav):
                            wjobs = jobs[8 * w:8 * w + 8]
                            wdeps = [(sO.h, pnv_l[-1]), (sO.h, pnv_l[-2] if len(pnv_l) >= 2 else 0), (dI.h, idv)]
                            if qg_cnt >= 1:
                                wdeps.append((tP.h, avv_l[qg_cnt - 1]))
                            if w >= 1:
                                wdeps.append((vO.h, wave_ev[w - 1]))
                            Pw = pT_(*wdeps)

                            def wtr(eng, wjobs=tuple(wjobs), Pw=Pw):
                                eTpre(eng, Pw)
                                for s8, (src, qs, kt) in enumerate(wjobs):
                                    i = eng.transpose(tps2[:, s8, :],
                                                      pn[:, src, kt * 128:(kt + 1) * 128],
                                                      idsb[:])
                                    eTw(i, Pw)
                                i.then_inc(tP.h, 1)
                            wtv = tP.inc()
                            E("tensor", wtr)

                            Pwe = []
                            for s8 in range(len(wjobs)):
                                if s8 == 0:
                                    dd = [(tP.h, wtv)]
                                    if qg_cnt >= 1 and w == 0:
                                        dd.append((tP.h, avv_l[qg_cnt - 1]))
                                    Pwe.append(pV(*dd))
                                else:
                                    Pwe.append(pV())

                            def wev(eng, wjobs=tuple(wjobs), Pwe=tuple(Pwe)):
                                for s8, (src, qs, kt) in enumerate(wjobs):
                                    eV(eng, Pwe[s8], lambda qs=qs, kt=kt, s8=s8:
                                       eng.tensor_copy(pt[:, kt, qs * 128:(qs + 1) * 128],
                                                       tps2[:, s8, :]))
                            wave_ev.append(vO.n)
                            E("vector", wev)
                        ptev = wave_ev[-1]
                        ptev_l.append(ptev)

                        adeps = [(vO.h, ptev), (dH[hs].h, qlv)]
                        if qg_cnt >= 2:
                            adeps.append((vO.h, oevv_l[qg_cnt - 2]))
                        Pa = pT_(*adeps)

                        def avm(eng, hs=hs, nkt=nkt, osl=osl, Pa=Pa):
                            eTpre(eng, Pa)
                            for kt in range(nkt):
                                i = eng.matmul(avb[:, osl, :], vh[:, hs, kt, :],
                                               pt[:, kt, 0:256],
                                               start=(kt == 0), stop=(kt == nkt - 1),
                                               skip_group_check=True)
                                eTw(i, Pa)
                            i.then_inc(tP.h, 1)
                        avv = tP.inc()
                        avv_l.append(avv)
                        E("tensor", avm)

                        odeps = [(tP.h, avv)]
                        if qg_cnt >= 2:
                            odeps.append((dO[qg_cnt % 2].h, ost_l[qg_cnt - 2]))
                        Po = pV(*odeps)

                        def oevf(eng, osl=osl, Po=Po):
                            eV(eng, Po, lambda: eng.tensor_copy(
                                oev[:, osl, :], avb[:, osl, :]))
                        oevv_l.append(vO.n)
                        E("vector", oevf)

                        Pos = pGD((vO.h, oevv_l[-1]))

                        def ost(eng, h=h, qg=qg, osl=osl, Pos=Pos, par=qg_cnt % 2):
                            eGDpre(eng, Pos)
                            i = eng.dma_start(
                                out=cin[:, :].rearrange("(hh p) t -> p hh t", p=128)[:, h, qg * 256:(qg + 1) * 256],
                                in_=oev[:, osl, :])
                            eGDw(i, Pos)
                            i.then_inc(dO[par].h, 16)
                        ostv = dO[qg_cnt % 2].inc(16)
                        ost_l.append(ostv)
                        E("gpsimd", ost)
                        qg_cnt += 1

            Pf = pG((dO[0].h, dO[0].n), (dO[1].h, dO[1].n))

            def fin1b(eng, Pf=Pf):
                eG(eng, Pf, lambda: eng.memset(finscr[0:1, 0:1], 0.0))
            E("gpsimd", fin1b)
            run_block()

        xnT2 = top.enter_context(nc.sbuf_tensor("xnT2", [128, KC, TOK], BF16))
        comb = top.enter_context(nc.sbuf_tensor("comb", [128, 8, 16], F32))
        with ExitStack() as ph:
            gb = [ph.enter_context(nc.psum_tensor(f"gb{i}", [128, 512], F32))
                  for i in range(2)]
            tps3 = ph.enter_context(nc.psum_tensor("tps3", [128, 8, 128], BF16))
            xt2 = ph.enter_context(nc.sbuf_tensor("xt2", [128, 2, H], F32))
            scr2 = ph.enter_context(nc.sbuf_tensor("scr2", [128, H], F32))
            stats2 = ph.enter_context(nc.sbuf_tensor("stats2", [128, 2, 8], F32))
            xnb2 = ph.enter_context(nc.sbuf_tensor("xnb2", [128, 2, H], BF16))
            gw2 = ph.enter_context(nc.sbuf_tensor("gw2", [128, KC, 16], BF16))
            gl = ph.enter_context(nc.sbuf_tensor("gl", [128, 8, 16], F32))
            gcols = ph.enter_context(nc.sbuf_tensor("gcols", [128, 8, 8], F32))

            def agather(eng):
                eng.collective_compute(
                    "AllGather", ALU.bypass,
                    replica_groups=[[0, 1, 2, 3], [4, 5, 6, 7]],
                    ins=[cin[:, :]], outs=[cout[:, :, :]]).then_inc(cc.h, 1)
            cc.inc()
            E("gpsimd", agather)

            apv2, stv2, thv2, evv2 = [], [], [], []
            for t in range(8):
                sl = t % 2
                bsl = t % 2
                Pld = pSY((vO.h, apv2[t - 2]) if t >= 2 else (vO.h, 0))

                def ld(eng, t=t, sl=sl, Pld=Pld):
                    eSYpre(eng, Pld)
                    i = eng.dma_start(out=xt2[:, sl, :],
                                      in_=xown[t * 128:(t + 1) * 128, :])
                    eSYw(i, Pld)
                    i.then_inc(dX[sl].h, 16)
                ldv = dX[sl].inc(16)
                E("sync", ld)

                ldep = (dX[sl].h, ldv)
                sdep = (sO.h, stv2[t - 2]) if t >= 2 else (sO.h, 0)
                P1 = pV(ldep, sdep)
                P2 = pV(ldep)
                P3, P4, P5, P6 = pV(), pV(), pV(), pV()
                lnv = vO.n

                def stats_fn(eng, sl=sl, Ps=(P1, P2, P3, P4, P5, P6)):
                    eV(eng, Ps[0], lambda: eng.tensor_reduce(
                        stats2[:, sl, 0:1], xt2[:, sl, :], axis=AX, op=ALU.add))
                    eV(eng, Ps[1], lambda: eng.scalar_tensor_tensor(
                        scr2[:], xt2[:, sl, :], 1.0, xt2[:, sl, :],
                        op0=ALU.mult, op1=ALU.mult, accum_out=stats2[:, sl, 1:2]))
                    eV(eng, Ps[2], lambda: eng.tensor_scalar(
                        stats2[:, sl, 2:3], stats2[:, sl, 0:1], 1.0 / H, None,
                        op0=ALU.mult))
                    eV(eng, Ps[3], lambda: eng.tensor_tensor(
                        stats2[:, sl, 3:4], stats2[:, sl, 2:3], stats2[:, sl, 2:3],
                        op=ALU.mult))
                    eV(eng, Ps[4], lambda: eng.tensor_scalar(
                        stats2[:, sl, 4:5], stats2[:, sl, 1:2], 1.0 / H, None,
                        op0=ALU.mult))
                    eV(eng, Ps[5], lambda: eng.tensor_tensor(
                        stats2[:, sl, 4:5], stats2[:, sl, 4:5], stats2[:, sl, 3:4],
                        op=ALU.subtract))
                E("vector", stats_fn)

                Pst = pS((vO.h, lnv), (vO.h, epsv))

                def stdf(eng, sl=sl, Pst=Pst):
                    eS(eng, Pst, lambda: eng.activation(
                        stats2[:, sl, 5:6], stats2[:, sl, 4:5], AF.Sqrt,
                        bias=epsT[:, 0:1]))
                stv = sO.n
                stv2.append(stv)
                E("scalar", stdf)

                Pa1 = pV((sO.h, stv),
                         (tP.h, thv2[2 * (t - 2) + 1]) if t >= 2 else (tP.h, 0))
                Pa2 = pV()

                def app(eng, sl=sl, bsl=bsl, Pa1=Pa1, Pa2=Pa2):
                    eV(eng, Pa1, lambda: eng.reciprocal(
                        stats2[:, sl, 6:7], stats2[:, sl, 5:6]))
                    eV(eng, Pa2, lambda: eng.tensor_scalar(
                        xnb2[:, bsl, :], xt2[:, sl, :],
                        stats2[:, sl, 2:3], stats2[:, sl, 6:7],
                        op0=ALU.subtract, op1=ALU.mult))
                apv = vO.n
                apv2.append(apv)
                E("vector", app)

                for half in range(2):
                    hidx = 2 * t + half
                    tdeps = [(vO.h, apv), (dI.h, idv)]
                    if hidx >= 1:
                        tdeps.append((vO.h, evv2[hidx - 1]))
                    Ptr = pT_(*tdeps)

                    def trf(eng, bsl=bsl, half=half, Ptr=Ptr):
                        eTpre(eng, Ptr)
                        for s8 in range(8):
                            kch = half * 8 + s8
                            i = eng.transpose(tps3[:, s8, :],
                                              xnb2[:, bsl, kch * 128:(kch + 1) * 128],
                                              idsb[:])
                            eTw(i, Ptr)
                        i.then_inc(tP.h, 1)
                    thv = tP.inc()
                    thv2.append(thv)
                    E("tensor", trf)

                    Pev = [pV((tP.h, thv)) if s8 == 0 else pV() for s8 in range(8)]

                    def evf(eng, half=half, t=t, Pev=tuple(Pev)):
                        for s8 in range(8):
                            kch = half * 8 + s8
                            eV(eng, Pev[s8], lambda kch=kch, s8=s8: eng.tensor_copy(
                                xnT2[:, kch, t * 128:(t + 1) * 128], tps3[:, s8, :]))
                    evv2.append(vO.n)
                    E("vector", evf)
            xnrdy2 = evv2[-1]

            def ldgw(eng):
                eng.dma_start(out=gw2[:],
                              in_=gwt[:, :].rearrange("(c p) n -> p c n", p=128)
                              ).then_inc(dW.h, 16)
            gwv = dW.inc(16)
            E("gpsimd", ldgw)

            gev_l2 = []
            for tt in range(8):
                gdeps = [(dW.h, gwv), (vO.h, xnrdy2)]
                if tt >= 2:
                    gdeps.append((vO.h, gev_l2[tt - 2]))
                Pg_ = pT_(*gdeps)

                def gmm(eng, tt=tt, Pg_=Pg_):
                    eTpre(eng, Pg_)
                    for kc in range(KC):
                        i = eng.matmul(gb[tt % 2][:, 0:16],
                                       xnT2[:, kc, tt * 128:(tt + 1) * 128],
                                       gw2[:, kc, :], start=(kc == 0),
                                       stop=(kc == KC - 1), skip_group_check=True)
                        eTw(i, Pg_)
                    i.then_inc(tP.h, 1)
                gmv = tP.inc()
                E("tensor", gmm)

                Pg1 = pV((tP.h, gmv))
                Pg2 = pV()
                gev_l2.append(vO.n - 1)
                gevv = vO.n

                def gev(eng, tt=tt, Pg1=Pg1, Pg2=Pg2):
                    eV(eng, Pg1, lambda: eng.tensor_copy(gl[:, tt, :],
                                                         gb[tt % 2][:, 0:16]))
                    eV(eng, Pg2, lambda: eng.tensor_reduce(
                        gcols[:, tt, 0:1], gl[:, tt, :], axis=AX, op=ALU.max,
                        negate=True))
                E("vector", gev)

                Pgx = pS((vO.h, gevv))
                gxv = sO.n

                def gexp(eng, tt=tt, Pgx=Pgx):
                    eS(eng, Pgx, lambda: eng.activation(
                        comb[:, tt, :], gl[:, tt, :], AF.Exp,
                        bias=gcols[:, tt, 0:1], accum_out=gcols[:, tt, 1:2]))
                E("scalar", gexp)

                Pt_ = [pV((sO.h, gxv))] + [pV() for _ in range(7)]

                def gtop(eng, tt=tt, Pt_=tuple(Pt_)):
                    eV(eng, Pt_[0], lambda: eng.reciprocal(
                        gcols[:, tt, 2:3], gcols[:, tt, 1:2]))
                    eV(eng, Pt_[1], lambda: eng.tensor_scalar(
                        gl[:, tt, :], comb[:, tt, :], gcols[:, tt, 2:3], None,
                        op0=ALU.mult))
                    eV(eng, Pt_[2], lambda: eng.tensor_reduce(
                        gcols[:, tt, 3:4], gl[:, tt, :], axis=AX, op=ALU.max))
                    eV(eng, Pt_[3], lambda: eng.tensor_scalar(
                        comb[:, tt, :], gl[:, tt, :], gcols[:, tt, 3:4], None,
                        op0=ALU.is_equal))
                    eV(eng, Pt_[4], lambda: eng.scalar_tensor_tensor(
                        comb[:, tt, :], comb[:, tt, :], -1.0e30, gl[:, tt, :],
                        op0=ALU.mult, op1=ALU.add))
                    eV(eng, Pt_[5], lambda: eng.tensor_reduce(
                        gcols[:, tt, 4:5], comb[:, tt, :], axis=AX, op=ALU.max))
                    eV(eng, Pt_[6], lambda: eng.tensor_scalar(
                        comb[:, tt, :], gl[:, tt, :], gcols[:, tt, 4:5], None,
                        op0=ALU.is_ge))
                    eV(eng, Pt_[7], lambda: eng.tensor_tensor(
                        comb[:, tt, :], comb[:, tt, :], gl[:, tt, :], op=ALU.mult))
                E("vector", gtop)
            combv = vO.n
            run_block()

        with ExitStack() as ph:
            PS = [ph.enter_context(nc.psum_tensor(f"eb{i}", [128, 512], F32))
                  for i in range(8)]
            pslab = ph.enter_context(nc.sbuf_tensor("pslab", [128, 2, KC, 256], BF16))
            dslab = ph.enter_context(nc.sbuf_tensor("dslab", [128, 11, H], BF16))
            hbuf = ph.enter_context(nc.sbuf_tensor("hbuf", [128, 11, TOK], BF16))
            silt = ph.enter_context(nc.sbuf_tensor("silt", [128, 2, 512], BF16))
            ybuf = ph.enter_context(nc.sbuf_tensor("ybuf", [128, 8, H], F32))

            Pyz = pV()

            def yz(eng, Pyz=Pyz):
                eV(eng, Pyz, lambda: eng.memset(ybuf[:], 0.0))
            yz_v = vO.n
            E("vector", yz)

            plist = [(0, e) for e in range(16)] + [(1, 0), (1, 1)]
            gp_cnt = 0
            gu_l, sil_l, hm_l, dn_l, yev_l = [], [], [], [], []

            for pe, (kind, e) in enumerate(plist):
                for p in range(11):
                    psl = gp_cnt % 2
                    src = wgu[e, :, p, :] if kind == 0 else sgu[:, e * 11 + p, :]
                    Ppl = pGD((tP.h, gu_l[gp_cnt - 2]) if gp_cnt >= 2 else (tP.h, 0))

                    def pload(eng, src=src, psl=psl, Ppl=Ppl):
                        eGDpre(eng, Ppl)
                        i = eng.dma_start(out=pslab[:, psl, :, :],
                                          in_=src.rearrange("(c p) n -> p c n", p=128))
                        eGDw(i, Ppl)
                        i.then_inc(dPsl[psl].h, 16)
                    plv = dPsl[psl].inc(16)
                    E("gpsimd", pload)

                    gdeps = [(dPsl[psl].h, plv)]
                    if gp_cnt >= 2:
                        gdeps.append((vO.h, hm_l[gp_cnt - 2]))
                        gdeps.append((sO.h, sil_l[gp_cnt - 2]))
                    if gp_cnt == 0:
                        gdeps.append((vO.h, combv))
                    if pe >= 1 and p <= 1:
                        gdeps.append((vO.h, yev_l[8 * pe - 1]))
                    Pgu = pT_(*gdeps)

                    def gumm(eng, psl=psl, Pgu=Pgu):
                        eTpre(eng, Pgu)
                        for kc in range(KC):
                            for th in range(2):
                                i = eng.matmul(PS[4 * psl + th][:],
                                               pslab[:, psl, kc, 0:128],
                                               xnT2[:, kc, th * 512:(th + 1) * 512],
                                               start=(kc == 0), stop=(kc == KC - 1),
                                               skip_group_check=True)
                                eTw(i, Pgu)
                                i2 = eng.matmul(PS[4 * psl + 2 + th][:],
                                                pslab[:, psl, kc, 128:256],
                                                xnT2[:, kc, th * 512:(th + 1) * 512],
                                                start=(kc == 0), stop=(kc == KC - 1),
                                                skip_group_check=True)
                                eTw(i2, Pgu)
                        i2.then_inc(tP.h, 1)
                    guv = tP.inc()
                    gu_l.append(guv)
                    E("tensor", gumm)

                    sdeps = [(tP.h, guv)]
                    if gp_cnt >= 1:
                        sdeps.append((vO.h, hm_l[gp_cnt - 1]))
                    Ps1 = pS(*sdeps)
                    Ps2 = pS()
                    silv = sO.n

                    def hsil(eng, psl=psl, Ps1=Ps1, Ps2=Ps2):
                        eS(eng, Ps1, lambda: eng.activation(
                            silt[:, 0, :], PS[4 * psl + 0][:], AF.Silu))
                        eS(eng, Ps2, lambda: eng.activation(
                            silt[:, 1, :], PS[4 * psl + 1][:], AF.Silu))
                    sil_l.append(silv)
                    E("scalar", hsil)

                    hdeps = [(sO.h, silv)]
                    if p == 0 and pe >= 1:
                        hdeps.append((tP.h, dn_l[8 * pe - 1]))
                    Ph1 = pV(*hdeps)
                    Ph2 = pV()

                    def hmul(eng, p=p, psl=psl, Ph1=Ph1, Ph2=Ph2):
                        eV(eng, Ph1, lambda: eng.tensor_tensor(
                            hbuf[:, p, 0:512], silt[:, 0, :], PS[4 * psl + 2][:],
                            op=ALU.mult))
                        eV(eng, Ph2, lambda: eng.tensor_tensor(
                            hbuf[:, p, 512:1024], silt[:, 1, :], PS[4 * psl + 3][:],
                            op=ALU.mult))
                    hm_l.append(vO.n)
                    E("vector", hmul)
                    gp_cnt += 1

                Pdl = pGD((tP.h, dn_l[8 * pe - 1]) if pe >= 1 else (tP.h, 0))

                def dload(eng, kind=kind, e=e, Pdl=Pdl):
                    eGDpre(eng, Pdl)
                    for ic in range(11):
                        dsrc = (wdn[e, ic * 128:(ic + 1) * 128, :] if kind == 0
                                else sdn[e * 1408 + ic * 128:e * 1408 + (ic + 1) * 128, :])
                        i = eng.dma_start(out=dslab[:, ic, :], in_=dsrc)
                        eGDw(i, Pdl)
                        i.then_inc(dD.h, 16)
                dslv = dD.inc(16 * 11)
                E("gpsimd", dload)

                hall = hm_l[-1]
                yv_base = len(yev_l)
                dn0 = tP.n
                Py = []
                for tt in range(8):
                    Py.append(pV((tP.h, dn0 + tt + 1)))
                    Py.extend(pV() for _ in range(3))
                    yev_l.append(vO.n)
                Ptt = []
                for tt in range(8):
                    gi = yv_base + tt - 2
                    ydep = (vO.h, yev_l[gi]) if gi >= 0 else (vO.h, yz_v)
                    if tt == 0:
                        Ptt.append(pT_((dD.h, dslv), (vO.h, hall), ydep))
                    else:
                        Ptt.append(pT_(ydep))

                def dnm(eng, Ptt=tuple(Ptt)):
                    for tt in range(8):
                        bs = (tt % 2) * 4
                        eTpre(eng, Ptt[tt])
                        for ic in range(11):
                            for fc in range(4):
                                i = eng.matmul(PS[bs + fc][:],
                                               hbuf[:, ic, tt * 128:(tt + 1) * 128],
                                               dslab[:, ic, fc * 512:(fc + 1) * 512],
                                               start=(ic == 0), stop=(ic == 10),
                                               skip_group_check=True)
                                eTw(i, Ptt[tt])
                        i.then_inc(tP.h, 1)
                for tt in range(8):
                    dn_l.append(dn0 + tt + 1)
                tP.inc(8)
                E("tensor", dnm)

                def yev(eng, kind=kind, e=e, Py=tuple(Py)):
                    pi = 0
                    for tt in range(8):
                        bs = (tt % 2) * 4
                        for fc in range(4):
                            if kind == 0:
                                eV(eng, Py[pi], lambda tt=tt, fc=fc, bs=bs:
                                   eng.scalar_tensor_tensor(
                                       ybuf[:, tt, fc * 512:(fc + 1) * 512],
                                       PS[bs + fc][:], comb[:, tt, e:e + 1],
                                       ybuf[:, tt, fc * 512:(fc + 1) * 512],
                                       op0=ALU.mult, op1=ALU.add))
                            else:
                                eV(eng, Py[pi], lambda tt=tt, fc=fc, bs=bs:
                                   eng.tensor_tensor(
                                       ybuf[:, tt, fc * 512:(fc + 1) * 512],
                                       ybuf[:, tt, fc * 512:(fc + 1) * 512],
                                       PS[bs + fc][:], op=ALU.add))
                            pi += 1
                E("vector", yev)

                if pe == 5:
                    Pr = pSY((cc.h, cc.n))

                    def reshp(eng, Pr=Pr):
                        eSYpre(eng, Pr)
                        for gg in range(4):
                            i = eng.dma_start(
                                out=cin3[:, gg * 512:(gg + 1) * 512, :],
                                in_=cout[gg, :, :].rearrange("r (m t) -> m r t", m=4))
                            eSYw(i, Pr)
                            i.then_inc(dR.h, 16)
                    cpv = dR.inc(64)
                    E("sync", reshp)

                    def rsc(eng, cpv=cpv):
                        i = eng.collective_compute(
                            "ReduceScatter", ALU.max,
                            replica_groups=[[0, 1, 2, 3], [4, 5, 6, 7]],
                            ins=[cin3[:, :, :]], outs=[cfin[:, :]])
                        _w(i, dR.h, cpv)
                        i.then_inc(cc.h, 1)
                    cc.inc()
                    E("gpsimd", rsc)

            yfin = yev_l[-1]
            Pyd = pGD((vO.h, yfin))

            def ydma(eng, Pyd=Pyd):
                eGDpre(eng, Pyd)
                i = eng.dma_start(out=ydr[:, :].rearrange("(t p) c -> p t c", p=128),
                                  in_=ybuf[:])
                eGDw(i, Pyd)
                i.then_inc(dY.h, 16)
            dY.inc(16)
            E("gpsimd", ydma)

            Pf = pG((dY.h, dY.n))

            def fin2(eng, Pf=Pf):
                eG(eng, Pf, lambda: eng.memset(finscr[0:1, 0:1], 0.0))
            E("gpsimd", fin2)
            run_block()

        with ExitStack() as ph:
            PS = [ph.enter_context(nc.psum_tensor(f"ob{i}", [128, 512], F32))
                  for i in range(8)]
            orc = ph.enter_context(nc.sbuf_tensor("orc", [128, KC, TOK], BF16))
            wos = ph.enter_context(nc.sbuf_tensor("wos", [128, KC, H], BF16))
            bos = ph.enter_context(nc.sbuf_tensor("bos", [128, H], F32))
            xo3 = ph.enter_context(nc.sbuf_tensor("xo3", [128, 2, H], F32))
            st3 = ph.enter_context(nc.sbuf_tensor("st3", [128, 2, H], F32))
            gt3 = ph.enter_context(nc.sbuf_tensor("gt3", [128, 2, H], F32))
            yt3 = ph.enter_context(nc.sbuf_tensor("yt3", [128, 2, H], F32))

            def wload(eng):
                for fc in range(KC):
                    eng.dma_start(out=wos[:, fc, :],
                                  in_=wo[fc * 128:(fc + 1) * 128, :]).then_inc(dWo.h, 16)
            wlv = dWo.inc(16 * KC)
            E("gpsimd", wload)

            Pol = pSY((cc.h, cc.n))

            def oload(eng, Pol=Pol):
                eSYpre(eng, Pol)
                i = eng.dma_start(out=orc[:],
                                  in_=cfin[:, :].rearrange("(c p) t -> p c t", p=128))
                eSYw(i, Pol)
                i.then_inc(dOr.h, 16)
                eng.dma_start(out=bos[:], in_=bob[:, :]).then_inc(dOr.h, 16)
            olv = dOr.inc(32)
            E("sync", oload)

            sev_l3, sg_l3, outst_l = [], [], []
            for tt in range(8):
                sl = tt % 2

                xdeps = []
                if tt >= 2:
                    xdeps.append((vO.h, sev_l3[tt - 2]))
                    xdeps.append((sO.h, sg_l3[tt - 2]))
                Pxl = pSY(*xdeps)

                def xload(eng, tt=tt, sl=sl, Pxl=Pxl):
                    eSYpre(eng, Pxl)
                    i = eng.dma_start(out=xo3[:, sl, :],
                                      in_=xown[tt * 128:(tt + 1) * 128, :])
                    eSYw(i, Pxl)
                    i.then_inc(dX[sl].h, 16)
                    i = eng.dma_start(out=yt3[:, sl, :],
                                      in_=ydr[tt * 128:(tt + 1) * 128, :])
                    eSYw(i, Pxl)
                    i.then_inc(dX[sl].h, 16)
                xlv = dX[sl].inc(32)
                E("sync", xload)

                mdeps = [(dWo.h, wlv), (dOr.h, olv)]
                if tt >= 2:
                    mdeps.append((vO.h, sev_l3[tt - 2]))
                Pm3 = pT_(*mdeps)

                def smm3(eng, tt=tt, Pm3=Pm3):
                    eTpre(eng, Pm3)
                    bs = (tt % 2) * 4
                    for fc in range(KC):
                        for oc in range(4):
                            i = eng.matmul(PS[bs + oc][:],
                                           orc[:, fc, tt * 128:(tt + 1) * 128],
                                           wos[:, fc, oc * 512:(oc + 1) * 512],
                                           start=(fc == 0), stop=(fc == KC - 1),
                                           skip_group_check=True)
                            eTw(i, Pm3)
                    i.then_inc(tP.h, 1)
                smv = tP.inc()
                E("tensor", smm3)

                gdeps = [(dX[sl].h, xlv)]
                if tt >= 2:
                    gdeps.append((vO.h, sev_l3[tt - 2]))
                Pgs = pS(*gdeps)
                sgv = sO.n
                sg_l3.append(sgv)

                def gsil(eng, sl=sl, Pgs=Pgs):
                    eS(eng, Pgs, lambda: eng.activation(
                        gt3[:, sl, :], yt3[:, sl, :], AF.Silu))
                E("scalar", gsil)

                cdeps = [(tP.h, smv), (dX[sl].h, xlv), (sO.h, sgv)]
                if tt >= 2:
                    cdeps.append((dU[tt % 2].h, outst_l[tt - 2]))
                Pc = [pV(*cdeps)] + [pV() for _ in range(5)]
                sevv = vO.n

                def sevc(eng, tt=tt, sl=sl, Pc=tuple(Pc)):
                    bs = (tt % 2) * 4
                    for oc in range(4):
                        eV(eng, Pc[oc], lambda oc=oc: eng.tensor_tensor(
                            st3[:, sl, oc * 512:(oc + 1) * 512], PS[bs + oc][:],
                            bos[:, oc * 512:(oc + 1) * 512], op=ALU.add))
                    eV(eng, Pc[4], lambda: eng.tensor_tensor(
                        st3[:, sl, :], st3[:, sl, :], gt3[:, sl, :], op=ALU.mult))
                    eV(eng, Pc[5], lambda: eng.tensor_tensor(
                        st3[:, sl, :], st3[:, sl, :], xo3[:, sl, :], op=ALU.add))
                sev_l3.append(sevv)
                E("vector", sevc)

                Pou = pGD((vO.h, sevv))

                def ostore(eng, tt=tt, sl=sl, Pou=Pou):
                    eGDpre(eng, Pou)
                    i = eng.dma_start(out=out[tt * 128:(tt + 1) * 128, :],
                                      in_=st3[:, sl, :])
                    eGDw(i, Pou)
                    i.then_inc(dU[tt % 2].h, 16)
                ov = dU[tt % 2].inc(16)
                outst_l.append(ov)
                E("gpsimd", ostore)

            Pf = pG((dU[0].h, dU[0].n), (dU[1].h, dU[1].n))

            def finout(eng, Pf=Pf):
                eG(eng, Pf, lambda: eng.memset(finscr[0:1, 0:1], 0.0))
            E("gpsimd", finout)
            run_block()

    return nc


def kernel(L0_attn_x, L0_attn_Wqkv, L0_attn_bqkv, L0_attn_Wo, L0_attn_bo,
           L0_ffn_gate_w, L0_ffn_w_gate_up, L0_ffn_w_down,
           L0_ffn_shared_gate_up, L0_ffn_shared_down):
    import ml_dtypes
    x = np.ascontiguousarray(np.asarray(L0_attn_x, np.float32))
    Wqkv = np.asarray(L0_attn_Wqkv, np.float32)
    bqkv = np.asarray(L0_attn_bqkv, np.float32)
    Wo = np.ascontiguousarray(np.asarray(L0_attn_Wo, np.float32))
    bo = np.asarray(L0_attn_bo, np.float32)
    gate_w = np.asarray(L0_ffn_gate_w, np.float32)
    w_gu = np.asarray(L0_ffn_w_gate_up, np.float32)
    w_dn = np.ascontiguousarray(np.asarray(L0_ffn_w_down, np.float32))
    s_gu = np.asarray(L0_ffn_shared_gate_up, np.float32)
    s_dn = np.ascontiguousarray(np.asarray(L0_ffn_shared_down, np.float32))

    slopes = np.array([1.0 / 2 ** (k / 2) for k in range(NH)], np.float32)
    jj = np.arange(S, dtype=np.float32)

    wgu_p = np.ascontiguousarray(
        w_gu.reshape(16, H, 2, 11, 128).transpose(0, 1, 3, 2, 4).reshape(16, H, 11, 256))
    sgu_p = np.ascontiguousarray(
        s_gu.reshape(H, 2, 22, 128).transpose(0, 2, 1, 3).reshape(H, 22, 256))
    gwt = np.ascontiguousarray(gate_w.T)
    bob = np.ascontiguousarray(np.broadcast_to(bo[None, :], (128, H)))
    ii = np.arange(128, dtype=np.float32)
    mtri = np.ascontiguousarray(
        np.where(np.arange(128)[None, :] <= ii[:, None], np.float32(0.0),
                 np.float32(NEG)))
    iden = np.eye(128, dtype=np.float32).astype(ml_dtypes.bfloat16)

    in_maps = []
    for c in range(8):
        b, hg = c // 4, c % 4
        heads = slice(hg * 512, (hg + 1) * 512)
        wq_slice = np.ascontiguousarray(np.concatenate(
            [Wqkv[:, heads], Wqkv[:, H:][:, heads], Wqkv[:, 2 * H:][:, heads]], axis=1))
        bq_s = np.ascontiguousarray(bqkv[:H][heads].reshape(4, 128).T)
        bk_s = np.ascontiguousarray(bqkv[H:2 * H][heads].reshape(4, 128).T)
        bv_s = np.ascontiguousarray(
            np.broadcast_to(bqkv[2 * H:][heads][None, :], (128, 512)))
        jr = np.ascontiguousarray(np.broadcast_to(
            (slopes[4 * hg:4 * hg + 4, None] * jj[None, :])[:, None, :],
            (4, 128, S)))
        in_maps.append({
            "xb": x[b],
            "xown": np.ascontiguousarray(x[b, hg * TOK:(hg + 1) * TOK]),
            "wqkv": wq_slice,
            "bq": bq_s, "bk": bk_s, "bv": bv_s,
            "jrow": jr, "mtri": mtri, "iden": iden,
            "wo": Wo, "bob": bob,
            "gwt": gwt,
            "wgu": wgu_p, "wdn": w_dn,
            "sgu": sgu_p, "sdn": s_dn,
        })

    nc = build_nc()
    res = run_bass_kernel_spmd(nc, in_maps, core_ids=list(range(8)))

    full = np.empty((B, S, H), np.float32)
    for c in range(8):
        b, hg = c // 4, c % 4
        full[b, hg * TOK:(hg + 1) * TOK] = res.results[c]["out"]
    return full


# version 06
# speedup vs baseline: 1.0747x; 1.0747x over previous
import numpy as np
from contextlib import ExitStack

import concourse.bass as bass
import concourse.mybir as mybir
from concourse.bass_utils import run_bass_kernel_spmd
from concourse.alu_op_type import AluOpType as ALU

F32 = mybir.dt.float32
BF16 = mybir.dt.bfloat16
AF = mybir.ActivationFunctionType
AX = mybir.AxisListType.X

B, S, H, NH, DH = 2, 4096, 2048, 16, 128
HPC = 4
TOK = 1024
KC = H // 128
SCALE = float(1.0 / np.sqrt(DH))
NEG = -1.0e30

ENGINES = ("sync", "tensor", "vector", "scalar", "gpsimd")


class Ctr:
    __slots__ = ("h", "n")

    def __init__(self, h):
        self.h = h
        self.n = 0

    def inc(self, k=1):
        self.n += k
        return self.n


def _w(inst, h, val):
    if val > 0:
        inst.wait_op(h, val, "sem-ge", check=False)
    return inst


def _collapse(deps):
    best = {}
    order = []
    for h, v in deps:
        if v is None or v <= 0:
            continue
        key = id(h)
        if key not in best:
            best[key] = [h, v]
            order.append(key)
        elif v > best[key][1]:
            best[key][1] = v
    return [tuple(best[k]) for k in order]


def build_nc():
    nc = bass.Bass("TRN2", target_bir_lowering=False)

    xb = nc.dram_tensor("xb", [S, H], F32, kind="ExternalInput").ap()
    xown = nc.dram_tensor("xown", [TOK, H], F32, kind="ExternalInput").ap()
    wqkv = nc.dram_tensor("wqkv", [H, 1536], F32, kind="ExternalInput").ap()
    bq = nc.dram_tensor("bq", [128, 4], F32, kind="ExternalInput").ap()
    bk = nc.dram_tensor("bk", [128, 4], F32, kind="ExternalInput").ap()
    bv = nc.dram_tensor("bv", [128, 512], F32, kind="ExternalInput").ap()
    jrow = nc.dram_tensor("jrow", [HPC, 128, S], F32, kind="ExternalInput").ap()
    mtri = nc.dram_tensor("mtri", [128, 128], F32, kind="ExternalInput").ap()
    iden = nc.dram_tensor("iden", [128, 128], BF16, kind="ExternalInput").ap()
    wo = nc.dram_tensor("wo", [H, H], F32, kind="ExternalInput").ap()
    bob = nc.dram_tensor("bob", [128, H], F32, kind="ExternalInput").ap()
    gwt = nc.dram_tensor("gwt", [H, 16], F32, kind="ExternalInput").ap()
    wgu = nc.dram_tensor("wgu", [16, H, 11, 256], F32, kind="ExternalInput").ap()
    wdn = nc.dram_tensor("wdn", [16, 1408, H], F32, kind="ExternalInput").ap()
    sgu = nc.dram_tensor("sgu", [H, 22, 256], F32, kind="ExternalInput").ap()
    sdn = nc.dram_tensor("sdn", [2816, H], F32, kind="ExternalInput").ap()
    out = nc.dram_tensor("out", [TOK, H], F32, kind="ExternalOutput").ap()

    dq = nc.dram_tensor("dq", [512, S], BF16, kind="Internal").ap()
    dk = nc.dram_tensor("dk", [512, S], BF16, kind="Internal").ap()
    dv = nc.dram_tensor("dv", [S, 512], BF16, kind="Internal").ap()
    cin = nc.dram_tensor("cin", [512, S], BF16, kind="Internal").ap()
    cout = nc.dram_tensor("cout", [4, 512, S], BF16, kind="Internal").ap()
    cin3 = nc.dram_tensor("cin3", [4, H, TOK], BF16, kind="Internal").ap()
    cfin = nc.dram_tensor("cfin", [H, TOK], BF16, kind="Internal").ap()
    ydr = nc.dram_tensor("ydr", [TOK, H], F32, kind="Internal").ap()

    with ExitStack() as top:
        def SEM(name):
            return Ctr(top.enter_context(nc.semaphore(name)))

        vO, sO, gO, tP, cc = SEM("vO"), SEM("sO"), SEM("gO"), SEM("tP"), SEM("ccs")
        fV, fS, fG, fT, fSY = SEM("fV"), SEM("fS"), SEM("fG"), SEM("fT"), SEM("fSY")
        dX = [SEM(f"dX{i}") for i in range(4)]
        dW = SEM("dW")
        dB = SEM("dB")
        dI = SEM("dI")
        dQs, dKs, dVs = SEM("dQs"), SEM("dKs"), SEM("dVs")
        dH = [SEM("dH0"), SEM("dH1")]
        dJ = SEM("dJ")
        dO = [SEM("dO0"), SEM("dO1")]
        dPsl = [SEM("dP0"), SEM("dP1")]
        dD = SEM("dD")
        dY = SEM("dY")
        dR = SEM("dR")
        dWo = SEM("dWo")
        dOr = SEM("dOr")
        dU = [SEM("dU0"), SEM("dU1")]

        epsT = top.enter_context(nc.sbuf_tensor("epsT", [128, 1], F32))
        finscr = top.enter_context(nc.sbuf_tensor("finscr", [128, 1], F32))
        idsb = top.enter_context(nc.sbuf_tensor("idsb", [128, 128], BF16))

        ops = {name: [] for name in ENGINES}

        def E(name, fn):
            ops[name].append(fn)

        def run_block():
            with nc.Block() as blk:
                for name in ENGINES:
                    if ops[name]:
                        def runner(eng, fns=tuple(ops[name])):
                            for f in fns:
                                f(eng)
                        getattr(blk, name)(runner)
            for name in ENGINES:
                ops[name] = []

        def mkchain(ordc, fanc):
            def plan(*deps):
                evs = _collapse([(ordc.h, ordc.n)] + list(deps))
                if len(evs) > 1:
                    fanc.inc(len(evs))
                    P = (tuple(evs), (fanc.h, fanc.n))
                else:
                    P = ((), evs[0] if evs else None)
                ordc.inc(1)
                return P

            def emit(eng, P, builder):
                for hh, vv in P[0]:
                    _w(eng.sem_inc(fanc.h, 1), hh, vv)
                i = builder()
                if P[1] is not None:
                    _w(i, P[1][0], P[1][1])
                i.then_inc(ordc.h, 1)
                return i
            return plan, emit

        pV, eV = mkchain(vO, fV)
        pS, eS = mkchain(sO, fS)
        pG, eG = mkchain(gO, fG)

        def mkgrp(fanc):
            def plan(*deps):
                evs = _collapse(list(deps))
                if len(evs) > 1:
                    fanc.inc(len(evs))
                    return (tuple(evs), (fanc.h, fanc.n))
                return ((), evs[0] if evs else None)

            def pre(eng, P):
                for hh, vv in P[0]:
                    _w(eng.sem_inc(fanc.h, 1), hh, vv)

            def wt(inst, P):
                if P[1] is not None:
                    _w(inst, P[1][0], P[1][1])
                return inst
            return plan, pre, wt
        pT_, eTpre, eTw = mkgrp(fT)
        pSY, eSYpre, eSYw = mkgrp(fSY)
        pGD, eGDpre, eGDw = mkgrp(fG)

        def ldid(eng):
            eng.dma_start(out=idsb[:], in_=iden[:, :]).then_inc(dI.h, 16)
        idv = dI.inc(16)
        E("gpsimd", ldid)

        Pe0 = pV()

        def setup(eng, Pe0=Pe0):
            eV(eng, Pe0, lambda: eng.memset(epsT[:], 1e-5))
        epsv = vO.n
        E("vector", setup)

        with ExitStack() as ph:
            pb = [ph.enter_context(nc.psum_tensor(f"pb{i}", [128, 512], F32))
                  for i in range(6)]
            tps1 = ph.enter_context(nc.psum_tensor("tps1", [128, 8, 128], BF16))
            xt1 = ph.enter_context(nc.sbuf_tensor("xt1", [128, 4, H], F32))
            scr1 = ph.enter_context(nc.sbuf_tensor("scr1", [128, H], F32))
            stats1 = ph.enter_context(nc.sbuf_tensor("stats1", [128, 4, 8], F32))
            xnb1 = ph.enter_context(nc.sbuf_tensor("xnb1", [128, 2, H], BF16))
            xnT1 = ph.enter_context(nc.sbuf_tensor("xnT1", [128, KC, 1024], BF16))
            wqs = ph.enter_context(nc.sbuf_tensor("wqs", [128, KC, 1536], BF16))
            bqT = ph.enter_context(nc.sbuf_tensor("bqT", [128, 4], F32))
            bkT = ph.enter_context(nc.sbuf_tensor("bkT", [128, 4], F32))
            bvT = ph.enter_context(nc.sbuf_tensor("bvT", [128, 512], F32))
            qev = ph.enter_context(nc.sbuf_tensor("qev", [128, 4, 512], BF16))
            kev = ph.enter_context(nc.sbuf_tensor("kev", [128, 4, 512], BF16))
            vev = ph.enter_context(nc.sbuf_tensor("vev", [128, 4, 512], BF16))

            def ldw(eng):
                eng.dma_start(out=wqs[:],
                              in_=wqkv[:, :].rearrange("(c p) n -> p c n", p=128)
                              ).then_inc(dW.h, 16)
            wqv = dW.inc(16)
            E("gpsimd", ldw)

            def ldb(eng):
                eng.dma_start(out=bqT[:], in_=bq[:, :]).then_inc(dB.h, 16)
                eng.dma_start(out=bkT[:], in_=bk[:, :]).then_inc(dB.h, 16)
                eng.dma_start(out=bvT[:], in_=bv[:, :]).then_inc(dB.h, 16)
            blv = dB.inc(48)
            E("scalar", ldb)

            vmm_l = []
            apv_l, stv_l1 = [], []
            thv_l = []
            evv_l = []
            st_l = []
            vmm_evac_l = []

            for g in range(8):
                for t in range(4 * g, 4 * g + 4):
                    sl = t % 4
                    bsl = t % 2
                    Pld = pSY((vO.h, apv_l[t - 4]) if t >= 4 else (vO.h, 0))

                    def ld(eng, t=t, sl=sl, Pld=Pld):
                        eSYpre(eng, Pld)
                        i = eng.dma_start(out=xt1[:, sl, :],
                                          in_=xb[t * 128:(t + 1) * 128, :])
                        eSYw(i, Pld)
                        i.then_inc(dX[sl].h, 16)
                    ldv = dX[sl].inc(16)
                    E("sync", ld)

                    ldep = (dX[sl].h, ldv)
                    sdep = (sO.h, stv_l1[t - 4]) if t >= 4 else (sO.h, 0)
                    P1 = pV(ldep, sdep)
                    P2 = pV(ldep)
                    P3, P4, P5, P6 = pV(), pV(), pV(), pV()
                    lnv = vO.n

                    def stats_fn(eng, sl=sl, Ps=(P1, P2, P3, P4, P5, P6)):
                        eV(eng, Ps[0], lambda: eng.tensor_reduce(
                            stats1[:, sl, 0:1], xt1[:, sl, :], axis=AX, op=ALU.add))
                        eV(eng, Ps[1], lambda: eng.scalar_tensor_tensor(
                            scr1[:], xt1[:, sl, :], 1.0, xt1[:, sl, :],
                            op0=ALU.mult, op1=ALU.mult,
                            accum_out=stats1[:, sl, 1:2]))
                        eV(eng, Ps[2], lambda: eng.tensor_scalar(
                            stats1[:, sl, 2:3], stats1[:, sl, 0:1], 1.0 / H, None,
                            op0=ALU.mult))
                        eV(eng, Ps[3], lambda: eng.tensor_tensor(
                            stats1[:, sl, 3:4], stats1[:, sl, 2:3],
                            stats1[:, sl, 2:3], op=ALU.mult))
                        eV(eng, Ps[4], lambda: eng.tensor_scalar(
                            stats1[:, sl, 4:5], stats1[:, sl, 1:2], 1.0 / H, None,
                            op0=ALU.mult))
                        eV(eng, Ps[5], lambda: eng.tensor_tensor(
                            stats1[:, sl, 4:5], stats1[:, sl, 4:5],
                            stats1[:, sl, 3:4], op=ALU.subtract))
                    E("vector", stats_fn)

                    Pst = pS((vO.h, lnv), (vO.h, epsv))

                    def stdf(eng, sl=sl, Pst=Pst):
                        eS(eng, Pst, lambda: eng.activation(
                            stats1[:, sl, 5:6], stats1[:, sl, 4:5], AF.Sqrt,
                            bias=epsT[:, 0:1]))
                    stv = sO.n
                    stv_l1.append(stv)
                    E("scalar", stdf)

                    Pa1 = pV((sO.h, stv),
                             (tP.h, thv_l[2 * (t - 2) + 1]) if t >= 2 else (tP.h, 0))
                    Pa2 = pV()

                    def app(eng, sl=sl, bsl=bsl, Pa1=Pa1, Pa2=Pa2):
                        eV(eng, Pa1, lambda: eng.reciprocal(
                            stats1[:, sl, 6:7], stats1[:, sl, 5:6]))
                        eV(eng, Pa2, lambda: eng.tensor_scalar(
                            xnb1[:, bsl, :], xt1[:, sl, :],
                            stats1[:, sl, 2:3], stats1[:, sl, 6:7],
                            op0=ALU.subtract, op1=ALU.mult))
                    apv = vO.n
                    apv_l.append(apv)
                    E("vector", app)

                    gidx = g % 2
                    for half in range(2):
                        hidx = 2 * t + half
                        tdeps = [(vO.h, apv), (dI.h, idv)]
                        if g >= 2 and t % 4 == 0 and half == 0:
                            tdeps.append((tP.h, vmm_l[g - 2]))
                        if hidx >= 1:
                            tdeps.append((vO.h, evv_l[hidx - 1]))
                        Ptr = pT_(*tdeps)

                        def trf(eng, bsl=bsl, half=half, Ptr=Ptr):
                            eTpre(eng, Ptr)
                            for s8 in range(8):
                                kch = half * 8 + s8
                                i = eng.transpose(tps1[:, s8, :],
                                                  xnb1[:, bsl, kch * 128:(kch + 1) * 128],
                                                  idsb[:])
                                eTw(i, Ptr)
                            i.then_inc(tP.h, 1)
                        thv = tP.inc()
                        thv_l.append(thv)
                        E("tensor", trf)

                        base = gidx * 512 + (t % 4) * 128
                        Pev = pV((tP.h, thv))

                        def evf(eng, half=half, base=base, Pev=Pev):
                            eV(eng, Pev, lambda: eng.tensor_copy(
                                xnT1[:, half * 8:half * 8 + 8, base:base + 128],
                                tps1[:, 0:8, :]))
                        evv_l.append(vO.n)
                        E("vector", evf)

                gs = (g % 2) * 512
                xnrdy = evv_l[2 * (4 * g + 3) + 1]
                qdeps = [(vO.h, xnrdy)]
                if g == 0:
                    qdeps.append((dW.h, wqv))
                else:
                    qdeps.append((vO.h, vmm_evac_l[g - 1][2]))
                Pq = pT_(*qdeps)

                def q_mms(eng, gs=gs, Pq=Pq):
                    eTpre(eng, Pq)
                    for oc in range(4):
                        for kc in range(KC):
                            i = eng.matmul(pb[oc][:],
                                           wqs[:, kc, oc * 128:(oc + 1) * 128],
                                           xnT1[:, kc, gs:gs + 512],
                                           start=(kc == 0), stop=(kc == KC - 1),
                                           skip_group_check=True)
                            eTw(i, Pq)
                    i.then_inc(tP.h, 1)
                mq = tP.inc()
                E("tensor", q_mms)

                k1deps = [(vO.h, xnrdy)]
                if g == 0:
                    k1deps.append((dW.h, wqv))
                else:
                    k1deps.append((vO.h, vmm_evac_l[g - 1][1]))
                Pk1 = pT_(*k1deps)

                def k1_mms(eng, gs=gs, Pk1=Pk1):
                    eTpre(eng, Pk1)
                    for oc in range(2):
                        for kc in range(KC):
                            i = eng.matmul(pb[4 + oc][:],
                                           wqs[:, kc, 512 + oc * 128:512 + (oc + 1) * 128],
                                           xnT1[:, kc, gs:gs + 512],
                                           start=(kc == 0), stop=(kc == KC - 1),
                                           skip_group_check=True)
                            eTw(i, Pk1)
                    i.then_inc(tP.h, 1)
                mk1 = tP.inc()
                E("tensor", k1_mms)

                Pqe = [None] * 6
                Pqe[0] = pV((tP.h, mq), (dB.h, blv),
                            (dQs.h, st_l[g - 1][0]) if g >= 1 else (dQs.h, 0))
                Pqe[1] = pV()
                Pqe[2] = pV()
                Pqe[3] = pV()
                Pqe[4] = pV((tP.h, mk1),
                            (dKs.h, st_l[g - 1][1]) if g >= 1 else (dKs.h, 0))
                Pqe[5] = pV()
                qevv = vO.n - 2
                k1evv = vO.n

                def qk1_ev(eng, Pqe=tuple(Pqe)):
                    for oc in range(4):
                        eV(eng, Pqe[oc], lambda oc=oc: eng.tensor_scalar(
                            qev[:, oc, :], pb[oc][:], bqT[:, oc:oc + 1], None,
                            op0=ALU.add))
                    for oc in range(2):
                        eV(eng, Pqe[4 + oc], lambda oc=oc: eng.tensor_scalar(
                            kev[:, oc, :], pb[4 + oc][:], bkT[:, oc:oc + 1], None,
                            op0=ALU.add))
                E("vector", qk1_ev)

                Pk2 = pT_((vO.h, k1evv))

                def k2_mms(eng, gs=gs, Pk2=Pk2):
                    eTpre(eng, Pk2)
                    for oc in range(2, 4):
                        for kc in range(KC):
                            i = eng.matmul(pb[2 + oc][:],
                                           wqs[:, kc, 512 + oc * 128:512 + (oc + 1) * 128],
                                           xnT1[:, kc, gs:gs + 512],
                                           start=(kc == 0), stop=(kc == KC - 1),
                                           skip_group_check=True)
                            eTw(i, Pk2)
                    i.then_inc(tP.h, 1)
                mk2 = tP.inc()
                E("tensor", k2_mms)

                Pvm = pT_((vO.h, qevv))

                def v_mms(eng, gs=gs, Pvm=Pvm):
                    eTpre(eng, Pvm)
                    for tt in range(4):
                        for kc in range(KC):
                            i = eng.matmul(pb[tt][:],
                                           xnT1[:, kc, gs + tt * 128:gs + (tt + 1) * 128],
                                           wqs[:, kc, 1024:1536],
                                           start=(kc == 0), stop=(kc == KC - 1),
                                           skip_group_check=True)
                            eTw(i, Pvm)
                    i.then_inc(tP.h, 1)
                mv = tP.inc()
                vmm_l.append(mv)
                E("tensor", v_mms)

                Pk2e = [None] * 6
                Pk2e[0] = pV((tP.h, mk2))
                Pk2e[1] = pV()
                Pk2e[2] = pV((tP.h, mv), (dB.h, blv),
                             (dVs.h, st_l[g - 1][2]) if g >= 1 else (dVs.h, 0))
                Pk2e[3] = pV()
                Pk2e[4] = pV()
                Pk2e[5] = pV()
                k2evv = vO.n - 4
                vevv = vO.n

                def k2v_ev(eng, Pk2e=tuple(Pk2e)):
                    for j, oc in enumerate(range(2, 4)):
                        eV(eng, Pk2e[j], lambda oc=oc: eng.tensor_scalar(
                            kev[:, oc, :], pb[2 + oc][:], bkT[:, oc:oc + 1], None,
                            op0=ALU.add))
                    for tt in range(4):
                        eV(eng, Pk2e[2 + tt], lambda tt=tt: eng.tensor_tensor(
                            vev[:, tt, :], pb[tt][:], bvT[:], op=ALU.add))
                vmm_evac_l.append((qevv, k2evv, vevv))
                E("vector", k2v_ev)

                Pst1 = pGD((vO.h, qevv))
                Pst2 = pGD((vO.h, k2evv))
                Pst3 = pGD((vO.h, vevv))

                def qkv_st(eng, g=g, Pst1=Pst1, Pst2=Pst2, Pst3=Pst3):
                    eGDpre(eng, Pst1)
                    i = eng.dma_start(
                        out=dq[:, :].rearrange("(oc p) t -> p oc t", p=128)[:, :, g * 512:(g + 1) * 512],
                        in_=qev[:])
                    eGDw(i, Pst1)
                    i.then_inc(dQs.h, 16)
                    eGDpre(eng, Pst2)
                    i = eng.dma_start(
                        out=dk[:, :].rearrange("(oc p) t -> p oc t", p=128)[:, :, g * 512:(g + 1) * 512],
                        in_=kev[:])
                    eGDw(i, Pst2)
                    i.then_inc(dKs.h, 16)
                    eGDpre(eng, Pst3)
                    i = eng.dma_start(
                        out=dv[:, :].rearrange("(tt p) c -> p tt c", p=128)[:, g * 4:(g + 1) * 4, :],
                        in_=vev[:])
                    eGDw(i, Pst3)
                    i.then_inc(dVs.h, 16)
                dQs.inc(16); dKs.inc(16); dVs.inc(16)
                st_l.append((dQs.n, dKs.n, dVs.n))
                E("gpsimd", qkv_st)

            Pf = pG((dQs.h, dQs.n), (dKs.h, dKs.n), (dVs.h, dVs.n))

            def fin1a(eng, Pf=Pf):
                eG(eng, Pf, lambda: eng.memset(finscr[0:1, 0:1], 0.0))
            E("gpsimd", fin1a)
            run_block()

        with ExitStack() as ph:
            scb = [ph.enter_context(nc.psum_tensor(f"sb{i}", [128, 512], F32))
                   for i in range(3)]
            avb = ph.enter_context(nc.psum_tensor("avb", [128, 2, 256], F32))
            tps2 = ph.enter_context(nc.psum_tensor("tps2", [128, 2, 8, 128], BF16))
            qh = ph.enter_context(nc.sbuf_tensor("qh", [128, 2, S], BF16))
            kh = ph.enter_context(nc.sbuf_tensor("kh", [128, 2, S], BF16))
            vh = ph.enter_context(nc.sbuf_tensor("vh", [128, 2, 32, 128], BF16))
            jbc = ph.enter_context(nc.sbuf_tensor("jbc", [128, S], F32))
            sc = ph.enter_context(nc.sbuf_tensor("sc", [128, 3, S], F32))
            pg = ph.enter_context(nc.sbuf_tensor("pg", [128, 3, S], BF16))
            pn = ph.enter_context(nc.sbuf_tensor("pn", [128, 3, S], BF16))
            pt = ph.enter_context(nc.sbuf_tensor("pt", [128, 2, 32, 256], BF16))
            oev = ph.enter_context(nc.sbuf_tensor("oev", [128, 2, 256], BF16))
            acols = ph.enter_context(nc.sbuf_tensor("acols", [128, 3, 4], F32))
            mtri_sb = ph.enter_context(nc.sbuf_tensor("mtri_sb", [128, 128], F32))

            def ldmtri(eng):
                eng.dma_start(out=mtri_sb[:], in_=mtri[:, :]).then_inc(dB.h, 16)
            mtv = dB.inc(16)
            E("scalar", ldmtri)

            sev_l, exp_l, pnv_l, ptev_l = [], [], [], []
            avv_l, oevv_l, ost_l = [], [], []
            ck_cnt = 0
            qt_cnt = 0
            qg_cnt = 0

            for h in range(HPC):
                hs = h % 2
                hdeps = []
                if h >= 2:
                    hdeps = [(vO.h, sev_l[144 * (h - 1) - 1]),
                             (tP.h, avv_l[16 * (h - 1) - 1])]
                Plh = pSY(*hdeps)

                def ldh(eng, h=h, hs=hs, Plh=Plh):
                    eSYpre(eng, Plh)
                    i = eng.dma_start(out=qh[:, hs, :], in_=dq[h * 128:(h + 1) * 128, :])
                    eSYw(i, Plh)
                    i.then_inc(dH[hs].h, 16)
                    i = eng.dma_start(out=kh[:, hs, :], in_=dk[h * 128:(h + 1) * 128, :])
                    eSYw(i, Plh)
                    i.then_inc(dH[hs].h, 16)
                    i = eng.dma_start(
                        out=vh[:, hs, :, :],
                        in_=dv[:, :].rearrange("(kt p) c -> p kt c", p=128)[:, :, h * 128:(h + 1) * 128])
                    eSYw(i, Plh)
                    i.then_inc(dH[hs].h, 16)
                qlv = dH[hs].inc(48)
                E("sync", ldh)

                Plj = pGD((vO.h, sev_l[144 * h - 1]) if h >= 1 else (vO.h, 0))

                def ldj(eng, h=h, Plj=Plj):
                    eGDpre(eng, Plj)
                    i = eng.dma_start(out=jbc[:], in_=jrow[h, :, :])
                    eGDw(i, Plj)
                    i.then_inc(dJ.h, 16)
                jlv = dJ.inc(16)
                E("gpsimd", ldj)

                for qt in range(32):
                    Eext = (qt + 1) * 128
                    ssl = qt % 3
                    nck = (Eext + 511) // 512

                    for ck in range(nck):
                        k0 = ck * 512
                        k1 = min(Eext, k0 + 512)
                        bank = ck_cnt % 3
                        mdeps = [(dH[hs].h, qlv)]
                        if ck_cnt >= 3:
                            mdeps.append((vO.h, sev_l[ck_cnt - 3]))
                        Pm = pT_(*mdeps)

                        def smm(eng, hs=hs, qt=qt, k0=k0, k1=k1, bank=bank, Pm=Pm):
                            eTpre(eng, Pm)
                            i = eng.matmul(scb[bank][:, 0:k1 - k0],
                                           qh[:, hs, qt * 128:(qt + 1) * 128],
                                           kh[:, hs, k0:k1], start=True, stop=True,
                                           skip_group_check=True)
                            eTw(i, Pm)
                            i.then_inc(tP.h, 1)
                        mmv = tP.inc()
                        E("tensor", smm)

                        edeps = [(tP.h, mmv), (dJ.h, jlv)]
                        if qt_cnt >= 3 and ck == 0:
                            edeps.append((sO.h, exp_l[qt_cnt - 3]))
                        Pe_ = pV(*edeps)

                        def sev(eng, ssl=ssl, k0=k0, k1=k1, bank=bank, Pe_=Pe_):
                            eV(eng, Pe_, lambda: eng.scalar_tensor_tensor(
                                sc[:, ssl, k0:k1], scb[bank][:, 0:k1 - k0],
                                SCALE, jbc[:, k0:k1], op0=ALU.mult, op1=ALU.add))
                        sev_l.append(vO.n)
                        E("vector", sev)
                        ck_cnt += 1

                    Pn1 = pV((dB.h, mtv))
                    nmv = vO.n

                    def nmx(eng, ssl=ssl, qt=qt, Pn1=Pn1):
                        eV(eng, Pn1, lambda: eng.tensor_tensor(
                            sc[:, ssl, qt * 128:(qt + 1) * 128],
                            sc[:, ssl, qt * 128:(qt + 1) * 128],
                            mtri_sb[:], op=ALU.add))
                    E("vector", nmx)

                    Pgr = pV()
                    grv = vO.n

                    def gmax(eng, ssl=ssl, Eext=Eext, Pgr=Pgr):
                        eV(eng, Pgr, lambda: eng.tensor_reduce(
                            acols[:, ssl, 0:1], sc[:, ssl, 0:Eext],
                            axis=AX, op=ALU.max, negate=True))
                    E("vector", gmax)

                    xdeps = [(vO.h, grv)]
                    Px = pS(*xdeps)
                    exv = sO.n

                    def expf(eng, ssl=ssl, Eext=Eext, Px=Px):
                        eS(eng, Px, lambda: eng.activation(
                            pg[:, ssl, 0:Eext], sc[:, ssl, 0:Eext], AF.Exp,
                            bias=acols[:, ssl, 0:1], accum_out=acols[:, ssl, 1:2]))
                    exp_l.append(exv)
                    E("scalar", expf)

                    if qt % 2 == 1:
                        Pz = pV((tP.h, avv_l[qg_cnt - 2]) if qg_cnt >= 2 else (tP.h, 0))

                        def ptz(eng, nkt=qt + 1, par=qg_cnt % 2, Pz=Pz):
                            eV(eng, Pz, lambda: eng.memset(
                                pt[:, par, nkt - 1, 0:128], 0.0))
                        E("vector", ptz)

                    Pr1 = pV((sO.h, exv))
                    rcv = vO.n

                    def recf(eng, ssl=ssl, Pr1=Pr1):
                        eV(eng, Pr1, lambda: eng.reciprocal(
                            acols[:, ssl, 2:3], acols[:, ssl, 1:2]))
                    E("vector", recf)

                    mdeps2 = [(vO.h, rcv)]
                    if qt_cnt >= 3:
                        mdeps2.append((vO.h, ptev_l[(qt_cnt - 3) // 2]))
                    Pmu = pS(*mdeps2)
                    pnv = sO.n

                    def pnorm(eng, ssl=ssl, Eext=Eext, Pmu=Pmu):
                        eS(eng, Pmu, lambda: eng.mul(
                            pn[:, ssl, 0:Eext], pg[:, ssl, 0:Eext],
                            acols[:, ssl, 2:3]))
                    pnv_l.append(pnv)
                    E("scalar", pnorm)
                    qt_cnt += 1

                    if qt % 2 == 1:
                        qg = qt // 2
                        nkt = qt + 1
                        osl = qg % 2
                        par = qg_cnt % 2
                        runs = [((qt - 1) % 3, 0, qt), (qt % 3, 1, nkt)]
                        wave_ev = []
                        widx = 0
                        for src, qs, cnt in runs:
                            for w0 in range(0, cnt, 8):
                                wn = min(8, cnt - w0)
                                half = widx % 2
                                wdeps = [(sO.h, pnv_l[-1]),
                                         (sO.h, pnv_l[-2] if len(pnv_l) >= 2 else 0),
                                         (dI.h, idv)]
                                if qg_cnt >= 2:
                                    wdeps.append((tP.h, avv_l[qg_cnt - 2]))
                                if widx >= 2:
                                    wdeps.append((vO.h, wave_ev[widx - 2]))
                                Pw = pT_(*wdeps)

                                def wtr(eng, src=src, w0=w0, wn=wn, half=half, Pw=Pw):
                                    eTpre(eng, Pw)
                                    for s8 in range(wn):
                                        i = eng.transpose(
                                            tps2[:, half, s8, :],
                                            pn[:, src, (w0 + s8) * 128:(w0 + s8 + 1) * 128],
                                            idsb[:])
                                        eTw(i, Pw)
                                    i.then_inc(tP.h, 1)
                                wtv = tP.inc()
                                E("tensor", wtr)

                                Pwe = pV((tP.h, wtv))

                                def wev(eng, src=src, qs=qs, w0=w0, wn=wn, half=half,
                                        par=par, Pwe=Pwe):
                                    eV(eng, Pwe, lambda: eng.tensor_copy(
                                        pt[:, par, w0:w0 + wn, qs * 128:(qs + 1) * 128],
                                        tps2[:, half, 0:wn, :]))
                                wave_ev.append(vO.n)
                                E("vector", wev)
                                widx += 1
                        ptev = wave_ev[-1]
                        ptev_l.append(ptev)

                        adeps = [(vO.h, ptev), (dH[hs].h, qlv)]
                        if qg_cnt >= 2:
                            adeps.append((vO.h, oevv_l[qg_cnt - 2]))
                        Pa = pT_(*adeps)

                        def avm(eng, hs=hs, nkt=nkt, osl=osl, par=par, Pa=Pa):
                            eTpre(eng, Pa)
                            for kt in range(nkt):
                                i = eng.matmul(avb[:, osl, :], vh[:, hs, kt, :],
                                               pt[:, par, kt, 0:256],
                                               start=(kt == 0), stop=(kt == nkt - 1),
                                               skip_group_check=True)
                                eTw(i, Pa)
                            i.then_inc(tP.h, 1)
                        avv = tP.inc()
                        avv_l.append(avv)
                        E("tensor", avm)

                        odeps = [(tP.h, avv)]
                        if qg_cnt >= 2:
                            odeps.append((dO[qg_cnt % 2].h, ost_l[qg_cnt - 2]))
                        Po = pV(*odeps)

                        def oevf(eng, osl=osl, Po=Po):
                            eV(eng, Po, lambda: eng.tensor_copy(
                                oev[:, osl, :], avb[:, osl, :]))
                        oevv_l.append(vO.n)
                        E("vector", oevf)

                        Pos = pGD((vO.h, oevv_l[-1]))

                        def ost(eng, h=h, qg=qg, osl=osl, Pos=Pos, par=qg_cnt % 2):
                            eGDpre(eng, Pos)
                            i = eng.dma_start(
                                out=cin[:, :].rearrange("(hh p) t -> p hh t", p=128)[:, h, qg * 256:(qg + 1) * 256],
                                in_=oev[:, osl, :])
                            eGDw(i, Pos)
                            i.then_inc(dO[par].h, 16)
                        ostv = dO[qg_cnt % 2].inc(16)
                        ost_l.append(ostv)
                        E("gpsimd", ost)
                        qg_cnt += 1

            Pf = pG((dO[0].h, dO[0].n), (dO[1].h, dO[1].n))

            def fin1b(eng, Pf=Pf):
                eG(eng, Pf, lambda: eng.memset(finscr[0:1, 0:1], 0.0))
            E("gpsimd", fin1b)
            run_block()

        xnT2 = top.enter_context(nc.sbuf_tensor("xnT2", [128, KC, TOK], BF16))
        comb = top.enter_context(nc.sbuf_tensor("comb", [128, 8, 16], F32))
        with ExitStack() as ph:
            gb = [ph.enter_context(nc.psum_tensor(f"gb{i}", [128, 512], F32))
                  for i in range(2)]
            tps3 = ph.enter_context(nc.psum_tensor("tps3", [128, 8, 128], BF16))
            xt2 = ph.enter_context(nc.sbuf_tensor("xt2", [128, 2, H], F32))
            scr2 = ph.enter_context(nc.sbuf_tensor("scr2", [128, H], F32))
            stats2 = ph.enter_context(nc.sbuf_tensor("stats2", [128, 2, 8], F32))
            xnb2 = ph.enter_context(nc.sbuf_tensor("xnb2", [128, 2, H], BF16))
            gw2 = ph.enter_context(nc.sbuf_tensor("gw2", [128, KC, 16], BF16))
            gl = ph.enter_context(nc.sbuf_tensor("gl", [128, 8, 16], F32))
            gcols = ph.enter_context(nc.sbuf_tensor("gcols", [128, 8, 8], F32))

            def agather(eng):
                eng.collective_compute(
                    "AllGather", ALU.bypass,
                    replica_groups=[[0, 1, 2, 3], [4, 5, 6, 7]],
                    ins=[cin[:, :]], outs=[cout[:, :, :]]).then_inc(cc.h, 1)
            cc.inc()
            E("gpsimd", agather)

            apv2, stv2, thv2, evv2 = [], [], [], []
            for t in range(8):
                sl = t % 2
                bsl = t % 2
                Pld = pSY((vO.h, apv2[t - 2]) if t >= 2 else (vO.h, 0))

                def ld(eng, t=t, sl=sl, Pld=Pld):
                    eSYpre(eng, Pld)
                    i = eng.dma_start(out=xt2[:, sl, :],
                                      in_=xown[t * 128:(t + 1) * 128, :])
                    eSYw(i, Pld)
                    i.then_inc(dX[sl].h, 16)
                ldv = dX[sl].inc(16)
                E("sync", ld)

                ldep = (dX[sl].h, ldv)
                sdep = (sO.h, stv2[t - 2]) if t >= 2 else (sO.h, 0)
                P1 = pV(ldep, sdep)
                P2 = pV(ldep)
                P3, P4, P5, P6 = pV(), pV(), pV(), pV()
                lnv = vO.n

                def stats_fn(eng, sl=sl, Ps=(P1, P2, P3, P4, P5, P6)):
                    eV(eng, Ps[0], lambda: eng.tensor_reduce(
                        stats2[:, sl, 0:1], xt2[:, sl, :], axis=AX, op=ALU.add))
                    eV(eng, Ps[1], lambda: eng.scalar_tensor_tensor(
                        scr2[:], xt2[:, sl, :], 1.0, xt2[:, sl, :],
                        op0=ALU.mult, op1=ALU.mult, accum_out=stats2[:, sl, 1:2]))
                    eV(eng, Ps[2], lambda: eng.tensor_scalar(
                        stats2[:, sl, 2:3], stats2[:, sl, 0:1], 1.0 / H, None,
                        op0=ALU.mult))
                    eV(eng, Ps[3], lambda: eng.tensor_tensor(
                        stats2[:, sl, 3:4], stats2[:, sl, 2:3], stats2[:, sl, 2:3],
                        op=ALU.mult))
                    eV(eng, Ps[4], lambda: eng.tensor_scalar(
                        stats2[:, sl, 4:5], stats2[:, sl, 1:2], 1.0 / H, None,
                        op0=ALU.mult))
                    eV(eng, Ps[5], lambda: eng.tensor_tensor(
                        stats2[:, sl, 4:5], stats2[:, sl, 4:5], stats2[:, sl, 3:4],
                        op=ALU.subtract))
                E("vector", stats_fn)

                Pst = pS((vO.h, lnv), (vO.h, epsv))

                def stdf(eng, sl=sl, Pst=Pst):
                    eS(eng, Pst, lambda: eng.activation(
                        stats2[:, sl, 5:6], stats2[:, sl, 4:5], AF.Sqrt,
                        bias=epsT[:, 0:1]))
                stv = sO.n
                stv2.append(stv)
                E("scalar", stdf)

                Pa1 = pV((sO.h, stv),
                         (tP.h, thv2[2 * (t - 2) + 1]) if t >= 2 else (tP.h, 0))
                Pa2 = pV()

                def app(eng, sl=sl, bsl=bsl, Pa1=Pa1, Pa2=Pa2):
                    eV(eng, Pa1, lambda: eng.reciprocal(
                        stats2[:, sl, 6:7], stats2[:, sl, 5:6]))
                    eV(eng, Pa2, lambda: eng.tensor_scalar(
                        xnb2[:, bsl, :], xt2[:, sl, :],
                        stats2[:, sl, 2:3], stats2[:, sl, 6:7],
                        op0=ALU.subtract, op1=ALU.mult))
                apv = vO.n
                apv2.append(apv)
                E("vector", app)

                for half in range(2):
                    hidx = 2 * t + half
                    tdeps = [(vO.h, apv), (dI.h, idv)]
                    if hidx >= 1:
                        tdeps.append((vO.h, evv2[hidx - 1]))
                    Ptr = pT_(*tdeps)

                    def trf(eng, bsl=bsl, half=half, Ptr=Ptr):
                        eTpre(eng, Ptr)
                        for s8 in range(8):
                            kch = half * 8 + s8
                            i = eng.transpose(tps3[:, s8, :],
                                              xnb2[:, bsl, kch * 128:(kch + 1) * 128],
                                              idsb[:])
                            eTw(i, Ptr)
                        i.then_inc(tP.h, 1)
                    thv = tP.inc()
                    thv2.append(thv)
                    E("tensor", trf)

                    Pev = pV((tP.h, thv))

                    def evf(eng, half=half, t=t, Pev=Pev):
                        eV(eng, Pev, lambda: eng.tensor_copy(
                            xnT2[:, half * 8:half * 8 + 8, t * 128:(t + 1) * 128],
                            tps3[:, 0:8, :]))
                    evv2.append(vO.n)
                    E("vector", evf)
            xnrdy2 = evv2[-1]

            def ldgw(eng):
                eng.dma_start(out=gw2[:],
                              in_=gwt[:, :].rearrange("(c p) n -> p c n", p=128)
                              ).then_inc(dW.h, 16)
            gwv = dW.inc(16)
            E("gpsimd", ldgw)

            gev_l2 = []
            for tt in range(8):
                gdeps = [(dW.h, gwv), (vO.h, xnrdy2)]
                if tt >= 2:
                    gdeps.append((vO.h, gev_l2[tt - 2]))
                Pg_ = pT_(*gdeps)

                def gmm(eng, tt=tt, Pg_=Pg_):
                    eTpre(eng, Pg_)
                    for kc in range(KC):
                        i = eng.matmul(gb[tt % 2][:, 0:16],
                                       xnT2[:, kc, tt * 128:(tt + 1) * 128],
                                       gw2[:, kc, :], start=(kc == 0),
                                       stop=(kc == KC - 1), skip_group_check=True)
                        eTw(i, Pg_)
                    i.then_inc(tP.h, 1)
                gmv = tP.inc()
                E("tensor", gmm)

                Pg1 = pV((tP.h, gmv))
                Pg2 = pV()
                gev_l2.append(vO.n - 1)
                gevv = vO.n

                def gev(eng, tt=tt, Pg1=Pg1, Pg2=Pg2):
                    eV(eng, Pg1, lambda: eng.tensor_copy(gl[:, tt, :],
                                                         gb[tt % 2][:, 0:16]))
                    eV(eng, Pg2, lambda: eng.tensor_reduce(
                        gcols[:, tt, 0:1], gl[:, tt, :], axis=AX, op=ALU.max,
                        negate=True))
                E("vector", gev)

                Pgx = pS((vO.h, gevv))
                gxv = sO.n

                def gexp(eng, tt=tt, Pgx=Pgx):
                    eS(eng, Pgx, lambda: eng.activation(
                        comb[:, tt, :], gl[:, tt, :], AF.Exp,
                        bias=gcols[:, tt, 0:1], accum_out=gcols[:, tt, 1:2]))
                E("scalar", gexp)

                Pt_ = [pV((sO.h, gxv))] + [pV() for _ in range(7)]

                def gtop(eng, tt=tt, Pt_=tuple(Pt_)):
                    eV(eng, Pt_[0], lambda: eng.reciprocal(
                        gcols[:, tt, 2:3], gcols[:, tt, 1:2]))
                    eV(eng, Pt_[1], lambda: eng.tensor_scalar(
                        gl[:, tt, :], comb[:, tt, :], gcols[:, tt, 2:3], None,
                        op0=ALU.mult))
                    eV(eng, Pt_[2], lambda: eng.tensor_reduce(
                        gcols[:, tt, 3:4], gl[:, tt, :], axis=AX, op=ALU.max))
                    eV(eng, Pt_[3], lambda: eng.tensor_scalar(
                        comb[:, tt, :], gl[:, tt, :], gcols[:, tt, 3:4], None,
                        op0=ALU.is_equal))
                    eV(eng, Pt_[4], lambda: eng.scalar_tensor_tensor(
                        comb[:, tt, :], comb[:, tt, :], -1.0e30, gl[:, tt, :],
                        op0=ALU.mult, op1=ALU.add))
                    eV(eng, Pt_[5], lambda: eng.tensor_reduce(
                        gcols[:, tt, 4:5], comb[:, tt, :], axis=AX, op=ALU.max))
                    eV(eng, Pt_[6], lambda: eng.tensor_scalar(
                        comb[:, tt, :], gl[:, tt, :], gcols[:, tt, 4:5], None,
                        op0=ALU.is_ge))
                    eV(eng, Pt_[7], lambda: eng.tensor_tensor(
                        comb[:, tt, :], comb[:, tt, :], gl[:, tt, :], op=ALU.mult))
                E("vector", gtop)
            combv = vO.n
            run_block()

        with ExitStack() as ph:
            PS = [ph.enter_context(nc.psum_tensor(f"eb{i}", [128, 512], F32))
                  for i in range(8)]
            pslab = ph.enter_context(nc.sbuf_tensor("pslab", [128, 2, KC, 256], BF16))
            dslab = ph.enter_context(nc.sbuf_tensor("dslab", [128, 11, H], BF16))
            hbuf = ph.enter_context(nc.sbuf_tensor("hbuf", [128, 11, TOK], BF16))
            silt = ph.enter_context(nc.sbuf_tensor("silt", [128, 2, 512], BF16))
            ybuf = ph.enter_context(nc.sbuf_tensor("ybuf", [128, 8, H], F32))

            Pyz = pV()

            def yz(eng, Pyz=Pyz):
                eV(eng, Pyz, lambda: eng.memset(ybuf[:], 0.0))
            yz_v = vO.n
            E("vector", yz)

            plist = [(0, e) for e in range(16)] + [(1, 0), (1, 1)]
            gp_cnt = 0
            gu_l, sil_l, hm_l, dn_l, yev_l = [], [], [], [], []

            for pe, (kind, e) in enumerate(plist):
                for p in range(11):
                    psl = gp_cnt % 2
                    src = wgu[e, :, p, :] if kind == 0 else sgu[:, e * 11 + p, :]
                    Ppl = pGD((tP.h, gu_l[gp_cnt - 2]) if gp_cnt >= 2 else (tP.h, 0))

                    def pload(eng, src=src, psl=psl, Ppl=Ppl):
                        eGDpre(eng, Ppl)
                        i = eng.dma_start(out=pslab[:, psl, :, :],
                                          in_=src.rearrange("(c p) n -> p c n", p=128))
                        eGDw(i, Ppl)
                        i.then_inc(dPsl[psl].h, 16)
                    plv = dPsl[psl].inc(16)
                    E("gpsimd", pload)

                    gdeps = [(dPsl[psl].h, plv)]
                    if gp_cnt >= 2:
                        gdeps.append((vO.h, hm_l[gp_cnt - 2]))
                        gdeps.append((sO.h, sil_l[gp_cnt - 2]))
                    if gp_cnt == 0:
                        gdeps.append((vO.h, combv))
                    if pe >= 1 and p <= 1:
                        gdeps.append((vO.h, yev_l[8 * pe - 1]))
                    Pgu = pT_(*gdeps)

                    def gumm(eng, psl=psl, Pgu=Pgu):
                        eTpre(eng, Pgu)
                        for kc in range(KC):
                            for th in range(2):
                                i = eng.matmul(PS[4 * psl + th][:],
                                               pslab[:, psl, kc, 0:128],
                                               xnT2[:, kc, th * 512:(th + 1) * 512],
                                               start=(kc == 0), stop=(kc == KC - 1),
                                               skip_group_check=True)
                                eTw(i, Pgu)
                                i2 = eng.matmul(PS[4 * psl + 2 + th][:],
                                                pslab[:, psl, kc, 128:256],
                                                xnT2[:, kc, th * 512:(th + 1) * 512],
                                                start=(kc == 0), stop=(kc == KC - 1),
                                                skip_group_check=True)
                                eTw(i2, Pgu)
                        i2.then_inc(tP.h, 1)
                    guv = tP.inc()
                    gu_l.append(guv)
                    E("tensor", gumm)

                    sdeps = [(tP.h, guv)]
                    if gp_cnt >= 1:
                        sdeps.append((vO.h, hm_l[gp_cnt - 1]))
                    Ps1 = pS(*sdeps)
                    Ps2 = pS()
                    silv = sO.n

                    def hsil(eng, psl=psl, Ps1=Ps1, Ps2=Ps2):
                        eS(eng, Ps1, lambda: eng.activation(
                            silt[:, 0, :], PS[4 * psl + 0][:], AF.Silu))
                        eS(eng, Ps2, lambda: eng.activation(
                            silt[:, 1, :], PS[4 * psl + 1][:], AF.Silu))
                    sil_l.append(silv)
                    E("scalar", hsil)

                    hdeps = [(sO.h, silv)]
                    if p == 0 and pe >= 1:
                        hdeps.append((tP.h, dn_l[8 * pe - 1]))
                    Ph1 = pV(*hdeps)
                    Ph2 = pV()

                    def hmul(eng, p=p, psl=psl, Ph1=Ph1, Ph2=Ph2):
                        eV(eng, Ph1, lambda: eng.tensor_tensor(
                            hbuf[:, p, 0:512], silt[:, 0, :], PS[4 * psl + 2][:],
                            op=ALU.mult))
                        eV(eng, Ph2, lambda: eng.tensor_tensor(
                            hbuf[:, p, 512:1024], silt[:, 1, :], PS[4 * psl + 3][:],
                            op=ALU.mult))
                    hm_l.append(vO.n)
                    E("vector", hmul)
                    gp_cnt += 1

                Pdl = pGD((tP.h, dn_l[8 * pe - 1]) if pe >= 1 else (tP.h, 0))

                def dload(eng, kind=kind, e=e, Pdl=Pdl):
                    eGDpre(eng, Pdl)
                    for ic in range(11):
                        dsrc = (wdn[e, ic * 128:(ic + 1) * 128, :] if kind == 0
                                else sdn[e * 1408 + ic * 128:e * 1408 + (ic + 1) * 128, :])
                        i = eng.dma_start(out=dslab[:, ic, :], in_=dsrc)
                        eGDw(i, Pdl)
                        i.then_inc(dD.h, 16)
                dslv = dD.inc(16 * 11)
                E("gpsimd", dload)

                hall = hm_l[-1]
                yv_base = len(yev_l)
                dn0 = tP.n
                Py = []
                for tt in range(8):
                    Py.append(pV((tP.h, dn0 + tt + 1)))
                    Py.extend(pV() for _ in range(3))
                    yev_l.append(vO.n)
                Ptt = []
                for tt in range(8):
                    gi = yv_base + tt - 2
                    ydep = (vO.h, yev_l[gi]) if gi >= 0 else (vO.h, yz_v)
                    if tt == 0:
                        Ptt.append(pT_((dD.h, dslv), (vO.h, hall), ydep))
                    else:
                        Ptt.append(pT_(ydep))

                def dnm(eng, Ptt=tuple(Ptt)):
                    for tt in range(8):
                        bs = (tt % 2) * 4
                        eTpre(eng, Ptt[tt])
                        for ic in range(11):
                            for fc in range(4):
                                i = eng.matmul(PS[bs + fc][:],
                                               hbuf[:, ic, tt * 128:(tt + 1) * 128],
                                               dslab[:, ic, fc * 512:(fc + 1) * 512],
                                               start=(ic == 0), stop=(ic == 10),
                                               skip_group_check=True)
                                eTw(i, Ptt[tt])
                        i.then_inc(tP.h, 1)
                for tt in range(8):
                    dn_l.append(dn0 + tt + 1)
                tP.inc(8)
                E("tensor", dnm)

                def yev(eng, kind=kind, e=e, Py=tuple(Py)):
                    pi = 0
                    for tt in range(8):
                        bs = (tt % 2) * 4
                        for fc in range(4):
                            if kind == 0:
                                eV(eng, Py[pi], lambda tt=tt, fc=fc, bs=bs:
                                   eng.scalar_tensor_tensor(
                                       ybuf[:, tt, fc * 512:(fc + 1) * 512],
                                       PS[bs + fc][:], comb[:, tt, e:e + 1],
                                       ybuf[:, tt, fc * 512:(fc + 1) * 512],
                                       op0=ALU.mult, op1=ALU.add))
                            else:
                                eV(eng, Py[pi], lambda tt=tt, fc=fc, bs=bs:
                                   eng.tensor_tensor(
                                       ybuf[:, tt, fc * 512:(fc + 1) * 512],
                                       ybuf[:, tt, fc * 512:(fc + 1) * 512],
                                       PS[bs + fc][:], op=ALU.add))
                            pi += 1
                E("vector", yev)

                if pe == 5:
                    Pr = pSY((cc.h, cc.n))

                    def reshp(eng, Pr=Pr):
                        eSYpre(eng, Pr)
                        for gg in range(4):
                            i = eng.dma_start(
                                out=cin3[:, gg * 512:(gg + 1) * 512, :],
                                in_=cout[gg, :, :].rearrange("r (m t) -> m r t", m=4))
                            eSYw(i, Pr)
                            i.then_inc(dR.h, 16)
                    cpv = dR.inc(64)
                    E("sync", reshp)

                    def rsc(eng, cpv=cpv):
                        i = eng.collective_compute(
                            "ReduceScatter", ALU.max,
                            replica_groups=[[0, 1, 2, 3], [4, 5, 6, 7]],
                            ins=[cin3[:, :, :]], outs=[cfin[:, :]])
                        _w(i, dR.h, cpv)
                        i.then_inc(cc.h, 1)
                    cc.inc()
                    E("gpsimd", rsc)

            yfin = yev_l[-1]
            Pyd = pGD((vO.h, yfin))

            def ydma(eng, Pyd=Pyd):
                eGDpre(eng, Pyd)
                i = eng.dma_start(out=ydr[:, :].rearrange("(t p) c -> p t c", p=128),
                                  in_=ybuf[:])
                eGDw(i, Pyd)
                i.then_inc(dY.h, 16)
            dY.inc(16)
            E("gpsimd", ydma)

            Pf = pG((dY.h, dY.n))

            def fin2(eng, Pf=Pf):
                eG(eng, Pf, lambda: eng.memset(finscr[0:1, 0:1], 0.0))
            E("gpsimd", fin2)
            run_block()

        with ExitStack() as ph:
            PS = [ph.enter_context(nc.psum_tensor(f"ob{i}", [128, 512], F32))
                  for i in range(8)]
            orc = ph.enter_context(nc.sbuf_tensor("orc", [128, KC, TOK], BF16))
            wos = ph.enter_context(nc.sbuf_tensor("wos", [128, KC, H], BF16))
            bos = ph.enter_context(nc.sbuf_tensor("bos", [128, H], F32))
            xo3 = ph.enter_context(nc.sbuf_tensor("xo3", [128, 2, H], F32))
            st3 = ph.enter_context(nc.sbuf_tensor("st3", [128, 2, H], F32))
            gt3 = ph.enter_context(nc.sbuf_tensor("gt3", [128, 2, H], F32))
            yt3 = ph.enter_context(nc.sbuf_tensor("yt3", [128, 2, H], F32))

            def wload(eng):
                for fc in range(KC):
                    eng.dma_start(out=wos[:, fc, :],
                                  in_=wo[fc * 128:(fc + 1) * 128, :]).then_inc(dWo.h, 16)
            wlv = dWo.inc(16 * KC)
            E("gpsimd", wload)

            Pol = pSY((cc.h, cc.n))

            def oload(eng, Pol=Pol):
                eSYpre(eng, Pol)
                i = eng.dma_start(out=orc[:],
                                  in_=cfin[:, :].rearrange("(c p) t -> p c t", p=128))
                eSYw(i, Pol)
                i.then_inc(dOr.h, 16)
                eng.dma_start(out=bos[:], in_=bob[:, :]).then_inc(dOr.h, 16)
            olv = dOr.inc(32)
            E("sync", oload)

            sev_l3, sg_l3, outst_l = [], [], []
            for tt in range(8):
                sl = tt % 2

                xdeps = []
                if tt >= 2:
                    xdeps.append((vO.h, sev_l3[tt - 2]))
                    xdeps.append((sO.h, sg_l3[tt - 2]))
                Pxl = pSY(*xdeps)

                def xload(eng, tt=tt, sl=sl, Pxl=Pxl):
                    eSYpre(eng, Pxl)
                    i = eng.dma_start(out=xo3[:, sl, :],
                                      in_=xown[tt * 128:(tt + 1) * 128, :])
                    eSYw(i, Pxl)
                    i.then_inc(dX[sl].h, 16)
                    i = eng.dma_start(out=yt3[:, sl, :],
                                      in_=ydr[tt * 128:(tt + 1) * 128, :])
                    eSYw(i, Pxl)
                    i.then_inc(dX[sl].h, 16)
                xlv = dX[sl].inc(32)
                E("sync", xload)

                mdeps = [(dWo.h, wlv), (dOr.h, olv)]
                if tt >= 2:
                    mdeps.append((vO.h, sev_l3[tt - 2]))
                Pm3 = pT_(*mdeps)

                def smm3(eng, tt=tt, Pm3=Pm3):
                    eTpre(eng, Pm3)
                    bs = (tt % 2) * 4
                    for fc in range(KC):
                        for oc in range(4):
                            i = eng.matmul(PS[bs + oc][:],
                                           orc[:, fc, tt * 128:(tt + 1) * 128],
                                           wos[:, fc, oc * 512:(oc + 1) * 512],
                                           start=(fc == 0), stop=(fc == KC - 1),
                                           skip_group_check=True)
                            eTw(i, Pm3)
                    i.then_inc(tP.h, 1)
                smv = tP.inc()
                E("tensor", smm3)

                gdeps = [(dX[sl].h, xlv)]
                if tt >= 2:
                    gdeps.append((vO.h, sev_l3[tt - 2]))
                Pgs = pS(*gdeps)
                sgv = sO.n
                sg_l3.append(sgv)

                def gsil(eng, sl=sl, Pgs=Pgs):
                    eS(eng, Pgs, lambda: eng.activation(
                        gt3[:, sl, :], yt3[:, sl, :], AF.Silu))
                E("scalar", gsil)

                cdeps = [(tP.h, smv), (dX[sl].h, xlv), (sO.h, sgv)]
                if tt >= 2:
                    cdeps.append((dU[tt % 2].h, outst_l[tt - 2]))
                Pc = [pV(*cdeps)] + [pV() for _ in range(5)]
                sevv = vO.n

                def sevc(eng, tt=tt, sl=sl, Pc=tuple(Pc)):
                    bs = (tt % 2) * 4
                    for oc in range(4):
                        eV(eng, Pc[oc], lambda oc=oc: eng.tensor_tensor(
                            st3[:, sl, oc * 512:(oc + 1) * 512], PS[bs + oc][:],
                            bos[:, oc * 512:(oc + 1) * 512], op=ALU.add))
                    eV(eng, Pc[4], lambda: eng.tensor_tensor(
                        st3[:, sl, :], st3[:, sl, :], gt3[:, sl, :], op=ALU.mult))
                    eV(eng, Pc[5], lambda: eng.tensor_tensor(
                        st3[:, sl, :], st3[:, sl, :], xo3[:, sl, :], op=ALU.add))
                sev_l3.append(sevv)
                E("vector", sevc)

                Pou = pGD((vO.h, sevv))

                def ostore(eng, tt=tt, sl=sl, Pou=Pou):
                    eGDpre(eng, Pou)
                    i = eng.dma_start(out=out[tt * 128:(tt + 1) * 128, :],
                                      in_=st3[:, sl, :])
                    eGDw(i, Pou)
                    i.then_inc(dU[tt % 2].h, 16)
                ov = dU[tt % 2].inc(16)
                outst_l.append(ov)
                E("gpsimd", ostore)

            Pf = pG((dU[0].h, dU[0].n), (dU[1].h, dU[1].n))

            def finout(eng, Pf=Pf):
                eG(eng, Pf, lambda: eng.memset(finscr[0:1, 0:1], 0.0))
            E("gpsimd", finout)
            run_block()

    return nc


def kernel(L0_attn_x, L0_attn_Wqkv, L0_attn_bqkv, L0_attn_Wo, L0_attn_bo,
           L0_ffn_gate_w, L0_ffn_w_gate_up, L0_ffn_w_down,
           L0_ffn_shared_gate_up, L0_ffn_shared_down):
    import ml_dtypes
    x = np.ascontiguousarray(np.asarray(L0_attn_x, np.float32))
    Wqkv = np.asarray(L0_attn_Wqkv, np.float32)
    bqkv = np.asarray(L0_attn_bqkv, np.float32)
    Wo = np.ascontiguousarray(np.asarray(L0_attn_Wo, np.float32))
    bo = np.asarray(L0_attn_bo, np.float32)
    gate_w = np.asarray(L0_ffn_gate_w, np.float32)
    w_gu = np.asarray(L0_ffn_w_gate_up, np.float32)
    w_dn = np.ascontiguousarray(np.asarray(L0_ffn_w_down, np.float32))
    s_gu = np.asarray(L0_ffn_shared_gate_up, np.float32)
    s_dn = np.ascontiguousarray(np.asarray(L0_ffn_shared_down, np.float32))

    slopes = np.array([1.0 / 2 ** (k / 2) for k in range(NH)], np.float32)
    jj = np.arange(S, dtype=np.float32)

    wgu_p = np.ascontiguousarray(
        w_gu.reshape(16, H, 2, 11, 128).transpose(0, 1, 3, 2, 4).reshape(16, H, 11, 256))
    sgu_p = np.ascontiguousarray(
        s_gu.reshape(H, 2, 22, 128).transpose(0, 2, 1, 3).reshape(H, 22, 256))
    gwt = np.ascontiguousarray(gate_w.T)
    bob = np.ascontiguousarray(np.broadcast_to(bo[None, :], (128, H)))
    ii = np.arange(128, dtype=np.float32)
    mtri = np.ascontiguousarray(
        np.where(np.arange(128)[None, :] <= ii[:, None], np.float32(0.0),
                 np.float32(NEG)))
    iden = np.eye(128, dtype=np.float32).astype(ml_dtypes.bfloat16)

    in_maps = []
    for c in range(8):
        b, hg = c // 4, c % 4
        heads = slice(hg * 512, (hg + 1) * 512)
        wq_slice = np.ascontiguousarray(np.concatenate(
            [Wqkv[:, heads], Wqkv[:, H:][:, heads], Wqkv[:, 2 * H:][:, heads]], axis=1))
        bq_s = np.ascontiguousarray(bqkv[:H][heads].reshape(4, 128).T)
        bk_s = np.ascontiguousarray(bqkv[H:2 * H][heads].reshape(4, 128).T)
        bv_s = np.ascontiguousarray(
            np.broadcast_to(bqkv[2 * H:][heads][None, :], (128, 512)))
        jr = np.ascontiguousarray(np.broadcast_to(
            (slopes[4 * hg:4 * hg + 4, None] * jj[None, :])[:, None, :],
            (4, 128, S)))
        in_maps.append({
            "xb": x[b],
            "xown": np.ascontiguousarray(x[b, hg * TOK:(hg + 1) * TOK]),
            "wqkv": wq_slice,
            "bq": bq_s, "bk": bk_s, "bv": bv_s,
            "jrow": jr, "mtri": mtri, "iden": iden,
            "wo": Wo, "bob": bob,
            "gwt": gwt,
            "wgu": wgu_p, "wdn": w_dn,
            "sgu": sgu_p, "sdn": s_dn,
        })

    nc = build_nc()
    res = run_bass_kernel_spmd(nc, in_maps, core_ids=list(range(8)))

    full = np.empty((B, S, H), np.float32)
    for c in range(8):
        b, hg = c // 4, c % 4
        full[b, hg * TOK:(hg + 1) * TOK] = res.results[c]["out"]
    return full


# version 07
# speedup vs baseline: 1.1659x; 1.0849x over previous
import numpy as np
from contextlib import ExitStack

import concourse.bass as bass
import concourse.mybir as mybir
from concourse.bass_utils import run_bass_kernel_spmd
from concourse.alu_op_type import AluOpType as ALU

F32 = mybir.dt.float32
BF16 = mybir.dt.bfloat16
AF = mybir.ActivationFunctionType
AX = mybir.AxisListType.X

B, S, H, NH, DH = 2, 4096, 2048, 16, 128
HPC = 4
TOK = 1024
KC = H // 128
SCALE = float(1.0 / np.sqrt(DH))
NEG = -1.0e30

ENGINES = ("sync", "tensor", "vector", "scalar", "gpsimd")


class Ctr:
    __slots__ = ("h", "n")

    def __init__(self, h):
        self.h = h
        self.n = 0

    def inc(self, k=1):
        self.n += k
        return self.n


def _w(inst, h, val):
    if val > 0:
        inst.wait_op(h, val, "sem-ge", check=False)
    return inst


def _collapse(deps):
    best = {}
    order = []
    for h, v in deps:
        if v is None or v <= 0:
            continue
        key = id(h)
        if key not in best:
            best[key] = [h, v]
            order.append(key)
        elif v > best[key][1]:
            best[key][1] = v
    return [tuple(best[k]) for k in order]


def build_nc():
    nc = bass.Bass("TRN2", target_bir_lowering=False)

    xb = nc.dram_tensor("xb", [S, H], F32, kind="ExternalInput").ap()
    xown = nc.dram_tensor("xown", [TOK, H], F32, kind="ExternalInput").ap()
    wqkv = nc.dram_tensor("wqkv", [H, 1536], F32, kind="ExternalInput").ap()
    bq = nc.dram_tensor("bq", [128, 4], F32, kind="ExternalInput").ap()
    bk = nc.dram_tensor("bk", [128, 4], F32, kind="ExternalInput").ap()
    bv = nc.dram_tensor("bv", [128, 512], F32, kind="ExternalInput").ap()
    jrow = nc.dram_tensor("jrow", [HPC, 128, S], F32, kind="ExternalInput").ap()
    mtri = nc.dram_tensor("mtri", [128, 128], F32, kind="ExternalInput").ap()
    iden = nc.dram_tensor("iden", [128, 128], BF16, kind="ExternalInput").ap()
    wo = nc.dram_tensor("wo", [H, H], F32, kind="ExternalInput").ap()
    bob = nc.dram_tensor("bob", [128, H], F32, kind="ExternalInput").ap()
    gwt = nc.dram_tensor("gwt", [H, 16], F32, kind="ExternalInput").ap()
    wgu = nc.dram_tensor("wgu", [16, H, 11, 256], F32, kind="ExternalInput").ap()
    wdn = nc.dram_tensor("wdn", [16, 1408, H], F32, kind="ExternalInput").ap()
    sgu = nc.dram_tensor("sgu", [H, 22, 256], F32, kind="ExternalInput").ap()
    sdn = nc.dram_tensor("sdn", [2816, H], F32, kind="ExternalInput").ap()
    out = nc.dram_tensor("out", [TOK, H], F32, kind="ExternalOutput").ap()

    dq = nc.dram_tensor("dq", [512, S], BF16, kind="Internal").ap()
    dk = nc.dram_tensor("dk", [512, S], BF16, kind="Internal").ap()
    dv = nc.dram_tensor("dv", [S, 512], BF16, kind="Internal").ap()
    cin = nc.dram_tensor("cin", [512, S], BF16, kind="Internal").ap()
    cout = nc.dram_tensor("cout", [4, 512, S], BF16, kind="Internal").ap()
    cin3 = nc.dram_tensor("cin3", [4, H, TOK], BF16, kind="Internal").ap()
    cfin = nc.dram_tensor("cfin", [H, TOK], BF16, kind="Internal").ap()
    ydr = nc.dram_tensor("ydr", [TOK, H], F32, kind="Internal").ap()

    with ExitStack() as top:
        def SEM(name):
            return Ctr(top.enter_context(nc.semaphore(name)))

        vO, sO, gO, tP, cc = SEM("vO"), SEM("sO"), SEM("gO"), SEM("tP"), SEM("ccs")
        fV, fS, fG, fT, fSY = SEM("fV"), SEM("fS"), SEM("fG"), SEM("fT"), SEM("fSY")
        dX = [SEM(f"dX{i}") for i in range(4)]
        dW = SEM("dW")
        dB = SEM("dB")
        dI = SEM("dI")
        dQs, dKs, dVs = SEM("dQs"), SEM("dKs"), SEM("dVs")
        dH = [SEM("dH0"), SEM("dH1")]
        dJ = SEM("dJ")
        dO = [SEM("dO0"), SEM("dO1")]
        dPsl = [SEM("dP0"), SEM("dP1")]
        dD = SEM("dD")
        dY = SEM("dY")
        dR = SEM("dR")
        dWo = SEM("dWo")
        dOr = SEM("dOr")
        dU = [SEM("dU0"), SEM("dU1")]

        epsT = top.enter_context(nc.sbuf_tensor("epsT", [128, 1], F32))
        finscr = top.enter_context(nc.sbuf_tensor("finscr", [128, 1], F32))
        idsb = top.enter_context(nc.sbuf_tensor("idsb", [128, 128], BF16))

        ops = {name: [] for name in ENGINES}

        def E(name, fn):
            ops[name].append(fn)

        def run_block():
            with nc.Block() as blk:
                for name in ENGINES:
                    if ops[name]:
                        def runner(eng, fns=tuple(ops[name])):
                            for f in fns:
                                f(eng)
                        getattr(blk, name)(runner)
            for name in ENGINES:
                ops[name] = []

        def mkchain(ordc, fanc):
            def plan(*deps):
                evs = _collapse([(ordc.h, ordc.n)] + list(deps))
                if len(evs) > 1:
                    fanc.inc(len(evs))
                    P = (tuple(evs), (fanc.h, fanc.n))
                else:
                    P = ((), evs[0] if evs else None)
                ordc.inc(1)
                return P

            def emit(eng, P, builder):
                for hh, vv in P[0]:
                    _w(eng.sem_inc(fanc.h, 1), hh, vv)
                i = builder()
                if P[1] is not None:
                    _w(i, P[1][0], P[1][1])
                i.then_inc(ordc.h, 1)
                return i
            return plan, emit

        pV, eV = mkchain(vO, fV)
        pS, eS = mkchain(sO, fS)
        pG, eG = mkchain(gO, fG)

        def mkgrp(fanc):
            def plan(*deps):
                evs = _collapse(list(deps))
                if len(evs) > 1:
                    fanc.inc(len(evs))
                    return (tuple(evs), (fanc.h, fanc.n))
                return ((), evs[0] if evs else None)

            def pre(eng, P):
                for hh, vv in P[0]:
                    _w(eng.sem_inc(fanc.h, 1), hh, vv)

            def wt(inst, P):
                if P[1] is not None:
                    _w(inst, P[1][0], P[1][1])
                return inst
            return plan, pre, wt
        pT_, eTpre, eTw = mkgrp(fT)
        pSY, eSYpre, eSYw = mkgrp(fSY)
        pGD, eGDpre, eGDw = mkgrp(fG)

        def ldid(eng):
            eng.dma_start(out=idsb[:], in_=iden[:, :]).then_inc(dI.h, 16)
        idv = dI.inc(16)
        E("gpsimd", ldid)

        Pe0 = pV()

        def setup(eng, Pe0=Pe0):
            eV(eng, Pe0, lambda: eng.memset(epsT[:], 1e-5))
        epsv = vO.n
        E("vector", setup)

        with ExitStack() as ph:
            pb = [ph.enter_context(nc.psum_tensor(f"pb{i}", [128, 512], F32))
                  for i in range(6)]
            tps1 = ph.enter_context(nc.psum_tensor("tps1", [128, 8, 128], BF16))
            xt1 = ph.enter_context(nc.sbuf_tensor("xt1", [128, 4, H], F32))
            scr1 = ph.enter_context(nc.sbuf_tensor("scr1", [128, H], F32))
            stats1 = ph.enter_context(nc.sbuf_tensor("stats1", [128, 4, 8], F32))
            xnb1 = ph.enter_context(nc.sbuf_tensor("xnb1", [128, 2, H], BF16))
            xnT1 = ph.enter_context(nc.sbuf_tensor("xnT1", [128, KC, 1024], BF16))
            wqs = ph.enter_context(nc.sbuf_tensor("wqs", [128, KC, 1536], BF16))
            bqT = ph.enter_context(nc.sbuf_tensor("bqT", [128, 4], F32))
            bkT = ph.enter_context(nc.sbuf_tensor("bkT", [128, 4], F32))
            bvT = ph.enter_context(nc.sbuf_tensor("bvT", [128, 512], F32))
            qev = ph.enter_context(nc.sbuf_tensor("qev", [128, 4, 512], BF16))
            kev = ph.enter_context(nc.sbuf_tensor("kev", [128, 4, 512], BF16))
            vev = ph.enter_context(nc.sbuf_tensor("vev", [128, 4, 512], BF16))

            def ldw(eng):
                eng.dma_start(out=wqs[:],
                              in_=wqkv[:, :].rearrange("(c p) n -> p c n", p=128)
                              ).then_inc(dW.h, 16)
            wqv = dW.inc(16)
            E("gpsimd", ldw)

            def ldb(eng):
                eng.dma_start(out=bqT[:], in_=bq[:, :]).then_inc(dB.h, 16)
                eng.dma_start(out=bkT[:], in_=bk[:, :]).then_inc(dB.h, 16)
                eng.dma_start(out=bvT[:], in_=bv[:, :]).then_inc(dB.h, 16)
            blv = dB.inc(48)
            E("scalar", ldb)

            vmm_l = []
            apv_l, stv_l1 = [], []
            thv_l = []
            evv_l = []
            st_l = []
            vmm_evac_l = []

            for g in range(8):
                for t in range(4 * g, 4 * g + 4):
                    sl = t % 4
                    bsl = t % 2
                    Pld = pSY((vO.h, apv_l[t - 4]) if t >= 4 else (vO.h, 0))

                    def ld(eng, t=t, sl=sl, Pld=Pld):
                        eSYpre(eng, Pld)
                        i = eng.dma_start(out=xt1[:, sl, :],
                                          in_=xb[t * 128:(t + 1) * 128, :])
                        eSYw(i, Pld)
                        i.then_inc(dX[sl].h, 16)
                    ldv = dX[sl].inc(16)
                    E("sync", ld)

                    ldep = (dX[sl].h, ldv)
                    sdep = (sO.h, stv_l1[t - 4]) if t >= 4 else (sO.h, 0)
                    P1 = pV(ldep, sdep)
                    P2 = pV(ldep)
                    P3, P4, P5, P6 = pV(), pV(), pV(), pV()
                    lnv = vO.n

                    def stats_fn(eng, sl=sl, Ps=(P1, P2, P3, P4, P5, P6)):
                        eV(eng, Ps[0], lambda: eng.tensor_reduce(
                            stats1[:, sl, 0:1], xt1[:, sl, :], axis=AX, op=ALU.add))
                        eV(eng, Ps[1], lambda: eng.scalar_tensor_tensor(
                            scr1[:], xt1[:, sl, :], 1.0, xt1[:, sl, :],
                            op0=ALU.mult, op1=ALU.mult,
                            accum_out=stats1[:, sl, 1:2]))
                        eV(eng, Ps[2], lambda: eng.tensor_scalar(
                            stats1[:, sl, 2:3], stats1[:, sl, 0:1], 1.0 / H, None,
                            op0=ALU.mult))
                        eV(eng, Ps[3], lambda: eng.tensor_tensor(
                            stats1[:, sl, 3:4], stats1[:, sl, 2:3],
                            stats1[:, sl, 2:3], op=ALU.mult))
                        eV(eng, Ps[4], lambda: eng.tensor_scalar(
                            stats1[:, sl, 4:5], stats1[:, sl, 1:2], 1.0 / H, None,
                            op0=ALU.mult))
                        eV(eng, Ps[5], lambda: eng.tensor_tensor(
                            stats1[:, sl, 4:5], stats1[:, sl, 4:5],
                            stats1[:, sl, 3:4], op=ALU.subtract))
                    E("vector", stats_fn)

                    Pst = pS((vO.h, lnv), (vO.h, epsv))

                    def stdf(eng, sl=sl, Pst=Pst):
                        eS(eng, Pst, lambda: eng.activation(
                            stats1[:, sl, 5:6], stats1[:, sl, 4:5], AF.Sqrt,
                            bias=epsT[:, 0:1]))
                    stv = sO.n
                    stv_l1.append(stv)
                    E("scalar", stdf)

                    Pa1 = pV((sO.h, stv),
                             (tP.h, thv_l[2 * (t - 2) + 1]) if t >= 2 else (tP.h, 0))
                    Pa2 = pV()

                    def app(eng, sl=sl, bsl=bsl, Pa1=Pa1, Pa2=Pa2):
                        eV(eng, Pa1, lambda: eng.reciprocal(
                            stats1[:, sl, 6:7], stats1[:, sl, 5:6]))
                        eV(eng, Pa2, lambda: eng.tensor_scalar(
                            xnb1[:, bsl, :], xt1[:, sl, :],
                            stats1[:, sl, 2:3], stats1[:, sl, 6:7],
                            op0=ALU.subtract, op1=ALU.mult))
                    apv = vO.n
                    apv_l.append(apv)
                    E("vector", app)

                    gidx = g % 2
                    for half in range(2):
                        hidx = 2 * t + half
                        tdeps = [(vO.h, apv), (dI.h, idv)]
                        if g >= 2 and t % 4 == 0 and half == 0:
                            tdeps.append((tP.h, vmm_l[g - 2]))
                        if hidx >= 1:
                            tdeps.append((vO.h, evv_l[hidx - 1]))
                        Ptr = pT_(*tdeps)

                        def trf(eng, bsl=bsl, half=half, Ptr=Ptr):
                            eTpre(eng, Ptr)
                            for s8 in range(8):
                                kch = half * 8 + s8
                                i = eng.transpose(tps1[:, s8, :],
                                                  xnb1[:, bsl, kch * 128:(kch + 1) * 128],
                                                  idsb[:])
                                eTw(i, Ptr)
                            i.then_inc(tP.h, 1)
                        thv = tP.inc()
                        thv_l.append(thv)
                        E("tensor", trf)

                        base = gidx * 512 + (t % 4) * 128
                        Pev = pV((tP.h, thv))

                        def evf(eng, half=half, base=base, Pev=Pev):
                            eV(eng, Pev, lambda: eng.tensor_copy(
                                xnT1[:, half * 8:half * 8 + 8, base:base + 128],
                                tps1[:, 0:8, :]))
                        evv_l.append(vO.n)
                        E("vector", evf)

                gs = (g % 2) * 512
                xnrdy = evv_l[2 * (4 * g + 3) + 1]
                qdeps = [(vO.h, xnrdy)]
                if g == 0:
                    qdeps.append((dW.h, wqv))
                else:
                    qdeps.append((vO.h, vmm_evac_l[g - 1][2]))
                Pq = pT_(*qdeps)

                def q_mms(eng, gs=gs, Pq=Pq):
                    eTpre(eng, Pq)
                    for oc in range(4):
                        for kc in range(KC):
                            i = eng.matmul(pb[oc][:],
                                           wqs[:, kc, oc * 128:(oc + 1) * 128],
                                           xnT1[:, kc, gs:gs + 512],
                                           start=(kc == 0), stop=(kc == KC - 1),
                                           skip_group_check=True)
                            eTw(i, Pq)
                    i.then_inc(tP.h, 1)
                mq = tP.inc()
                E("tensor", q_mms)

                k1deps = [(vO.h, xnrdy)]
                if g == 0:
                    k1deps.append((dW.h, wqv))
                else:
                    k1deps.append((vO.h, vmm_evac_l[g - 1][1]))
                Pk1 = pT_(*k1deps)

                def k1_mms(eng, gs=gs, Pk1=Pk1):
                    eTpre(eng, Pk1)
                    for oc in range(2):
                        for kc in range(KC):
                            i = eng.matmul(pb[4 + oc][:],
                                           wqs[:, kc, 512 + oc * 128:512 + (oc + 1) * 128],
                                           xnT1[:, kc, gs:gs + 512],
                                           start=(kc == 0), stop=(kc == KC - 1),
                                           skip_group_check=True)
                            eTw(i, Pk1)
                    i.then_inc(tP.h, 1)
                mk1 = tP.inc()
                E("tensor", k1_mms)

                Pqe = [None] * 6
                Pqe[0] = pV((tP.h, mq), (dB.h, blv),
                            (dQs.h, st_l[g - 1][0]) if g >= 1 else (dQs.h, 0))
                Pqe[1] = pV()
                Pqe[2] = pV()
                Pqe[3] = pV()
                Pqe[4] = pV((tP.h, mk1),
                            (dKs.h, st_l[g - 1][1]) if g >= 1 else (dKs.h, 0))
                Pqe[5] = pV()
                qevv = vO.n - 2
                k1evv = vO.n

                def qk1_ev(eng, Pqe=tuple(Pqe)):
                    for oc in range(4):
                        eV(eng, Pqe[oc], lambda oc=oc: eng.tensor_scalar(
                            qev[:, oc, :], pb[oc][:], bqT[:, oc:oc + 1], None,
                            op0=ALU.add))
                    for oc in range(2):
                        eV(eng, Pqe[4 + oc], lambda oc=oc: eng.tensor_scalar(
                            kev[:, oc, :], pb[4 + oc][:], bkT[:, oc:oc + 1], None,
                            op0=ALU.add))
                E("vector", qk1_ev)

                Pk2 = pT_((vO.h, k1evv))

                def k2_mms(eng, gs=gs, Pk2=Pk2):
                    eTpre(eng, Pk2)
                    for oc in range(2, 4):
                        for kc in range(KC):
                            i = eng.matmul(pb[2 + oc][:],
                                           wqs[:, kc, 512 + oc * 128:512 + (oc + 1) * 128],
                                           xnT1[:, kc, gs:gs + 512],
                                           start=(kc == 0), stop=(kc == KC - 1),
                                           skip_group_check=True)
                            eTw(i, Pk2)
                    i.then_inc(tP.h, 1)
                mk2 = tP.inc()
                E("tensor", k2_mms)

                Pvm = pT_((vO.h, qevv))

                def v_mms(eng, gs=gs, Pvm=Pvm):
                    eTpre(eng, Pvm)
                    for tt in range(4):
                        for kc in range(KC):
                            i = eng.matmul(pb[tt][:],
                                           xnT1[:, kc, gs + tt * 128:gs + (tt + 1) * 128],
                                           wqs[:, kc, 1024:1536],
                                           start=(kc == 0), stop=(kc == KC - 1),
                                           skip_group_check=True)
                            eTw(i, Pvm)
                    i.then_inc(tP.h, 1)
                mv = tP.inc()
                vmm_l.append(mv)
                E("tensor", v_mms)

                Pk2e = [None] * 6
                Pk2e[0] = pV((tP.h, mk2))
                Pk2e[1] = pV()
                Pk2e[2] = pV((tP.h, mv), (dB.h, blv),
                             (dVs.h, st_l[g - 1][2]) if g >= 1 else (dVs.h, 0))
                Pk2e[3] = pV()
                Pk2e[4] = pV()
                Pk2e[5] = pV()
                k2evv = vO.n - 4
                vevv = vO.n

                def k2v_ev(eng, Pk2e=tuple(Pk2e)):
                    for j, oc in enumerate(range(2, 4)):
                        eV(eng, Pk2e[j], lambda oc=oc: eng.tensor_scalar(
                            kev[:, oc, :], pb[2 + oc][:], bkT[:, oc:oc + 1], None,
                            op0=ALU.add))
                    for tt in range(4):
                        eV(eng, Pk2e[2 + tt], lambda tt=tt: eng.tensor_tensor(
                            vev[:, tt, :], pb[tt][:], bvT[:], op=ALU.add))
                vmm_evac_l.append((qevv, k2evv, vevv))
                E("vector", k2v_ev)

                Pst1 = pGD((vO.h, qevv))
                Pst2 = pGD((vO.h, k2evv))
                Pst3 = pGD((vO.h, vevv))

                def qkv_st(eng, g=g, Pst1=Pst1, Pst2=Pst2, Pst3=Pst3):
                    eGDpre(eng, Pst1)
                    i = eng.dma_start(
                        out=dq[:, :].rearrange("(oc p) t -> p oc t", p=128)[:, :, g * 512:(g + 1) * 512],
                        in_=qev[:])
                    eGDw(i, Pst1)
                    i.then_inc(dQs.h, 16)
                    eGDpre(eng, Pst2)
                    i = eng.dma_start(
                        out=dk[:, :].rearrange("(oc p) t -> p oc t", p=128)[:, :, g * 512:(g + 1) * 512],
                        in_=kev[:])
                    eGDw(i, Pst2)
                    i.then_inc(dKs.h, 16)
                    eGDpre(eng, Pst3)
                    i = eng.dma_start(
                        out=dv[:, :].rearrange("(tt p) c -> p tt c", p=128)[:, g * 4:(g + 1) * 4, :],
                        in_=vev[:])
                    eGDw(i, Pst3)
                    i.then_inc(dVs.h, 16)
                dQs.inc(16); dKs.inc(16); dVs.inc(16)
                st_l.append((dQs.n, dKs.n, dVs.n))
                E("gpsimd", qkv_st)

            Pf = pG((dQs.h, dQs.n), (dKs.h, dKs.n), (dVs.h, dVs.n))

            def fin1a(eng, Pf=Pf):
                eG(eng, Pf, lambda: eng.memset(finscr[0:1, 0:1], 0.0))
            E("gpsimd", fin1a)
            run_block()

        with ExitStack() as ph:
            scb = [ph.enter_context(nc.psum_tensor(f"sb{i}", [128, 512], F32))
                   for i in range(3)]
            avb = ph.enter_context(nc.psum_tensor("avb", [128, 2, 256], F32))
            tps2 = ph.enter_context(nc.psum_tensor("tps2", [128, 2, 8, 128], BF16))
            qh = ph.enter_context(nc.sbuf_tensor("qh", [128, 2, S], BF16))
            kh = ph.enter_context(nc.sbuf_tensor("kh", [128, 2, S], BF16))
            vh = ph.enter_context(nc.sbuf_tensor("vh", [128, 2, 32, 128], BF16))
            jbc = ph.enter_context(nc.sbuf_tensor("jbc", [128, S], F32))
            sc = ph.enter_context(nc.sbuf_tensor("sc", [128, 3, S], F32))
            pg = ph.enter_context(nc.sbuf_tensor("pg", [128, 3, S], BF16))
            pn = ph.enter_context(nc.sbuf_tensor("pn", [128, 3, S], BF16))
            pt = ph.enter_context(nc.sbuf_tensor("pt", [128, 2, 32, 256], BF16))
            oev = ph.enter_context(nc.sbuf_tensor("oev", [128, 2, 256], BF16))
            acols = ph.enter_context(nc.sbuf_tensor("acols", [128, 3, 4], F32))
            mtri_sb = ph.enter_context(nc.sbuf_tensor("mtri_sb", [128, 128], F32))

            def ldmtri(eng):
                eng.dma_start(out=mtri_sb[:], in_=mtri[:, :]).then_inc(dB.h, 16)
            mtv = dB.inc(16)
            E("scalar", ldmtri)

            sev_l, exp_l, pnv_l, ptev_l = [], [], [], []
            avv_l, oevv_l, ost_l = [], [], []
            ck_cnt = 0
            qt_cnt = 0
            qg_cnt = 0

            qlv_l, jlv_l = [], []
            ck_cnt = 0

            def emit_ldh(h):
                hs = h % 2
                hdeps = []
                if h >= 2:
                    hdeps = [(vO.h, sev_l[144 * (h - 1) - 1]),
                             (tP.h, avv_l[16 * (h - 1) - 1])]
                Plh = pSY(*hdeps)

                def ldh(eng, h=h, hs=hs, Plh=Plh):
                    eSYpre(eng, Plh)
                    i = eng.dma_start(out=qh[:, hs, :], in_=dq[h * 128:(h + 1) * 128, :])
                    eSYw(i, Plh)
                    i.then_inc(dH[hs].h, 16)
                    i = eng.dma_start(out=kh[:, hs, :], in_=dk[h * 128:(h + 1) * 128, :])
                    eSYw(i, Plh)
                    i.then_inc(dH[hs].h, 16)
                    i = eng.dma_start(
                        out=vh[:, hs, :, :],
                        in_=dv[:, :].rearrange("(kt p) c -> p kt c", p=128)[:, :, h * 128:(h + 1) * 128])
                    eSYw(i, Plh)
                    i.then_inc(dH[hs].h, 16)
                qlv_l.append(dH[hs].inc(48))
                E("sync", ldh)

                Plj = pGD((vO.h, sev_l[144 * h - 1]) if h >= 1 else (vO.h, 0))

                def ldj(eng, h=h, Plj=Plj):
                    eGDpre(eng, Plj)
                    i = eng.dma_start(out=jbc[:], in_=jrow[h, :, :])
                    eGDw(i, Plj)
                    i.then_inc(dJ.h, 16)
                jlv_l.append(dJ.inc(16))
                E("gpsimd", ldj)

            def stageA(gq):
                nonlocal ck_cnt
                h, qt = gq // 32, gq % 32
                hs = h % 2
                qlv, jlv = qlv_l[h], jlv_l[h]
                Eext = (qt + 1) * 128
                ssl = gq % 3
                nck = (Eext + 511) // 512
                for ck in range(nck):
                    k0 = ck * 512
                    k1 = min(Eext, k0 + 512)
                    bank = ck_cnt % 3
                    mdeps = [(dH[hs].h, qlv)]
                    if ck_cnt >= 3:
                        mdeps.append((vO.h, sev_l[ck_cnt - 3]))
                    Pm = pT_(*mdeps)

                    def smm(eng, hs=hs, qt=qt, k0=k0, k1=k1, bank=bank, Pm=Pm):
                        eTpre(eng, Pm)
                        i = eng.matmul(scb[bank][:, 0:k1 - k0],
                                       qh[:, hs, qt * 128:(qt + 1) * 128],
                                       kh[:, hs, k0:k1], start=True, stop=True,
                                       skip_group_check=True)
                        eTw(i, Pm)
                        i.then_inc(tP.h, 1)
                    mmv = tP.inc()
                    E("tensor", smm)

                    edeps = [(tP.h, mmv), (dJ.h, jlv)]
                    if gq >= 3 and ck == 0:
                        edeps.append((sO.h, exp_l[gq - 3]))
                    Pe_ = pV(*edeps)

                    def sev(eng, ssl=ssl, k0=k0, k1=k1, bank=bank, Pe_=Pe_):
                        eV(eng, Pe_, lambda: eng.scalar_tensor_tensor(
                            sc[:, ssl, k0:k1], scb[bank][:, 0:k1 - k0],
                            SCALE, jbc[:, k0:k1], op0=ALU.mult, op1=ALU.add))
                    sev_l.append(vO.n)
                    E("vector", sev)
                    ck_cnt += 1

                Pn1 = pV((dB.h, mtv))

                def nmx(eng, ssl=ssl, qt=qt, Pn1=Pn1):
                    eV(eng, Pn1, lambda: eng.tensor_tensor(
                        sc[:, ssl, qt * 128:(qt + 1) * 128],
                        sc[:, ssl, qt * 128:(qt + 1) * 128],
                        mtri_sb[:], op=ALU.add))
                E("vector", nmx)

                Pgr = pV()
                grv = vO.n

                def gmax(eng, ssl=ssl, Eext=Eext, Pgr=Pgr):
                    eV(eng, Pgr, lambda: eng.tensor_reduce(
                        acols[:, ssl, 0:1], sc[:, ssl, 0:Eext],
                        axis=AX, op=ALU.max, negate=True))
                E("vector", gmax)

                Px = pS((vO.h, grv))

                def expf(eng, ssl=ssl, Eext=Eext, Px=Px):
                    eS(eng, Px, lambda: eng.activation(
                        pg[:, ssl, 0:Eext], sc[:, ssl, 0:Eext], AF.Exp,
                        bias=acols[:, ssl, 0:1], accum_out=acols[:, ssl, 1:2]))
                exp_l.append(sO.n)
                E("scalar", expf)

            def stageB(gq):
                ssl = gq % 3
                Eext = (gq % 32 + 1) * 128
                Pr1 = pV((sO.h, exp_l[gq]))
                rcv = vO.n

                def recf(eng, ssl=ssl, Pr1=Pr1):
                    eV(eng, Pr1, lambda: eng.reciprocal(
                        acols[:, ssl, 2:3], acols[:, ssl, 1:2]))
                E("vector", recf)

                Pmu = pS((vO.h, rcv))

                def pnorm(eng, ssl=ssl, Eext=Eext, Pmu=Pmu):
                    eS(eng, Pmu, lambda: eng.mul(
                        pn[:, ssl, 0:Eext], pg[:, ssl, 0:Eext],
                        acols[:, ssl, 2:3]))
                pnv_l.append(sO.n)
                E("scalar", pnorm)

            def stageC1(p):
                g1 = 2 * p + 1
                nkt = g1 % 32 + 1
                par = p % 2
                Pz = pV((tP.h, avv_l[p - 2]) if p >= 2 else (tP.h, 0))

                def ptz(eng, nkt=nkt, par=par, Pz=Pz):
                    eV(eng, Pz, lambda: eng.memset(
                        pt[:, par, nkt - 1, 0:128], 0.0))
                E("vector", ptz)

                runs = [((g1 - 1) % 3, 0, nkt - 1), (g1 % 3, 1, nkt)]
                wave_ev = []
                widx = 0
                for src, qs, cnt in runs:
                    for w0 in range(0, cnt, 8):
                        wn = min(8, cnt - w0)
                        half = widx % 2
                        wdeps = [(sO.h, pnv_l[g1]), (dI.h, idv)]
                        if p >= 2:
                            wdeps.append((tP.h, avv_l[p - 2]))
                        if widx >= 2:
                            wdeps.append((vO.h, wave_ev[widx - 2]))
                        Pw = pT_(*wdeps)

                        def wtr(eng, src=src, w0=w0, wn=wn, half=half, Pw=Pw):
                            eTpre(eng, Pw)
                            for s8 in range(wn):
                                i = eng.transpose(
                                    tps2[:, half, s8, :],
                                    pn[:, src, (w0 + s8) * 128:(w0 + s8 + 1) * 128],
                                    idsb[:])
                                eTw(i, Pw)
                            i.then_inc(tP.h, 1)
                        wtv = tP.inc()
                        E("tensor", wtr)

                        Pwe = pV((tP.h, wtv))

                        def wev(eng, qs=qs, w0=w0, wn=wn, half=half,
                                par=par, Pwe=Pwe):
                            eV(eng, Pwe, lambda: eng.tensor_copy(
                                pt[:, par, w0:w0 + wn, qs * 128:(qs + 1) * 128],
                                tps2[:, half, 0:wn, :]))
                        wave_ev.append(vO.n)
                        E("vector", wev)
                        widx += 1
                ptev_l.append(wave_ev[-1])

            def stageC2(p):
                h = p // 16
                hs = h % 2
                qlv = qlv_l[h]
                qg = p % 16
                nkt = (2 * p + 1) % 32 + 1
                osl = p % 2
                par = p % 2
                adeps = [(vO.h, ptev_l[p]), (dH[hs].h, qlv)]
                if p >= 2:
                    adeps.append((vO.h, oevv_l[p - 2]))
                Pa = pT_(*adeps)

                def avm(eng, hs=hs, nkt=nkt, osl=osl, par=par, Pa=Pa):
                    eTpre(eng, Pa)
                    for kt in range(nkt):
                        i = eng.matmul(avb[:, osl, :], vh[:, hs, kt, :],
                                       pt[:, par, kt, 0:256],
                                       start=(kt == 0), stop=(kt == nkt - 1),
                                       skip_group_check=True)
                        eTw(i, Pa)
                    i.then_inc(tP.h, 1)
                avv = tP.inc()
                avv_l.append(avv)
                E("tensor", avm)

                odeps = [(tP.h, avv)]
                if p >= 2:
                    odeps.append((dO[p % 2].h, ost_l[p - 2]))
                Po = pV(*odeps)

                def oevf(eng, osl=osl, Po=Po):
                    eV(eng, Po, lambda: eng.tensor_copy(
                        oev[:, osl, :], avb[:, osl, :]))
                oevv_l.append(vO.n)
                E("vector", oevf)

                Pos = pGD((vO.h, oevv_l[-1]))

                def ost(eng, h=h, qg=qg, osl=osl, Pos=Pos, par=par):
                    eGDpre(eng, Pos)
                    i = eng.dma_start(
                        out=cin[:, :].rearrange("(hh p) t -> p hh t", p=128)[:, h, qg * 256:(qg + 1) * 256],
                        in_=oev[:, osl, :])
                    eGDw(i, Pos)
                    i.then_inc(dO[par].h, 16)
                ost_l.append(dO[par].inc(16))
                E("gpsimd", ost)

            NGQ = HPC * 32
            for gq in range(NGQ):
                if gq % 32 == 0:
                    emit_ldh(gq // 32)
                if gq >= 1:
                    stageB(gq - 1)
                stageA(gq)
                if gq >= 2 and gq % 2 == 0:
                    stageC1((gq - 2) // 2)
                if gq >= 3 and gq % 2 == 1:
                    stageC2((gq - 3) // 2)
            stageB(NGQ - 1)
            stageC1((NGQ - 2) // 2)
            stageC2((NGQ - 2) // 2)

            Pf = pG((dO[0].h, dO[0].n), (dO[1].h, dO[1].n))

            def fin1b(eng, Pf=Pf):
                eG(eng, Pf, lambda: eng.memset(finscr[0:1, 0:1], 0.0))
            E("gpsimd", fin1b)
            run_block()

        xnT2 = top.enter_context(nc.sbuf_tensor("xnT2", [128, KC, TOK], BF16))
        comb = top.enter_context(nc.sbuf_tensor("comb", [128, 8, 16], F32))
        with ExitStack() as ph:
            gb = [ph.enter_context(nc.psum_tensor(f"gb{i}", [128, 512], F32))
                  for i in range(2)]
            tps3 = ph.enter_context(nc.psum_tensor("tps3", [128, 8, 128], BF16))
            xt2 = ph.enter_context(nc.sbuf_tensor("xt2", [128, 2, H], F32))
            scr2 = ph.enter_context(nc.sbuf_tensor("scr2", [128, H], F32))
            stats2 = ph.enter_context(nc.sbuf_tensor("stats2", [128, 2, 8], F32))
            xnb2 = ph.enter_context(nc.sbuf_tensor("xnb2", [128, 2, H], BF16))
            gw2 = ph.enter_context(nc.sbuf_tensor("gw2", [128, KC, 16], BF16))
            gl = ph.enter_context(nc.sbuf_tensor("gl", [128, 8, 16], F32))
            gcols = ph.enter_context(nc.sbuf_tensor("gcols", [128, 8, 8], F32))

            def agather(eng):
                eng.collective_compute(
                    "AllGather", ALU.bypass,
                    replica_groups=[[0, 1, 2, 3], [4, 5, 6, 7]],
                    ins=[cin[:, :]], outs=[cout[:, :, :]]).then_inc(cc.h, 1)
            cc.inc()
            E("gpsimd", agather)

            apv2, stv2, thv2, evv2 = [], [], [], []
            for t in range(8):
                sl = t % 2
                bsl = t % 2
                Pld = pSY((vO.h, apv2[t - 2]) if t >= 2 else (vO.h, 0))

                def ld(eng, t=t, sl=sl, Pld=Pld):
                    eSYpre(eng, Pld)
                    i = eng.dma_start(out=xt2[:, sl, :],
                                      in_=xown[t * 128:(t + 1) * 128, :])
                    eSYw(i, Pld)
                    i.then_inc(dX[sl].h, 16)
                ldv = dX[sl].inc(16)
                E("sync", ld)

                ldep = (dX[sl].h, ldv)
                sdep = (sO.h, stv2[t - 2]) if t >= 2 else (sO.h, 0)
                P1 = pV(ldep, sdep)
                P2 = pV(ldep)
                P3, P4, P5, P6 = pV(), pV(), pV(), pV()
                lnv = vO.n

                def stats_fn(eng, sl=sl, Ps=(P1, P2, P3, P4, P5, P6)):
                    eV(eng, Ps[0], lambda: eng.tensor_reduce(
                        stats2[:, sl, 0:1], xt2[:, sl, :], axis=AX, op=ALU.add))
                    eV(eng, Ps[1], lambda: eng.scalar_tensor_tensor(
                        scr2[:], xt2[:, sl, :], 1.0, xt2[:, sl, :],
                        op0=ALU.mult, op1=ALU.mult, accum_out=stats2[:, sl, 1:2]))
                    eV(eng, Ps[2], lambda: eng.tensor_scalar(
                        stats2[:, sl, 2:3], stats2[:, sl, 0:1], 1.0 / H, None,
                        op0=ALU.mult))
                    eV(eng, Ps[3], lambda: eng.tensor_tensor(
                        stats2[:, sl, 3:4], stats2[:, sl, 2:3], stats2[:, sl, 2:3],
                        op=ALU.mult))
                    eV(eng, Ps[4], lambda: eng.tensor_scalar(
                        stats2[:, sl, 4:5], stats2[:, sl, 1:2], 1.0 / H, None,
                        op0=ALU.mult))
                    eV(eng, Ps[5], lambda: eng.tensor_tensor(
                        stats2[:, sl, 4:5], stats2[:, sl, 4:5], stats2[:, sl, 3:4],
                        op=ALU.subtract))
                E("vector", stats_fn)

                Pst = pS((vO.h, lnv), (vO.h, epsv))

                def stdf(eng, sl=sl, Pst=Pst):
                    eS(eng, Pst, lambda: eng.activation(
                        stats2[:, sl, 5:6], stats2[:, sl, 4:5], AF.Sqrt,
                        bias=epsT[:, 0:1]))
                stv = sO.n
                stv2.append(stv)
                E("scalar", stdf)

                Pa1 = pV((sO.h, stv),
                         (tP.h, thv2[2 * (t - 2) + 1]) if t >= 2 else (tP.h, 0))
                Pa2 = pV()

                def app(eng, sl=sl, bsl=bsl, Pa1=Pa1, Pa2=Pa2):
                    eV(eng, Pa1, lambda: eng.reciprocal(
                        stats2[:, sl, 6:7], stats2[:, sl, 5:6]))
                    eV(eng, Pa2, lambda: eng.tensor_scalar(
                        xnb2[:, bsl, :], xt2[:, sl, :],
                        stats2[:, sl, 2:3], stats2[:, sl, 6:7],
                        op0=ALU.subtract, op1=ALU.mult))
                apv = vO.n
                apv2.append(apv)
                E("vector", app)

                for half in range(2):
                    hidx = 2 * t + half
                    tdeps = [(vO.h, apv), (dI.h, idv)]
                    if hidx >= 1:
                        tdeps.append((vO.h, evv2[hidx - 1]))
                    Ptr = pT_(*tdeps)

                    def trf(eng, bsl=bsl, half=half, Ptr=Ptr):
                        eTpre(eng, Ptr)
                        for s8 in range(8):
                            kch = half * 8 + s8
                            i = eng.transpose(tps3[:, s8, :],
                                              xnb2[:, bsl, kch * 128:(kch + 1) * 128],
                                              idsb[:])
                            eTw(i, Ptr)
                        i.then_inc(tP.h, 1)
                    thv = tP.inc()
                    thv2.append(thv)
                    E("tensor", trf)

                    Pev = pV((tP.h, thv))

                    def evf(eng, half=half, t=t, Pev=Pev):
                        eV(eng, Pev, lambda: eng.tensor_copy(
                            xnT2[:, half * 8:half * 8 + 8, t * 128:(t + 1) * 128],
                            tps3[:, 0:8, :]))
                    evv2.append(vO.n)
                    E("vector", evf)
            xnrdy2 = evv2[-1]

            def ldgw(eng):
                eng.dma_start(out=gw2[:],
                              in_=gwt[:, :].rearrange("(c p) n -> p c n", p=128)
                              ).then_inc(dW.h, 16)
            gwv = dW.inc(16)
            E("gpsimd", ldgw)

            gev_l2 = []
            for tt in range(8):
                gdeps = [(dW.h, gwv), (vO.h, xnrdy2)]
                if tt >= 2:
                    gdeps.append((vO.h, gev_l2[tt - 2]))
                Pg_ = pT_(*gdeps)

                def gmm(eng, tt=tt, Pg_=Pg_):
                    eTpre(eng, Pg_)
                    for kc in range(KC):
                        i = eng.matmul(gb[tt % 2][:, 0:16],
                                       xnT2[:, kc, tt * 128:(tt + 1) * 128],
                                       gw2[:, kc, :], start=(kc == 0),
                                       stop=(kc == KC - 1), skip_group_check=True)
                        eTw(i, Pg_)
                    i.then_inc(tP.h, 1)
                gmv = tP.inc()
                E("tensor", gmm)

                Pg1 = pV((tP.h, gmv))
                Pg2 = pV()
                gev_l2.append(vO.n - 1)
                gevv = vO.n

                def gev(eng, tt=tt, Pg1=Pg1, Pg2=Pg2):
                    eV(eng, Pg1, lambda: eng.tensor_copy(gl[:, tt, :],
                                                         gb[tt % 2][:, 0:16]))
                    eV(eng, Pg2, lambda: eng.tensor_reduce(
                        gcols[:, tt, 0:1], gl[:, tt, :], axis=AX, op=ALU.max,
                        negate=True))
                E("vector", gev)

                Pgx = pS((vO.h, gevv))
                gxv = sO.n

                def gexp(eng, tt=tt, Pgx=Pgx):
                    eS(eng, Pgx, lambda: eng.activation(
                        comb[:, tt, :], gl[:, tt, :], AF.Exp,
                        bias=gcols[:, tt, 0:1], accum_out=gcols[:, tt, 1:2]))
                E("scalar", gexp)

                Pt_ = [pV((sO.h, gxv))] + [pV() for _ in range(7)]

                def gtop(eng, tt=tt, Pt_=tuple(Pt_)):
                    eV(eng, Pt_[0], lambda: eng.reciprocal(
                        gcols[:, tt, 2:3], gcols[:, tt, 1:2]))
                    eV(eng, Pt_[1], lambda: eng.tensor_scalar(
                        gl[:, tt, :], comb[:, tt, :], gcols[:, tt, 2:3], None,
                        op0=ALU.mult))
                    eV(eng, Pt_[2], lambda: eng.tensor_reduce(
                        gcols[:, tt, 3:4], gl[:, tt, :], axis=AX, op=ALU.max))
                    eV(eng, Pt_[3], lambda: eng.tensor_scalar(
                        comb[:, tt, :], gl[:, tt, :], gcols[:, tt, 3:4], None,
                        op0=ALU.is_equal))
                    eV(eng, Pt_[4], lambda: eng.scalar_tensor_tensor(
                        comb[:, tt, :], comb[:, tt, :], -1.0e30, gl[:, tt, :],
                        op0=ALU.mult, op1=ALU.add))
                    eV(eng, Pt_[5], lambda: eng.tensor_reduce(
                        gcols[:, tt, 4:5], comb[:, tt, :], axis=AX, op=ALU.max))
                    eV(eng, Pt_[6], lambda: eng.tensor_scalar(
                        comb[:, tt, :], gl[:, tt, :], gcols[:, tt, 4:5], None,
                        op0=ALU.is_ge))
                    eV(eng, Pt_[7], lambda: eng.tensor_tensor(
                        comb[:, tt, :], comb[:, tt, :], gl[:, tt, :], op=ALU.mult))
                E("vector", gtop)
            combv = vO.n
            run_block()

        with ExitStack() as ph:
            PS = [ph.enter_context(nc.psum_tensor(f"eb{i}", [128, 512], F32))
                  for i in range(8)]
            pslab = ph.enter_context(nc.sbuf_tensor("pslab", [128, 2, KC, 256], BF16))
            dslab = ph.enter_context(nc.sbuf_tensor("dslab", [128, 11, H], BF16))
            hbuf = ph.enter_context(nc.sbuf_tensor("hbuf", [128, 11, TOK], BF16))
            silt = ph.enter_context(nc.sbuf_tensor("silt", [128, 2, 512], BF16))
            ybuf = ph.enter_context(nc.sbuf_tensor("ybuf", [128, 8, H], F32))

            Pyz = pV()

            def yz(eng, Pyz=Pyz):
                eV(eng, Pyz, lambda: eng.memset(ybuf[:], 0.0))
            yz_v = vO.n
            E("vector", yz)

            plist = [(0, e) for e in range(16)] + [(1, 0), (1, 1)]
            gp_cnt = 0
            gu_l, sil_l, hm_l, dn_l, yev_l = [], [], [], [], []

            for pe, (kind, e) in enumerate(plist):
                for p in range(11):
                    psl = gp_cnt % 2
                    src = wgu[e, :, p, :] if kind == 0 else sgu[:, e * 11 + p, :]
                    Ppl = pGD((tP.h, gu_l[gp_cnt - 2]) if gp_cnt >= 2 else (tP.h, 0))

                    def pload(eng, src=src, psl=psl, Ppl=Ppl):
                        eGDpre(eng, Ppl)
                        i = eng.dma_start(out=pslab[:, psl, :, :],
                                          in_=src.rearrange("(c p) n -> p c n", p=128))
                        eGDw(i, Ppl)
                        i.then_inc(dPsl[psl].h, 16)
                    plv = dPsl[psl].inc(16)
                    E("gpsimd", pload)

                    gdeps = [(dPsl[psl].h, plv)]
                    if gp_cnt >= 2:
                        gdeps.append((vO.h, hm_l[gp_cnt - 2]))
                        gdeps.append((sO.h, sil_l[gp_cnt - 2]))
                    if gp_cnt == 0:
                        gdeps.append((vO.h, combv))
                    if pe >= 1 and p <= 1:
                        gdeps.append((vO.h, yev_l[8 * pe - 1]))
                    Pgu = pT_(*gdeps)

                    def gumm(eng, psl=psl, Pgu=Pgu):
                        eTpre(eng, Pgu)
                        for kc in range(KC):
                            for th in range(2):
                                i = eng.matmul(PS[4 * psl + th][:],
                                               pslab[:, psl, kc, 0:128],
                                               xnT2[:, kc, th * 512:(th + 1) * 512],
                                               start=(kc == 0), stop=(kc == KC - 1),
                                               skip_group_check=True)
                                eTw(i, Pgu)
                                i2 = eng.matmul(PS[4 * psl + 2 + th][:],
                                                pslab[:, psl, kc, 128:256],
                                                xnT2[:, kc, th * 512:(th + 1) * 512],
                                                start=(kc == 0), stop=(kc == KC - 1),
                                                skip_group_check=True)
                                eTw(i2, Pgu)
                        i2.then_inc(tP.h, 1)
                    guv = tP.inc()
                    gu_l.append(guv)
                    E("tensor", gumm)

                    sdeps = [(tP.h, guv)]
                    if gp_cnt >= 1:
                        sdeps.append((vO.h, hm_l[gp_cnt - 1]))
                    Ps1 = pS(*sdeps)
                    Ps2 = pS()
                    silv = sO.n

                    def hsil(eng, psl=psl, Ps1=Ps1, Ps2=Ps2):
                        eS(eng, Ps1, lambda: eng.activation(
                            silt[:, 0, :], PS[4 * psl + 0][:], AF.Silu))
                        eS(eng, Ps2, lambda: eng.activation(
                            silt[:, 1, :], PS[4 * psl + 1][:], AF.Silu))
                    sil_l.append(silv)
                    E("scalar", hsil)

                    hdeps = [(sO.h, silv)]
                    if p == 0 and pe >= 1:
                        hdeps.append((tP.h, dn_l[8 * pe - 1]))
                    Ph1 = pV(*hdeps)
                    Ph2 = pV()

                    def hmul(eng, p=p, psl=psl, Ph1=Ph1, Ph2=Ph2):
                        eV(eng, Ph1, lambda: eng.tensor_tensor(
                            hbuf[:, p, 0:512], silt[:, 0, :], PS[4 * psl + 2][:],
                            op=ALU.mult))
                        eV(eng, Ph2, lambda: eng.tensor_tensor(
                            hbuf[:, p, 512:1024], silt[:, 1, :], PS[4 * psl + 3][:],
                            op=ALU.mult))
                    hm_l.append(vO.n)
                    E("vector", hmul)
                    gp_cnt += 1

                Pdl = pGD((tP.h, dn_l[8 * pe - 1]) if pe >= 1 else (tP.h, 0))

                def dload(eng, kind=kind, e=e, Pdl=Pdl):
                    eGDpre(eng, Pdl)
                    for ic in range(11):
                        dsrc = (wdn[e, ic * 128:(ic + 1) * 128, :] if kind == 0
                                else sdn[e * 1408 + ic * 128:e * 1408 + (ic + 1) * 128, :])
                        i = eng.dma_start(out=dslab[:, ic, :], in_=dsrc)
                        eGDw(i, Pdl)
                        i.then_inc(dD.h, 16)
                dslv = dD.inc(16 * 11)
                E("gpsimd", dload)

                hall = hm_l[-1]
                yv_base = len(yev_l)
                dn0 = tP.n
                Py = []
                for tt in range(8):
                    Py.append(pV((tP.h, dn0 + tt + 1)))
                    Py.extend(pV() for _ in range(3))
                    yev_l.append(vO.n)
                Ptt = []
                for tt in range(8):
                    gi = yv_base + tt - 2
                    ydep = (vO.h, yev_l[gi]) if gi >= 0 else (vO.h, yz_v)
                    if tt == 0:
                        Ptt.append(pT_((dD.h, dslv), (vO.h, hall), ydep))
                    else:
                        Ptt.append(pT_(ydep))

                def dnm(eng, Ptt=tuple(Ptt)):
                    for tt in range(8):
                        bs = (tt % 2) * 4
                        eTpre(eng, Ptt[tt])
                        for ic in range(11):
                            for fc in range(4):
                                i = eng.matmul(PS[bs + fc][:],
                                               hbuf[:, ic, tt * 128:(tt + 1) * 128],
                                               dslab[:, ic, fc * 512:(fc + 1) * 512],
                                               start=(ic == 0), stop=(ic == 10),
                                               skip_group_check=True)
                                eTw(i, Ptt[tt])
                        i.then_inc(tP.h, 1)
                for tt in range(8):
                    dn_l.append(dn0 + tt + 1)
                tP.inc(8)
                E("tensor", dnm)

                def yev(eng, kind=kind, e=e, Py=tuple(Py)):
                    pi = 0
                    for tt in range(8):
                        bs = (tt % 2) * 4
                        for fc in range(4):
                            if kind == 0:
                                eV(eng, Py[pi], lambda tt=tt, fc=fc, bs=bs:
                                   eng.scalar_tensor_tensor(
                                       ybuf[:, tt, fc * 512:(fc + 1) * 512],
                                       PS[bs + fc][:], comb[:, tt, e:e + 1],
                                       ybuf[:, tt, fc * 512:(fc + 1) * 512],
                                       op0=ALU.mult, op1=ALU.add))
                            else:
                                eV(eng, Py[pi], lambda tt=tt, fc=fc, bs=bs:
                                   eng.tensor_tensor(
                                       ybuf[:, tt, fc * 512:(fc + 1) * 512],
                                       ybuf[:, tt, fc * 512:(fc + 1) * 512],
                                       PS[bs + fc][:], op=ALU.add))
                            pi += 1
                E("vector", yev)

                if pe == 5:
                    Pr = pSY((cc.h, cc.n))

                    def reshp(eng, Pr=Pr):
                        eSYpre(eng, Pr)
                        for gg in range(4):
                            i = eng.dma_start(
                                out=cin3[:, gg * 512:(gg + 1) * 512, :],
                                in_=cout[gg, :, :].rearrange("r (m t) -> m r t", m=4))
                            eSYw(i, Pr)
                            i.then_inc(dR.h, 16)
                    cpv = dR.inc(64)
                    E("sync", reshp)

                    def rsc(eng, cpv=cpv):
                        i = eng.collective_compute(
                            "ReduceScatter", ALU.max,
                            replica_groups=[[0, 1, 2, 3], [4, 5, 6, 7]],
                            ins=[cin3[:, :, :]], outs=[cfin[:, :]])
                        _w(i, dR.h, cpv)
                        i.then_inc(cc.h, 1)
                    cc.inc()
                    E("gpsimd", rsc)

            yfin = yev_l[-1]
            Pyd = pGD((vO.h, yfin))

            def ydma(eng, Pyd=Pyd):
                eGDpre(eng, Pyd)
                i = eng.dma_start(out=ydr[:, :].rearrange("(t p) c -> p t c", p=128),
                                  in_=ybuf[:])
                eGDw(i, Pyd)
                i.then_inc(dY.h, 16)
            dY.inc(16)
            E("gpsimd", ydma)

            Pf = pG((dY.h, dY.n))

            def fin2(eng, Pf=Pf):
                eG(eng, Pf, lambda: eng.memset(finscr[0:1, 0:1], 0.0))
            E("gpsimd", fin2)
            run_block()

        with ExitStack() as ph:
            PS = [ph.enter_context(nc.psum_tensor(f"ob{i}", [128, 512], F32))
                  for i in range(8)]
            orc = ph.enter_context(nc.sbuf_tensor("orc", [128, KC, TOK], BF16))
            wos = ph.enter_context(nc.sbuf_tensor("wos", [128, KC, H], BF16))
            bos = ph.enter_context(nc.sbuf_tensor("bos", [128, H], F32))
            xo3 = ph.enter_context(nc.sbuf_tensor("xo3", [128, 2, H], F32))
            st3 = ph.enter_context(nc.sbuf_tensor("st3", [128, 2, H], F32))
            gt3 = ph.enter_context(nc.sbuf_tensor("gt3", [128, 2, H], F32))
            yt3 = ph.enter_context(nc.sbuf_tensor("yt3", [128, 2, H], F32))

            def wload(eng):
                for fc in range(KC):
                    eng.dma_start(out=wos[:, fc, :],
                                  in_=wo[fc * 128:(fc + 1) * 128, :]).then_inc(dWo.h, 16)
            wlv = dWo.inc(16 * KC)
            E("gpsimd", wload)

            Pol = pSY((cc.h, cc.n))

            def oload(eng, Pol=Pol):
                eSYpre(eng, Pol)
                i = eng.dma_start(out=orc[:],
                                  in_=cfin[:, :].rearrange("(c p) t -> p c t", p=128))
                eSYw(i, Pol)
                i.then_inc(dOr.h, 16)
                eng.dma_start(out=bos[:], in_=bob[:, :]).then_inc(dOr.h, 16)
            olv = dOr.inc(32)
            E("sync", oload)

            sev_l3, sg_l3, outst_l = [], [], []
            for tt in range(8):
                sl = tt % 2

                xdeps = []
                if tt >= 2:
                    xdeps.append((vO.h, sev_l3[tt - 2]))
                    xdeps.append((sO.h, sg_l3[tt - 2]))
                Pxl = pSY(*xdeps)

                def xload(eng, tt=tt, sl=sl, Pxl=Pxl):
                    eSYpre(eng, Pxl)
                    i = eng.dma_start(out=xo3[:, sl, :],
                                      in_=xown[tt * 128:(tt + 1) * 128, :])
                    eSYw(i, Pxl)
                    i.then_inc(dX[sl].h, 16)
                    i = eng.dma_start(out=yt3[:, sl, :],
                                      in_=ydr[tt * 128:(tt + 1) * 128, :])
                    eSYw(i, Pxl)
                    i.then_inc(dX[sl].h, 16)
                xlv = dX[sl].inc(32)
                E("sync", xload)

                mdeps = [(dWo.h, wlv), (dOr.h, olv)]
                if tt >= 2:
                    mdeps.append((vO.h, sev_l3[tt - 2]))
                Pm3 = pT_(*mdeps)

                def smm3(eng, tt=tt, Pm3=Pm3):
                    eTpre(eng, Pm3)
                    bs = (tt % 2) * 4
                    for fc in range(KC):
                        for oc in range(4):
                            i = eng.matmul(PS[bs + oc][:],
                                           orc[:, fc, tt * 128:(tt + 1) * 128],
                                           wos[:, fc, oc * 512:(oc + 1) * 512],
                                           start=(fc == 0), stop=(fc == KC - 1),
                                           skip_group_check=True)
                            eTw(i, Pm3)
                    i.then_inc(tP.h, 1)
                smv = tP.inc()
                E("tensor", smm3)

                gdeps = [(dX[sl].h, xlv)]
                if tt >= 2:
                    gdeps.append((vO.h, sev_l3[tt - 2]))
                Pgs = pS(*gdeps)
                sgv = sO.n
                sg_l3.append(sgv)

                def gsil(eng, sl=sl, Pgs=Pgs):
                    eS(eng, Pgs, lambda: eng.activation(
                        gt3[:, sl, :], yt3[:, sl, :], AF.Silu))
                E("scalar", gsil)

                cdeps = [(tP.h, smv), (dX[sl].h, xlv), (sO.h, sgv)]
                if tt >= 2:
                    cdeps.append((dU[tt % 2].h, outst_l[tt - 2]))
                Pc = [pV(*cdeps)] + [pV() for _ in range(5)]
                sevv = vO.n

                def sevc(eng, tt=tt, sl=sl, Pc=tuple(Pc)):
                    bs = (tt % 2) * 4
                    for oc in range(4):
                        eV(eng, Pc[oc], lambda oc=oc: eng.tensor_tensor(
                            st3[:, sl, oc * 512:(oc + 1) * 512], PS[bs + oc][:],
                            bos[:, oc * 512:(oc + 1) * 512], op=ALU.add))
                    eV(eng, Pc[4], lambda: eng.tensor_tensor(
                        st3[:, sl, :], st3[:, sl, :], gt3[:, sl, :], op=ALU.mult))
                    eV(eng, Pc[5], lambda: eng.tensor_tensor(
                        st3[:, sl, :], st3[:, sl, :], xo3[:, sl, :], op=ALU.add))
                sev_l3.append(sevv)
                E("vector", sevc)

                Pou = pGD((vO.h, sevv))

                def ostore(eng, tt=tt, sl=sl, Pou=Pou):
                    eGDpre(eng, Pou)
                    i = eng.dma_start(out=out[tt * 128:(tt + 1) * 128, :],
                                      in_=st3[:, sl, :])
                    eGDw(i, Pou)
                    i.then_inc(dU[tt % 2].h, 16)
                ov = dU[tt % 2].inc(16)
                outst_l.append(ov)
                E("gpsimd", ostore)

            Pf = pG((dU[0].h, dU[0].n), (dU[1].h, dU[1].n))

            def finout(eng, Pf=Pf):
                eG(eng, Pf, lambda: eng.memset(finscr[0:1, 0:1], 0.0))
            E("gpsimd", finout)
            run_block()

    return nc


def kernel(L0_attn_x, L0_attn_Wqkv, L0_attn_bqkv, L0_attn_Wo, L0_attn_bo,
           L0_ffn_gate_w, L0_ffn_w_gate_up, L0_ffn_w_down,
           L0_ffn_shared_gate_up, L0_ffn_shared_down):
    import ml_dtypes
    x = np.ascontiguousarray(np.asarray(L0_attn_x, np.float32))
    Wqkv = np.asarray(L0_attn_Wqkv, np.float32)
    bqkv = np.asarray(L0_attn_bqkv, np.float32)
    Wo = np.ascontiguousarray(np.asarray(L0_attn_Wo, np.float32))
    bo = np.asarray(L0_attn_bo, np.float32)
    gate_w = np.asarray(L0_ffn_gate_w, np.float32)
    w_gu = np.asarray(L0_ffn_w_gate_up, np.float32)
    w_dn = np.ascontiguousarray(np.asarray(L0_ffn_w_down, np.float32))
    s_gu = np.asarray(L0_ffn_shared_gate_up, np.float32)
    s_dn = np.ascontiguousarray(np.asarray(L0_ffn_shared_down, np.float32))

    slopes = np.array([1.0 / 2 ** (k / 2) for k in range(NH)], np.float32)
    jj = np.arange(S, dtype=np.float32)

    wgu_p = np.ascontiguousarray(
        w_gu.reshape(16, H, 2, 11, 128).transpose(0, 1, 3, 2, 4).reshape(16, H, 11, 256))
    sgu_p = np.ascontiguousarray(
        s_gu.reshape(H, 2, 22, 128).transpose(0, 2, 1, 3).reshape(H, 22, 256))
    gwt = np.ascontiguousarray(gate_w.T)
    bob = np.ascontiguousarray(np.broadcast_to(bo[None, :], (128, H)))
    ii = np.arange(128, dtype=np.float32)
    mtri = np.ascontiguousarray(
        np.where(np.arange(128)[None, :] <= ii[:, None], np.float32(0.0),
                 np.float32(NEG)))
    iden = np.eye(128, dtype=np.float32).astype(ml_dtypes.bfloat16)

    in_maps = []
    for c in range(8):
        b, hg = c // 4, c % 4
        heads = slice(hg * 512, (hg + 1) * 512)
        wq_slice = np.ascontiguousarray(np.concatenate(
            [Wqkv[:, heads], Wqkv[:, H:][:, heads], Wqkv[:, 2 * H:][:, heads]], axis=1))
        bq_s = np.ascontiguousarray(bqkv[:H][heads].reshape(4, 128).T)
        bk_s = np.ascontiguousarray(bqkv[H:2 * H][heads].reshape(4, 128).T)
        bv_s = np.ascontiguousarray(
            np.broadcast_to(bqkv[2 * H:][heads][None, :], (128, 512)))
        jr = np.ascontiguousarray(np.broadcast_to(
            (slopes[4 * hg:4 * hg + 4, None] * jj[None, :])[:, None, :],
            (4, 128, S)))
        in_maps.append({
            "xb": x[b],
            "xown": np.ascontiguousarray(x[b, hg * TOK:(hg + 1) * TOK]),
            "wqkv": wq_slice,
            "bq": bq_s, "bk": bk_s, "bv": bv_s,
            "jrow": jr, "mtri": mtri, "iden": iden,
            "wo": Wo, "bob": bob,
            "gwt": gwt,
            "wgu": wgu_p, "wdn": w_dn,
            "sgu": sgu_p, "sdn": s_dn,
        })

    nc = build_nc()
    res = run_bass_kernel_spmd(nc, in_maps, core_ids=list(range(8)))

    full = np.empty((B, S, H), np.float32)
    for c in range(8):
        b, hg = c // 4, c % 4
        full[b, hg * TOK:(hg + 1) * TOK] = res.results[c]["out"]
    return full


# version 08
# speedup vs baseline: 1.2038x; 1.0325x over previous
import numpy as np
from contextlib import ExitStack

import concourse.bass as bass
import concourse.mybir as mybir
from concourse.bass_utils import run_bass_kernel_spmd
from concourse.alu_op_type import AluOpType as ALU

F32 = mybir.dt.float32
BF16 = mybir.dt.bfloat16
AF = mybir.ActivationFunctionType
AX = mybir.AxisListType.X

B, S, H, NH, DH = 2, 4096, 2048, 16, 128
HPC = 4
TOK = 1024
KC = H // 128
SCALE = float(1.0 / np.sqrt(DH))
NEG = -1.0e30

ENGINES = ("sync", "tensor", "vector", "scalar", "gpsimd")


class Ctr:
    __slots__ = ("h", "n")

    def __init__(self, h):
        self.h = h
        self.n = 0

    def inc(self, k=1):
        self.n += k
        return self.n


def _w(inst, h, val):
    if val > 0:
        inst.wait_op(h, val, "sem-ge", check=False)
    return inst


def _collapse(deps):
    best = {}
    order = []
    for h, v in deps:
        if v is None or v <= 0:
            continue
        key = id(h)
        if key not in best:
            best[key] = [h, v]
            order.append(key)
        elif v > best[key][1]:
            best[key][1] = v
    return [tuple(best[k]) for k in order]


def build_nc():
    nc = bass.Bass("TRN2", target_bir_lowering=False)

    xb = nc.dram_tensor("xb", [S, H], F32, kind="ExternalInput").ap()
    xown = nc.dram_tensor("xown", [TOK, H], F32, kind="ExternalInput").ap()
    wqkv = nc.dram_tensor("wqkv", [H, 1536], F32, kind="ExternalInput").ap()
    bq = nc.dram_tensor("bq", [128, 4], F32, kind="ExternalInput").ap()
    bk = nc.dram_tensor("bk", [128, 4], F32, kind="ExternalInput").ap()
    bv = nc.dram_tensor("bv", [128, 512], F32, kind="ExternalInput").ap()
    jrow = nc.dram_tensor("jrow", [HPC, 128, S], F32, kind="ExternalInput").ap()
    arow = nc.dram_tensor("arow", [HPC, 128, 32], F32, kind="ExternalInput").ap()
    mtri = nc.dram_tensor("mtri", [128, 128], F32, kind="ExternalInput").ap()
    iden = nc.dram_tensor("iden", [128, 128], BF16, kind="ExternalInput").ap()
    wo = nc.dram_tensor("wo", [H, H], F32, kind="ExternalInput").ap()
    bob = nc.dram_tensor("bob", [128, H], F32, kind="ExternalInput").ap()
    gwt = nc.dram_tensor("gwt", [H, 16], F32, kind="ExternalInput").ap()
    wgu = nc.dram_tensor("wgu", [16, H, 11, 256], F32, kind="ExternalInput").ap()
    wdn = nc.dram_tensor("wdn", [16, 1408, H], F32, kind="ExternalInput").ap()
    sgu = nc.dram_tensor("sgu", [H, 22, 256], F32, kind="ExternalInput").ap()
    sdn = nc.dram_tensor("sdn", [2816, H], F32, kind="ExternalInput").ap()
    out = nc.dram_tensor("out", [TOK, H], F32, kind="ExternalOutput").ap()

    dq = nc.dram_tensor("dq", [512, S], BF16, kind="Internal").ap()
    dk = nc.dram_tensor("dk", [512, S], BF16, kind="Internal").ap()
    dv = nc.dram_tensor("dv", [S, 512], BF16, kind="Internal").ap()
    cin = nc.dram_tensor("cin", [512, S], BF16, kind="Internal").ap()
    cout = nc.dram_tensor("cout", [4, 512, S], BF16, kind="Internal").ap()
    cin3 = nc.dram_tensor("cin3", [4, H, TOK], BF16, kind="Internal").ap()
    cfin = nc.dram_tensor("cfin", [H, TOK], BF16, kind="Internal").ap()
    ydr = nc.dram_tensor("ydr", [TOK, H], F32, kind="Internal").ap()

    with ExitStack() as top:
        def SEM(name):
            return Ctr(top.enter_context(nc.semaphore(name)))

        vO, sO, gO, tP, cc = SEM("vO"), SEM("sO"), SEM("gO"), SEM("tP"), SEM("ccs")
        fV, fS, fG, fT, fSY = SEM("fV"), SEM("fS"), SEM("fG"), SEM("fT"), SEM("fSY")
        dX = [SEM(f"dX{i}") for i in range(4)]
        dW = SEM("dW")
        dB = SEM("dB")
        dI = SEM("dI")
        dQs, dKs, dVs = SEM("dQs"), SEM("dKs"), SEM("dVs")
        dH = [SEM("dH0"), SEM("dH1")]
        dJ = SEM("dJ")
        dO = [SEM("dO0"), SEM("dO1")]
        dPsl = [SEM("dP0"), SEM("dP1")]
        dD = SEM("dD")
        dY = SEM("dY")
        dR = SEM("dR")
        dWo = SEM("dWo")
        dOr = SEM("dOr")
        dU = [SEM("dU0"), SEM("dU1")]

        epsT = top.enter_context(nc.sbuf_tensor("epsT", [128, 1], F32))
        finscr = top.enter_context(nc.sbuf_tensor("finscr", [128, 1], F32))
        idsb = top.enter_context(nc.sbuf_tensor("idsb", [128, 128], BF16))

        ops = {name: [] for name in ENGINES}

        def E(name, fn):
            ops[name].append(fn)

        def run_block():
            with nc.Block() as blk:
                for name in ENGINES:
                    if ops[name]:
                        def runner(eng, fns=tuple(ops[name])):
                            for f in fns:
                                f(eng)
                        getattr(blk, name)(runner)
            for name in ENGINES:
                ops[name] = []

        def mkchain(ordc, fanc):
            def plan(*deps):
                evs = _collapse([(ordc.h, ordc.n)] + list(deps))
                if len(evs) > 1:
                    fanc.inc(len(evs))
                    P = (tuple(evs), (fanc.h, fanc.n))
                else:
                    P = ((), evs[0] if evs else None)
                ordc.inc(1)
                return P

            def emit(eng, P, builder):
                for hh, vv in P[0]:
                    _w(eng.sem_inc(fanc.h, 1), hh, vv)
                i = builder()
                if P[1] is not None:
                    _w(i, P[1][0], P[1][1])
                i.then_inc(ordc.h, 1)
                return i
            return plan, emit

        pV, eV = mkchain(vO, fV)
        pS, eS = mkchain(sO, fS)
        pG, eG = mkchain(gO, fG)

        def mkgrp(fanc):
            def plan(*deps):
                evs = _collapse(list(deps))
                if len(evs) > 1:
                    fanc.inc(len(evs))
                    return (tuple(evs), (fanc.h, fanc.n))
                return ((), evs[0] if evs else None)

            def pre(eng, P):
                for hh, vv in P[0]:
                    _w(eng.sem_inc(fanc.h, 1), hh, vv)

            def wt(inst, P):
                if P[1] is not None:
                    _w(inst, P[1][0], P[1][1])
                return inst
            return plan, pre, wt
        pT_, eTpre, eTw = mkgrp(fT)
        pSY, eSYpre, eSYw = mkgrp(fSY)
        pGD, eGDpre, eGDw = mkgrp(fG)

        def ldid(eng):
            eng.dma_start(out=idsb[:], in_=iden[:, :]).then_inc(dI.h, 16)
        idv = dI.inc(16)
        E("gpsimd", ldid)

        Pe0 = pV()

        def setup(eng, Pe0=Pe0):
            eV(eng, Pe0, lambda: eng.memset(epsT[:], 1e-5))
        epsv = vO.n
        E("vector", setup)

        with ExitStack() as ph:
            pb = [ph.enter_context(nc.psum_tensor(f"pb{i}", [128, 512], F32))
                  for i in range(6)]
            tps1 = ph.enter_context(nc.psum_tensor("tps1", [128, 8, 128], BF16))
            xt1 = ph.enter_context(nc.sbuf_tensor("xt1", [128, 4, H], F32))
            scr1 = ph.enter_context(nc.sbuf_tensor("scr1", [128, H], F32))
            stats1 = ph.enter_context(nc.sbuf_tensor("stats1", [128, 4, 8], F32))
            xnb1 = ph.enter_context(nc.sbuf_tensor("xnb1", [128, 2, H], BF16))
            xnT1 = ph.enter_context(nc.sbuf_tensor("xnT1", [128, KC, 1024], BF16))
            wqs = ph.enter_context(nc.sbuf_tensor("wqs", [128, KC, 1536], BF16))
            bqT = ph.enter_context(nc.sbuf_tensor("bqT", [128, 4], F32))
            bkT = ph.enter_context(nc.sbuf_tensor("bkT", [128, 4], F32))
            bvT = ph.enter_context(nc.sbuf_tensor("bvT", [128, 512], F32))
            qev = ph.enter_context(nc.sbuf_tensor("qev", [128, 4, 512], BF16))
            kev = ph.enter_context(nc.sbuf_tensor("kev", [128, 4, 512], BF16))
            vev = ph.enter_context(nc.sbuf_tensor("vev", [128, 4, 512], BF16))

            def ldw(eng):
                eng.dma_start(out=wqs[:],
                              in_=wqkv[:, :].rearrange("(c p) n -> p c n", p=128)
                              ).then_inc(dW.h, 16)
            wqv = dW.inc(16)
            E("gpsimd", ldw)

            def ldb(eng):
                eng.dma_start(out=bqT[:], in_=bq[:, :]).then_inc(dB.h, 16)
                eng.dma_start(out=bkT[:], in_=bk[:, :]).then_inc(dB.h, 16)
                eng.dma_start(out=bvT[:], in_=bv[:, :]).then_inc(dB.h, 16)
            blv = dB.inc(48)
            E("scalar", ldb)

            vmm_l = []
            apv_l, stv_l1 = [], []
            thv_l = []
            evv_l = []
            st_l = []
            vmm_evac_l = []

            for g in range(8):
                for t in range(4 * g, 4 * g + 4):
                    sl = t % 4
                    bsl = t % 2
                    Pld = pSY((vO.h, apv_l[t - 4]) if t >= 4 else (vO.h, 0))

                    def ld(eng, t=t, sl=sl, Pld=Pld):
                        eSYpre(eng, Pld)
                        i = eng.dma_start(out=xt1[:, sl, :],
                                          in_=xb[t * 128:(t + 1) * 128, :])
                        eSYw(i, Pld)
                        i.then_inc(dX[sl].h, 16)
                    ldv = dX[sl].inc(16)
                    E("sync", ld)

                    ldep = (dX[sl].h, ldv)
                    sdep = (sO.h, stv_l1[t - 4]) if t >= 4 else (sO.h, 0)
                    P1 = pV(ldep, sdep)
                    P2 = pV(ldep)
                    P3, P4, P5, P6 = pV(), pV(), pV(), pV()
                    lnv = vO.n

                    def stats_fn(eng, sl=sl, Ps=(P1, P2, P3, P4, P5, P6)):
                        eV(eng, Ps[0], lambda: eng.tensor_reduce(
                            stats1[:, sl, 0:1], xt1[:, sl, :], axis=AX, op=ALU.add))
                        eV(eng, Ps[1], lambda: eng.scalar_tensor_tensor(
                            scr1[:], xt1[:, sl, :], 1.0, xt1[:, sl, :],
                            op0=ALU.mult, op1=ALU.mult,
                            accum_out=stats1[:, sl, 1:2]))
                        eV(eng, Ps[2], lambda: eng.tensor_scalar(
                            stats1[:, sl, 2:3], stats1[:, sl, 0:1], 1.0 / H, None,
                            op0=ALU.mult))
                        eV(eng, Ps[3], lambda: eng.tensor_tensor(
                            stats1[:, sl, 3:4], stats1[:, sl, 2:3],
                            stats1[:, sl, 2:3], op=ALU.mult))
                        eV(eng, Ps[4], lambda: eng.tensor_scalar(
                            stats1[:, sl, 4:5], stats1[:, sl, 1:2], 1.0 / H, None,
                            op0=ALU.mult))
                        eV(eng, Ps[5], lambda: eng.tensor_tensor(
                            stats1[:, sl, 4:5], stats1[:, sl, 4:5],
                            stats1[:, sl, 3:4], op=ALU.subtract))
                    E("vector", stats_fn)

                    Pst = pS((vO.h, lnv), (vO.h, epsv))

                    def stdf(eng, sl=sl, Pst=Pst):
                        eS(eng, Pst, lambda: eng.activation(
                            stats1[:, sl, 5:6], stats1[:, sl, 4:5], AF.Sqrt,
                            bias=epsT[:, 0:1]))
                    stv = sO.n
                    stv_l1.append(stv)
                    E("scalar", stdf)

                    Pa1 = pV((sO.h, stv),
                             (tP.h, thv_l[2 * (t - 2) + 1]) if t >= 2 else (tP.h, 0))
                    Pa2 = pV()

                    def app(eng, sl=sl, bsl=bsl, Pa1=Pa1, Pa2=Pa2):
                        eV(eng, Pa1, lambda: eng.reciprocal(
                            stats1[:, sl, 6:7], stats1[:, sl, 5:6]))
                        eV(eng, Pa2, lambda: eng.tensor_scalar(
                            xnb1[:, bsl, :], xt1[:, sl, :],
                            stats1[:, sl, 2:3], stats1[:, sl, 6:7],
                            op0=ALU.subtract, op1=ALU.mult))
                    apv = vO.n
                    apv_l.append(apv)
                    E("vector", app)

                    gidx = g % 2
                    for half in range(2):
                        hidx = 2 * t + half
                        tdeps = [(vO.h, apv), (dI.h, idv)]
                        if g >= 2 and t % 4 == 0 and half == 0:
                            tdeps.append((tP.h, vmm_l[g - 2]))
                        if hidx >= 1:
                            tdeps.append((vO.h, evv_l[hidx - 1]))
                        Ptr = pT_(*tdeps)

                        def trf(eng, bsl=bsl, half=half, Ptr=Ptr):
                            eTpre(eng, Ptr)
                            for s8 in range(8):
                                kch = half * 8 + s8
                                i = eng.transpose(tps1[:, s8, :],
                                                  xnb1[:, bsl, kch * 128:(kch + 1) * 128],
                                                  idsb[:])
                                eTw(i, Ptr)
                            i.then_inc(tP.h, 1)
                        thv = tP.inc()
                        thv_l.append(thv)
                        E("tensor", trf)

                        base = gidx * 512 + (t % 4) * 128
                        Pev = pV((tP.h, thv))

                        def evf(eng, half=half, base=base, Pev=Pev):
                            eV(eng, Pev, lambda: eng.tensor_copy(
                                xnT1[:, half * 8:half * 8 + 8, base:base + 128],
                                tps1[:, 0:8, :]))
                        evv_l.append(vO.n)
                        E("vector", evf)

                gs = (g % 2) * 512
                xnrdy = evv_l[2 * (4 * g + 3) + 1]
                qdeps = [(vO.h, xnrdy)]
                if g == 0:
                    qdeps.append((dW.h, wqv))
                else:
                    qdeps.append((vO.h, vmm_evac_l[g - 1][2]))
                Pq = pT_(*qdeps)

                def q_mms(eng, gs=gs, Pq=Pq):
                    eTpre(eng, Pq)
                    for oc in range(4):
                        for kc in range(KC):
                            i = eng.matmul(pb[oc][:],
                                           wqs[:, kc, oc * 128:(oc + 1) * 128],
                                           xnT1[:, kc, gs:gs + 512],
                                           start=(kc == 0), stop=(kc == KC - 1),
                                           skip_group_check=True)
                            eTw(i, Pq)
                    i.then_inc(tP.h, 1)
                mq = tP.inc()
                E("tensor", q_mms)

                k1deps = [(vO.h, xnrdy)]
                if g == 0:
                    k1deps.append((dW.h, wqv))
                else:
                    k1deps.append((vO.h, vmm_evac_l[g - 1][1]))
                Pk1 = pT_(*k1deps)

                def k1_mms(eng, gs=gs, Pk1=Pk1):
                    eTpre(eng, Pk1)
                    for oc in range(2):
                        for kc in range(KC):
                            i = eng.matmul(pb[4 + oc][:],
                                           wqs[:, kc, 512 + oc * 128:512 + (oc + 1) * 128],
                                           xnT1[:, kc, gs:gs + 512],
                                           start=(kc == 0), stop=(kc == KC - 1),
                                           skip_group_check=True)
                            eTw(i, Pk1)
                    i.then_inc(tP.h, 1)
                mk1 = tP.inc()
                E("tensor", k1_mms)

                Pqe = [None] * 6
                Pqe[0] = pV((tP.h, mq), (dB.h, blv),
                            (dQs.h, st_l[g - 1][0]) if g >= 1 else (dQs.h, 0))
                Pqe[1] = pV()
                Pqe[2] = pV()
                Pqe[3] = pV()
                Pqe[4] = pV((tP.h, mk1),
                            (dKs.h, st_l[g - 1][1]) if g >= 1 else (dKs.h, 0))
                Pqe[5] = pV()
                qevv = vO.n - 2
                k1evv = vO.n

                def qk1_ev(eng, Pqe=tuple(Pqe)):
                    for oc in range(4):
                        eV(eng, Pqe[oc], lambda oc=oc: eng.tensor_scalar(
                            qev[:, oc, :], pb[oc][:], bqT[:, oc:oc + 1], None,
                            op0=ALU.add))
                    for oc in range(2):
                        eV(eng, Pqe[4 + oc], lambda oc=oc: eng.tensor_scalar(
                            kev[:, oc, :], pb[4 + oc][:], bkT[:, oc:oc + 1], None,
                            op0=ALU.add))
                E("vector", qk1_ev)

                Pk2 = pT_((vO.h, k1evv))

                def k2_mms(eng, gs=gs, Pk2=Pk2):
                    eTpre(eng, Pk2)
                    for oc in range(2, 4):
                        for kc in range(KC):
                            i = eng.matmul(pb[2 + oc][:],
                                           wqs[:, kc, 512 + oc * 128:512 + (oc + 1) * 128],
                                           xnT1[:, kc, gs:gs + 512],
                                           start=(kc == 0), stop=(kc == KC - 1),
                                           skip_group_check=True)
                            eTw(i, Pk2)
                    i.then_inc(tP.h, 1)
                mk2 = tP.inc()
                E("tensor", k2_mms)

                Pvm = pT_((vO.h, qevv))

                def v_mms(eng, gs=gs, Pvm=Pvm):
                    eTpre(eng, Pvm)
                    for tt in range(4):
                        for kc in range(KC):
                            i = eng.matmul(pb[tt][:],
                                           xnT1[:, kc, gs + tt * 128:gs + (tt + 1) * 128],
                                           wqs[:, kc, 1024:1536],
                                           start=(kc == 0), stop=(kc == KC - 1),
                                           skip_group_check=True)
                            eTw(i, Pvm)
                    i.then_inc(tP.h, 1)
                mv = tP.inc()
                vmm_l.append(mv)
                E("tensor", v_mms)

                Pk2e = [None] * 6
                Pk2e[0] = pV((tP.h, mk2))
                Pk2e[1] = pV()
                Pk2e[2] = pV((tP.h, mv), (dB.h, blv),
                             (dVs.h, st_l[g - 1][2]) if g >= 1 else (dVs.h, 0))
                Pk2e[3] = pV()
                Pk2e[4] = pV()
                Pk2e[5] = pV()
                k2evv = vO.n - 4
                vevv = vO.n

                def k2v_ev(eng, Pk2e=tuple(Pk2e)):
                    for j, oc in enumerate(range(2, 4)):
                        eV(eng, Pk2e[j], lambda oc=oc: eng.tensor_scalar(
                            kev[:, oc, :], pb[2 + oc][:], bkT[:, oc:oc + 1], None,
                            op0=ALU.add))
                    for tt in range(4):
                        eV(eng, Pk2e[2 + tt], lambda tt=tt: eng.tensor_tensor(
                            vev[:, tt, :], pb[tt][:], bvT[:], op=ALU.add))
                vmm_evac_l.append((qevv, k2evv, vevv))
                E("vector", k2v_ev)

                Pst1 = pGD((vO.h, qevv))
                Pst2 = pGD((vO.h, k2evv))
                Pst3 = pGD((vO.h, vevv))

                def qkv_st(eng, g=g, Pst1=Pst1, Pst2=Pst2, Pst3=Pst3):
                    eGDpre(eng, Pst1)
                    i = eng.dma_start(
                        out=dq[:, :].rearrange("(oc p) t -> p oc t", p=128)[:, :, g * 512:(g + 1) * 512],
                        in_=qev[:])
                    eGDw(i, Pst1)
                    i.then_inc(dQs.h, 16)
                    eGDpre(eng, Pst2)
                    i = eng.dma_start(
                        out=dk[:, :].rearrange("(oc p) t -> p oc t", p=128)[:, :, g * 512:(g + 1) * 512],
                        in_=kev[:])
                    eGDw(i, Pst2)
                    i.then_inc(dKs.h, 16)
                    eGDpre(eng, Pst3)
                    i = eng.dma_start(
                        out=dv[:, :].rearrange("(tt p) c -> p tt c", p=128)[:, g * 4:(g + 1) * 4, :],
                        in_=vev[:])
                    eGDw(i, Pst3)
                    i.then_inc(dVs.h, 16)
                dQs.inc(16); dKs.inc(16); dVs.inc(16)
                st_l.append((dQs.n, dKs.n, dVs.n))
                E("gpsimd", qkv_st)

            Pf = pG((dQs.h, dQs.n), (dKs.h, dKs.n), (dVs.h, dVs.n))

            def fin1a(eng, Pf=Pf):
                eG(eng, Pf, lambda: eng.memset(finscr[0:1, 0:1], 0.0))
            E("gpsimd", fin1a)
            run_block()

        with ExitStack() as ph:
            scb = [ph.enter_context(nc.psum_tensor(f"sb{i}", [128, 512], F32))
                   for i in range(3)]
            avb = ph.enter_context(nc.psum_tensor("avb", [128, 2, 256], F32))
            tps2 = ph.enter_context(nc.psum_tensor("tps2", [128, 2, 8, 128], BF16))
            qh = ph.enter_context(nc.sbuf_tensor("qh", [128, 2, S], BF16))
            kh = ph.enter_context(nc.sbuf_tensor("kh", [128, 2, S], BF16))
            vh = ph.enter_context(nc.sbuf_tensor("vh", [128, 2, 32, 128], BF16))
            jbc = ph.enter_context(nc.sbuf_tensor("jbc", [128, S], F32))
            sc = ph.enter_context(nc.sbuf_tensor("sc", [128, 3, S], F32))
            pg = ph.enter_context(nc.sbuf_tensor("pg", [128, 3, S], BF16))
            pn = ph.enter_context(nc.sbuf_tensor("pn", [128, 3, S], BF16))
            pt = ph.enter_context(nc.sbuf_tensor("pt", [128, 2, 32, 256], BF16))
            oev = ph.enter_context(nc.sbuf_tensor("oev", [128, 2, 256], BF16))
            acols = ph.enter_context(nc.sbuf_tensor("acols", [128, 3, 4], F32))
            mtri_sb = ph.enter_context(nc.sbuf_tensor("mtri_sb", [128, 128], F32))
            arow_sb = ph.enter_context(nc.sbuf_tensor("arow_sb", [128, HPC, 32], F32))

            def ldmtri(eng):
                eng.dma_start(out=mtri_sb[:], in_=mtri[:, :]).then_inc(dB.h, 16)
                eng.dma_start(
                    out=arow_sb[:],
                    in_=arow[:, :, :].rearrange("h p c -> p h c")).then_inc(dB.h, 16)
            mtv = dB.inc(32)
            E("scalar", ldmtri)

            sev_l, exp_l, pnv_l, ptev_l = [], [], [], []
            avv_l, oevv_l, ost_l = [], [], []
            ck_cnt = 0
            qt_cnt = 0
            qg_cnt = 0

            qlv_l, jlv_l = [], []
            ck_cnt = 0

            def emit_ldh(h):
                hs = h % 2
                hdeps = []
                if h >= 2:
                    hdeps = [(vO.h, sev_l[144 * (h - 1) - 1]),
                             (tP.h, avv_l[16 * (h - 1) - 1])]
                Plh = pSY(*hdeps)

                def ldh(eng, h=h, hs=hs, Plh=Plh):
                    eSYpre(eng, Plh)
                    i = eng.dma_start(out=qh[:, hs, :], in_=dq[h * 128:(h + 1) * 128, :])
                    eSYw(i, Plh)
                    i.then_inc(dH[hs].h, 16)
                    i = eng.dma_start(out=kh[:, hs, :], in_=dk[h * 128:(h + 1) * 128, :])
                    eSYw(i, Plh)
                    i.then_inc(dH[hs].h, 16)
                    i = eng.dma_start(
                        out=vh[:, hs, :, :],
                        in_=dv[:, :].rearrange("(kt p) c -> p kt c", p=128)[:, :, h * 128:(h + 1) * 128])
                    eSYw(i, Plh)
                    i.then_inc(dH[hs].h, 16)
                qlv_l.append(dH[hs].inc(48))
                E("sync", ldh)

                Plj = pGD((vO.h, sev_l[144 * h - 1]) if h >= 1 else (vO.h, 0))

                def ldj(eng, h=h, Plj=Plj):
                    eGDpre(eng, Plj)
                    i = eng.dma_start(out=jbc[:], in_=jrow[h, :, :])
                    eGDw(i, Plj)
                    i.then_inc(dJ.h, 16)
                jlv_l.append(dJ.inc(16))
                E("gpsimd", ldj)

            def stageA(gq):
                nonlocal ck_cnt
                h, qt = gq // 32, gq % 32
                hs = h % 2
                qlv, jlv = qlv_l[h], jlv_l[h]
                Eext = (qt + 1) * 128
                ssl = gq % 3
                nck = (Eext + 511) // 512
                for ck in range(nck):
                    k0 = ck * 512
                    k1 = min(Eext, k0 + 512)
                    bank = ck_cnt % 3
                    mdeps = [(dH[hs].h, qlv)]
                    if ck_cnt >= 3:
                        mdeps.append((vO.h, sev_l[ck_cnt - 3]))
                    Pm = pT_(*mdeps)

                    def smm(eng, hs=hs, qt=qt, k0=k0, k1=k1, bank=bank, Pm=Pm):
                        eTpre(eng, Pm)
                        i = eng.matmul(scb[bank][:, 0:k1 - k0],
                                       qh[:, hs, qt * 128:(qt + 1) * 128],
                                       kh[:, hs, k0:k1], start=True, stop=True,
                                       skip_group_check=True)
                        eTw(i, Pm)
                        i.then_inc(tP.h, 1)
                    mmv = tP.inc()
                    E("tensor", smm)

                    edeps = [(tP.h, mmv), (dJ.h, jlv)]
                    if gq >= 3 and ck == 0:
                        edeps.append((sO.h, exp_l[gq - 3]))
                    Pe_ = pV(*edeps)

                    def sev(eng, ssl=ssl, k0=k0, k1=k1, bank=bank, Pe_=Pe_):
                        eV(eng, Pe_, lambda: eng.scalar_tensor_tensor(
                            sc[:, ssl, k0:k1], scb[bank][:, 0:k1 - k0],
                            SCALE, jbc[:, k0:k1], op0=ALU.mult, op1=ALU.add))
                    sev_l.append(vO.n)
                    E("vector", sev)
                    ck_cnt += 1

                Pn1 = pV((dB.h, mtv))
                nmv = vO.n

                def nmx(eng, ssl=ssl, qt=qt, Pn1=Pn1):
                    eV(eng, Pn1, lambda: eng.tensor_tensor(
                        sc[:, ssl, qt * 128:(qt + 1) * 128],
                        sc[:, ssl, qt * 128:(qt + 1) * 128],
                        mtri_sb[:], op=ALU.add))
                E("vector", nmx)

                Px = pS((vO.h, nmv))

                def expf(eng, ssl=ssl, Eext=Eext, h=h, qt=qt, Px=Px):
                    eS(eng, Px, lambda: eng.activation(
                        pg[:, ssl, 0:Eext], sc[:, ssl, 0:Eext], AF.Exp,
                        bias=arow_sb[:, h, qt:qt + 1], accum_out=acols[:, ssl, 1:2]))
                exp_l.append(sO.n)
                E("scalar", expf)

            def stageB(gq):
                ssl = gq % 3
                Eext = (gq % 32 + 1) * 128
                Pr1 = pV((sO.h, exp_l[gq]))
                rcv = vO.n

                def recf(eng, ssl=ssl, Pr1=Pr1):
                    eV(eng, Pr1, lambda: eng.reciprocal(
                        acols[:, ssl, 2:3], acols[:, ssl, 1:2]))
                E("vector", recf)

                Pmu = pS((vO.h, rcv))

                def pnorm(eng, ssl=ssl, Eext=Eext, Pmu=Pmu):
                    eS(eng, Pmu, lambda: eng.mul(
                        pn[:, ssl, 0:Eext], pg[:, ssl, 0:Eext],
                        acols[:, ssl, 2:3]))
                pnv_l.append(sO.n)
                E("scalar", pnorm)

            def stageC1(p):
                g1 = 2 * p + 1
                nkt = g1 % 32 + 1
                par = p % 2
                Pz = pV((tP.h, avv_l[p - 2]) if p >= 2 else (tP.h, 0))

                def ptz(eng, nkt=nkt, par=par, Pz=Pz):
                    eV(eng, Pz, lambda: eng.memset(
                        pt[:, par, nkt - 1, 0:128], 0.0))
                E("vector", ptz)

                runs = [((g1 - 1) % 3, 0, nkt - 1), (g1 % 3, 1, nkt)]
                wave_ev = []
                widx = 0
                for src, qs, cnt in runs:
                    for w0 in range(0, cnt, 8):
                        wn = min(8, cnt - w0)
                        half = widx % 2
                        wdeps = [(sO.h, pnv_l[g1]), (dI.h, idv)]
                        if p >= 2:
                            wdeps.append((tP.h, avv_l[p - 2]))
                        if widx >= 2:
                            wdeps.append((vO.h, wave_ev[widx - 2]))
                        Pw = pT_(*wdeps)

                        def wtr(eng, src=src, w0=w0, wn=wn, half=half, Pw=Pw):
                            eTpre(eng, Pw)
                            for s8 in range(wn):
                                i = eng.transpose(
                                    tps2[:, half, s8, :],
                                    pn[:, src, (w0 + s8) * 128:(w0 + s8 + 1) * 128],
                                    idsb[:])
                                eTw(i, Pw)
                            i.then_inc(tP.h, 1)
                        wtv = tP.inc()
                        E("tensor", wtr)

                        Pwe = pV((tP.h, wtv))

                        def wev(eng, qs=qs, w0=w0, wn=wn, half=half,
                                par=par, Pwe=Pwe):
                            eV(eng, Pwe, lambda: eng.tensor_copy(
                                pt[:, par, w0:w0 + wn, qs * 128:(qs + 1) * 128],
                                tps2[:, half, 0:wn, :]))
                        wave_ev.append(vO.n)
                        E("vector", wev)
                        widx += 1
                ptev_l.append(wave_ev[-1])

            def stageC2(p):
                h = p // 16
                hs = h % 2
                qlv = qlv_l[h]
                qg = p % 16
                nkt = (2 * p + 1) % 32 + 1
                osl = p % 2
                par = p % 2
                adeps = [(vO.h, ptev_l[p]), (dH[hs].h, qlv)]
                if p >= 2:
                    adeps.append((vO.h, oevv_l[p - 2]))
                Pa = pT_(*adeps)

                def avm(eng, hs=hs, nkt=nkt, osl=osl, par=par, Pa=Pa):
                    eTpre(eng, Pa)
                    for kt in range(nkt):
                        i = eng.matmul(avb[:, osl, :], vh[:, hs, kt, :],
                                       pt[:, par, kt, 0:256],
                                       start=(kt == 0), stop=(kt == nkt - 1),
                                       skip_group_check=True)
                        eTw(i, Pa)
                    i.then_inc(tP.h, 1)
                avv = tP.inc()
                avv_l.append(avv)
                E("tensor", avm)

                odeps = [(tP.h, avv)]
                if p >= 2:
                    odeps.append((dO[p % 2].h, ost_l[p - 2]))
                Po = pV(*odeps)

                def oevf(eng, osl=osl, Po=Po):
                    eV(eng, Po, lambda: eng.tensor_copy(
                        oev[:, osl, :], avb[:, osl, :]))
                oevv_l.append(vO.n)
                E("vector", oevf)

                Pos = pGD((vO.h, oevv_l[-1]))

                def ost(eng, h=h, qg=qg, osl=osl, Pos=Pos, par=par):
                    eGDpre(eng, Pos)
                    i = eng.dma_start(
                        out=cin[:, :].rearrange("(hh p) t -> p hh t", p=128)[:, h, qg * 256:(qg + 1) * 256],
                        in_=oev[:, osl, :])
                    eGDw(i, Pos)
                    i.then_inc(dO[par].h, 16)
                ost_l.append(dO[par].inc(16))
                E("gpsimd", ost)

            NGQ = HPC * 32
            for gq in range(NGQ):
                if gq % 32 == 0:
                    emit_ldh(gq // 32)
                if gq >= 1:
                    stageB(gq - 1)
                stageA(gq)
                if gq >= 2 and gq % 2 == 0:
                    stageC1((gq - 2) // 2)
                if gq >= 3 and gq % 2 == 1:
                    stageC2((gq - 3) // 2)
            stageB(NGQ - 1)
            stageC1((NGQ - 2) // 2)
            stageC2((NGQ - 2) // 2)

            Pf = pG((dO[0].h, dO[0].n), (dO[1].h, dO[1].n))

            def fin1b(eng, Pf=Pf):
                eG(eng, Pf, lambda: eng.memset(finscr[0:1, 0:1], 0.0))
            E("gpsimd", fin1b)
            run_block()

        xnT2 = top.enter_context(nc.sbuf_tensor("xnT2", [128, KC, TOK], BF16))
        comb = top.enter_context(nc.sbuf_tensor("comb", [128, 8, 16], F32))
        with ExitStack() as ph:
            gb = [ph.enter_context(nc.psum_tensor(f"gb{i}", [128, 512], F32))
                  for i in range(2)]
            tps3 = ph.enter_context(nc.psum_tensor("tps3", [128, 8, 128], BF16))
            xt2 = ph.enter_context(nc.sbuf_tensor("xt2", [128, 2, H], F32))
            scr2 = ph.enter_context(nc.sbuf_tensor("scr2", [128, H], F32))
            stats2 = ph.enter_context(nc.sbuf_tensor("stats2", [128, 2, 8], F32))
            xnb2 = ph.enter_context(nc.sbuf_tensor("xnb2", [128, 2, H], BF16))
            gw2 = ph.enter_context(nc.sbuf_tensor("gw2", [128, KC, 16], BF16))
            gl = ph.enter_context(nc.sbuf_tensor("gl", [128, 8, 16], F32))
            gcols = ph.enter_context(nc.sbuf_tensor("gcols", [128, 8, 8], F32))

            def agather(eng):
                eng.collective_compute(
                    "AllGather", ALU.bypass,
                    replica_groups=[[0, 1, 2, 3], [4, 5, 6, 7]],
                    ins=[cin[:, :]], outs=[cout[:, :, :]]).then_inc(cc.h, 1)
            cc.inc()
            E("gpsimd", agather)

            apv2, stv2, thv2, evv2 = [], [], [], []
            for t in range(8):
                sl = t % 2
                bsl = t % 2
                Pld = pSY((vO.h, apv2[t - 2]) if t >= 2 else (vO.h, 0))

                def ld(eng, t=t, sl=sl, Pld=Pld):
                    eSYpre(eng, Pld)
                    i = eng.dma_start(out=xt2[:, sl, :],
                                      in_=xown[t * 128:(t + 1) * 128, :])
                    eSYw(i, Pld)
                    i.then_inc(dX[sl].h, 16)
                ldv = dX[sl].inc(16)
                E("sync", ld)

                ldep = (dX[sl].h, ldv)
                sdep = (sO.h, stv2[t - 2]) if t >= 2 else (sO.h, 0)
                P1 = pV(ldep, sdep)
                P2 = pV(ldep)
                P3, P4, P5, P6 = pV(), pV(), pV(), pV()
                lnv = vO.n

                def stats_fn(eng, sl=sl, Ps=(P1, P2, P3, P4, P5, P6)):
                    eV(eng, Ps[0], lambda: eng.tensor_reduce(
                        stats2[:, sl, 0:1], xt2[:, sl, :], axis=AX, op=ALU.add))
                    eV(eng, Ps[1], lambda: eng.scalar_tensor_tensor(
                        scr2[:], xt2[:, sl, :], 1.0, xt2[:, sl, :],
                        op0=ALU.mult, op1=ALU.mult, accum_out=stats2[:, sl, 1:2]))
                    eV(eng, Ps[2], lambda: eng.tensor_scalar(
                        stats2[:, sl, 2:3], stats2[:, sl, 0:1], 1.0 / H, None,
                        op0=ALU.mult))
                    eV(eng, Ps[3], lambda: eng.tensor_tensor(
                        stats2[:, sl, 3:4], stats2[:, sl, 2:3], stats2[:, sl, 2:3],
                        op=ALU.mult))
                    eV(eng, Ps[4], lambda: eng.tensor_scalar(
                        stats2[:, sl, 4:5], stats2[:, sl, 1:2], 1.0 / H, None,
                        op0=ALU.mult))
                    eV(eng, Ps[5], lambda: eng.tensor_tensor(
                        stats2[:, sl, 4:5], stats2[:, sl, 4:5], stats2[:, sl, 3:4],
                        op=ALU.subtract))
                E("vector", stats_fn)

                Pst = pS((vO.h, lnv), (vO.h, epsv))

                def stdf(eng, sl=sl, Pst=Pst):
                    eS(eng, Pst, lambda: eng.activation(
                        stats2[:, sl, 5:6], stats2[:, sl, 4:5], AF.Sqrt,
                        bias=epsT[:, 0:1]))
                stv = sO.n
                stv2.append(stv)
                E("scalar", stdf)

                Pa1 = pV((sO.h, stv),
                         (tP.h, thv2[2 * (t - 2) + 1]) if t >= 2 else (tP.h, 0))
                Pa2 = pV()

                def app(eng, sl=sl, bsl=bsl, Pa1=Pa1, Pa2=Pa2):
                    eV(eng, Pa1, lambda: eng.reciprocal(
                        stats2[:, sl, 6:7], stats2[:, sl, 5:6]))
                    eV(eng, Pa2, lambda: eng.tensor_scalar(
                        xnb2[:, bsl, :], xt2[:, sl, :],
                        stats2[:, sl, 2:3], stats2[:, sl, 6:7],
                        op0=ALU.subtract, op1=ALU.mult))
                apv = vO.n
                apv2.append(apv)
                E("vector", app)

                for half in range(2):
                    hidx = 2 * t + half
                    tdeps = [(vO.h, apv), (dI.h, idv)]
                    if hidx >= 1:
                        tdeps.append((vO.h, evv2[hidx - 1]))
                    Ptr = pT_(*tdeps)

                    def trf(eng, bsl=bsl, half=half, Ptr=Ptr):
                        eTpre(eng, Ptr)
                        for s8 in range(8):
                            kch = half * 8 + s8
                            i = eng.transpose(tps3[:, s8, :],
                                              xnb2[:, bsl, kch * 128:(kch + 1) * 128],
                                              idsb[:])
                            eTw(i, Ptr)
                        i.then_inc(tP.h, 1)
                    thv = tP.inc()
                    thv2.append(thv)
                    E("tensor", trf)

                    Pev = pV((tP.h, thv))

                    def evf(eng, half=half, t=t, Pev=Pev):
                        eV(eng, Pev, lambda: eng.tensor_copy(
                            xnT2[:, half * 8:half * 8 + 8, t * 128:(t + 1) * 128],
                            tps3[:, 0:8, :]))
                    evv2.append(vO.n)
                    E("vector", evf)
            xnrdy2 = evv2[-1]

            def ldgw(eng):
                eng.dma_start(out=gw2[:],
                              in_=gwt[:, :].rearrange("(c p) n -> p c n", p=128)
                              ).then_inc(dW.h, 16)
            gwv = dW.inc(16)
            E("gpsimd", ldgw)

            gev_l2 = []
            for tt in range(8):
                gdeps = [(dW.h, gwv), (vO.h, xnrdy2)]
                if tt >= 2:
                    gdeps.append((vO.h, gev_l2[tt - 2]))
                Pg_ = pT_(*gdeps)

                def gmm(eng, tt=tt, Pg_=Pg_):
                    eTpre(eng, Pg_)
                    for kc in range(KC):
                        i = eng.matmul(gb[tt % 2][:, 0:16],
                                       xnT2[:, kc, tt * 128:(tt + 1) * 128],
                                       gw2[:, kc, :], start=(kc == 0),
                                       stop=(kc == KC - 1), skip_group_check=True)
                        eTw(i, Pg_)
                    i.then_inc(tP.h, 1)
                gmv = tP.inc()
                E("tensor", gmm)

                Pg1 = pV((tP.h, gmv))
                Pg2 = pV()
                gev_l2.append(vO.n - 1)
                gevv = vO.n

                def gev(eng, tt=tt, Pg1=Pg1, Pg2=Pg2):
                    eV(eng, Pg1, lambda: eng.tensor_copy(gl[:, tt, :],
                                                         gb[tt % 2][:, 0:16]))
                    eV(eng, Pg2, lambda: eng.tensor_reduce(
                        gcols[:, tt, 0:1], gl[:, tt, :], axis=AX, op=ALU.max,
                        negate=True))
                E("vector", gev)

                Pgx = pS((vO.h, gevv))
                gxv = sO.n

                def gexp(eng, tt=tt, Pgx=Pgx):
                    eS(eng, Pgx, lambda: eng.activation(
                        comb[:, tt, :], gl[:, tt, :], AF.Exp,
                        bias=gcols[:, tt, 0:1], accum_out=gcols[:, tt, 1:2]))
                E("scalar", gexp)

                Pt_ = [pV((sO.h, gxv))] + [pV() for _ in range(7)]

                def gtop(eng, tt=tt, Pt_=tuple(Pt_)):
                    eV(eng, Pt_[0], lambda: eng.reciprocal(
                        gcols[:, tt, 2:3], gcols[:, tt, 1:2]))
                    eV(eng, Pt_[1], lambda: eng.tensor_scalar(
                        gl[:, tt, :], comb[:, tt, :], gcols[:, tt, 2:3], None,
                        op0=ALU.mult))
                    eV(eng, Pt_[2], lambda: eng.tensor_reduce(
                        gcols[:, tt, 3:4], gl[:, tt, :], axis=AX, op=ALU.max))
                    eV(eng, Pt_[3], lambda: eng.tensor_scalar(
                        comb[:, tt, :], gl[:, tt, :], gcols[:, tt, 3:4], None,
                        op0=ALU.is_equal))
                    eV(eng, Pt_[4], lambda: eng.scalar_tensor_tensor(
                        comb[:, tt, :], comb[:, tt, :], -1.0e30, gl[:, tt, :],
                        op0=ALU.mult, op1=ALU.add))
                    eV(eng, Pt_[5], lambda: eng.tensor_reduce(
                        gcols[:, tt, 4:5], comb[:, tt, :], axis=AX, op=ALU.max))
                    eV(eng, Pt_[6], lambda: eng.tensor_scalar(
                        comb[:, tt, :], gl[:, tt, :], gcols[:, tt, 4:5], None,
                        op0=ALU.is_ge))
                    eV(eng, Pt_[7], lambda: eng.tensor_tensor(
                        comb[:, tt, :], comb[:, tt, :], gl[:, tt, :], op=ALU.mult))
                E("vector", gtop)
            combv = vO.n
            run_block()

        with ExitStack() as ph:
            PS = [ph.enter_context(nc.psum_tensor(f"eb{i}", [128, 512], F32))
                  for i in range(8)]
            pslab = ph.enter_context(nc.sbuf_tensor("pslab", [128, 2, KC, 256], BF16))
            dslab = ph.enter_context(nc.sbuf_tensor("dslab", [128, 11, H], BF16))
            hbuf = ph.enter_context(nc.sbuf_tensor("hbuf", [128, 11, TOK], BF16))
            silt = ph.enter_context(nc.sbuf_tensor("silt", [128, 2, 512], BF16))
            ybuf = ph.enter_context(nc.sbuf_tensor("ybuf", [128, 8, H], F32))

            Pyz = pV()

            def yz(eng, Pyz=Pyz):
                eV(eng, Pyz, lambda: eng.memset(ybuf[:], 0.0))
            yz_v = vO.n
            E("vector", yz)

            plist = [(0, e) for e in range(16)] + [(1, 0), (1, 1)]
            gp_cnt = 0
            gu_l, sil_l, hm_l, dn_l, yev_l = [], [], [], [], []

            for pe, (kind, e) in enumerate(plist):
                for p in range(11):
                    psl = gp_cnt % 2
                    src = wgu[e, :, p, :] if kind == 0 else sgu[:, e * 11 + p, :]
                    Ppl = pGD((tP.h, gu_l[gp_cnt - 2]) if gp_cnt >= 2 else (tP.h, 0))

                    def pload(eng, src=src, psl=psl, Ppl=Ppl):
                        eGDpre(eng, Ppl)
                        i = eng.dma_start(out=pslab[:, psl, :, :],
                                          in_=src.rearrange("(c p) n -> p c n", p=128))
                        eGDw(i, Ppl)
                        i.then_inc(dPsl[psl].h, 16)
                    plv = dPsl[psl].inc(16)
                    E("gpsimd", pload)

                    gdeps = [(dPsl[psl].h, plv)]
                    if gp_cnt >= 2:
                        gdeps.append((vO.h, hm_l[gp_cnt - 2]))
                        gdeps.append((sO.h, sil_l[gp_cnt - 2]))
                    if gp_cnt == 0:
                        gdeps.append((vO.h, combv))
                    if pe >= 1 and p <= 1:
                        gdeps.append((vO.h, yev_l[8 * pe - 1]))
                    Pgu = pT_(*gdeps)

                    def gumm(eng, psl=psl, Pgu=Pgu):
                        eTpre(eng, Pgu)
                        for kc in range(KC):
                            for th in range(2):
                                i = eng.matmul(PS[4 * psl + th][:],
                                               pslab[:, psl, kc, 0:128],
                                               xnT2[:, kc, th * 512:(th + 1) * 512],
                                               start=(kc == 0), stop=(kc == KC - 1),
                                               skip_group_check=True)
                                eTw(i, Pgu)
                                i2 = eng.matmul(PS[4 * psl + 2 + th][:],
                                                pslab[:, psl, kc, 128:256],
                                                xnT2[:, kc, th * 512:(th + 1) * 512],
                                                start=(kc == 0), stop=(kc == KC - 1),
                                                skip_group_check=True)
                                eTw(i2, Pgu)
                        i2.then_inc(tP.h, 1)
                    guv = tP.inc()
                    gu_l.append(guv)
                    E("tensor", gumm)

                    sdeps = [(tP.h, guv)]
                    if gp_cnt >= 1:
                        sdeps.append((vO.h, hm_l[gp_cnt - 1]))
                    Ps1 = pS(*sdeps)
                    Ps2 = pS()
                    silv = sO.n

                    def hsil(eng, psl=psl, Ps1=Ps1, Ps2=Ps2):
                        eS(eng, Ps1, lambda: eng.activation(
                            silt[:, 0, :], PS[4 * psl + 0][:], AF.Silu))
                        eS(eng, Ps2, lambda: eng.activation(
                            silt[:, 1, :], PS[4 * psl + 1][:], AF.Silu))
                    sil_l.append(silv)
                    E("scalar", hsil)

                    hdeps = [(sO.h, silv)]
                    if p == 0 and pe >= 1:
                        hdeps.append((tP.h, dn_l[8 * pe - 1]))
                    Ph1 = pV(*hdeps)
                    Ph2 = pV()

                    def hmul(eng, p=p, psl=psl, Ph1=Ph1, Ph2=Ph2):
                        eV(eng, Ph1, lambda: eng.tensor_tensor(
                            hbuf[:, p, 0:512], silt[:, 0, :], PS[4 * psl + 2][:],
                            op=ALU.mult))
                        eV(eng, Ph2, lambda: eng.tensor_tensor(
                            hbuf[:, p, 512:1024], silt[:, 1, :], PS[4 * psl + 3][:],
                            op=ALU.mult))
                    hm_l.append(vO.n)
                    E("vector", hmul)
                    gp_cnt += 1

                Pdl = pGD((tP.h, dn_l[8 * pe - 1]) if pe >= 1 else (tP.h, 0))

                def dload(eng, kind=kind, e=e, Pdl=Pdl):
                    eGDpre(eng, Pdl)
                    for ic in range(11):
                        dsrc = (wdn[e, ic * 128:(ic + 1) * 128, :] if kind == 0
                                else sdn[e * 1408 + ic * 128:e * 1408 + (ic + 1) * 128, :])
                        i = eng.dma_start(out=dslab[:, ic, :], in_=dsrc)
                        eGDw(i, Pdl)
                        i.then_inc(dD.h, 16)
                dslv = dD.inc(16 * 11)
                E("gpsimd", dload)

                hall = hm_l[-1]
                yv_base = len(yev_l)
                dn0 = tP.n
                Py = []
                for tt in range(8):
                    Py.append(pV((tP.h, dn0 + tt + 1)))
                    Py.extend(pV() for _ in range(3))
                    yev_l.append(vO.n)
                Ptt = []
                for tt in range(8):
                    gi = yv_base + tt - 2
                    ydep = (vO.h, yev_l[gi]) if gi >= 0 else (vO.h, yz_v)
                    if tt == 0:
                        Ptt.append(pT_((dD.h, dslv), (vO.h, hall), ydep))
                    else:
                        Ptt.append(pT_(ydep))

                def dnm(eng, Ptt=tuple(Ptt)):
                    for tt in range(8):
                        bs = (tt % 2) * 4
                        eTpre(eng, Ptt[tt])
                        for ic in range(11):
                            for fc in range(4):
                                i = eng.matmul(PS[bs + fc][:],
                                               hbuf[:, ic, tt * 128:(tt + 1) * 128],
                                               dslab[:, ic, fc * 512:(fc + 1) * 512],
                                               start=(ic == 0), stop=(ic == 10),
                                               skip_group_check=True)
                                eTw(i, Ptt[tt])
                        i.then_inc(tP.h, 1)
                for tt in range(8):
                    dn_l.append(dn0 + tt + 1)
                tP.inc(8)
                E("tensor", dnm)

                def yev(eng, kind=kind, e=e, Py=tuple(Py)):
                    pi = 0
                    for tt in range(8):
                        bs = (tt % 2) * 4
                        for fc in range(4):
                            if kind == 0:
                                eV(eng, Py[pi], lambda tt=tt, fc=fc, bs=bs:
                                   eng.scalar_tensor_tensor(
                                       ybuf[:, tt, fc * 512:(fc + 1) * 512],
                                       PS[bs + fc][:], comb[:, tt, e:e + 1],
                                       ybuf[:, tt, fc * 512:(fc + 1) * 512],
                                       op0=ALU.mult, op1=ALU.add))
                            else:
                                eV(eng, Py[pi], lambda tt=tt, fc=fc, bs=bs:
                                   eng.tensor_tensor(
                                       ybuf[:, tt, fc * 512:(fc + 1) * 512],
                                       ybuf[:, tt, fc * 512:(fc + 1) * 512],
                                       PS[bs + fc][:], op=ALU.add))
                            pi += 1
                E("vector", yev)

                if pe == 5:
                    Pr = pSY((cc.h, cc.n))

                    def reshp(eng, Pr=Pr):
                        eSYpre(eng, Pr)
                        for gg in range(4):
                            i = eng.dma_start(
                                out=cin3[:, gg * 512:(gg + 1) * 512, :],
                                in_=cout[gg, :, :].rearrange("r (m t) -> m r t", m=4))
                            eSYw(i, Pr)
                            i.then_inc(dR.h, 16)
                    cpv = dR.inc(64)
                    E("sync", reshp)

                    def rsc(eng, cpv=cpv):
                        i = eng.collective_compute(
                            "ReduceScatter", ALU.max,
                            replica_groups=[[0, 1, 2, 3], [4, 5, 6, 7]],
                            ins=[cin3[:, :, :]], outs=[cfin[:, :]])
                        _w(i, dR.h, cpv)
                        i.then_inc(cc.h, 1)
                    cc.inc()
                    E("gpsimd", rsc)

            yfin = yev_l[-1]
            Pyd = pGD((vO.h, yfin))

            def ydma(eng, Pyd=Pyd):
                eGDpre(eng, Pyd)
                i = eng.dma_start(out=ydr[:, :].rearrange("(t p) c -> p t c", p=128),
                                  in_=ybuf[:])
                eGDw(i, Pyd)
                i.then_inc(dY.h, 16)
            dY.inc(16)
            E("gpsimd", ydma)

            Pf = pG((dY.h, dY.n))

            def fin2(eng, Pf=Pf):
                eG(eng, Pf, lambda: eng.memset(finscr[0:1, 0:1], 0.0))
            E("gpsimd", fin2)
            run_block()

        with ExitStack() as ph:
            PS = [ph.enter_context(nc.psum_tensor(f"ob{i}", [128, 512], F32))
                  for i in range(8)]
            orc = ph.enter_context(nc.sbuf_tensor("orc", [128, KC, TOK], BF16))
            wos = ph.enter_context(nc.sbuf_tensor("wos", [128, KC, H], BF16))
            bos = ph.enter_context(nc.sbuf_tensor("bos", [128, H], F32))
            xo3 = ph.enter_context(nc.sbuf_tensor("xo3", [128, 2, H], F32))
            st3 = ph.enter_context(nc.sbuf_tensor("st3", [128, 2, H], F32))
            gt3 = ph.enter_context(nc.sbuf_tensor("gt3", [128, 2, H], F32))
            yt3 = ph.enter_context(nc.sbuf_tensor("yt3", [128, 2, H], F32))

            def wload(eng):
                for fc in range(KC):
                    eng.dma_start(out=wos[:, fc, :],
                                  in_=wo[fc * 128:(fc + 1) * 128, :]).then_inc(dWo.h, 16)
            wlv = dWo.inc(16 * KC)
            E("gpsimd", wload)

            Pol = pSY((cc.h, cc.n))

            def oload(eng, Pol=Pol):
                eSYpre(eng, Pol)
                i = eng.dma_start(out=orc[:],
                                  in_=cfin[:, :].rearrange("(c p) t -> p c t", p=128))
                eSYw(i, Pol)
                i.then_inc(dOr.h, 16)
                eng.dma_start(out=bos[:], in_=bob[:, :]).then_inc(dOr.h, 16)
            olv = dOr.inc(32)
            E("sync", oload)

            sev_l3, sg_l3, outst_l = [], [], []
            for tt in range(8):
                sl = tt % 2

                xdeps = []
                if tt >= 2:
                    xdeps.append((vO.h, sev_l3[tt - 2]))
                    xdeps.append((sO.h, sg_l3[tt - 2]))
                Pxl = pSY(*xdeps)

                def xload(eng, tt=tt, sl=sl, Pxl=Pxl):
                    eSYpre(eng, Pxl)
                    i = eng.dma_start(out=xo3[:, sl, :],
                                      in_=xown[tt * 128:(tt + 1) * 128, :])
                    eSYw(i, Pxl)
                    i.then_inc(dX[sl].h, 16)
                    i = eng.dma_start(out=yt3[:, sl, :],
                                      in_=ydr[tt * 128:(tt + 1) * 128, :])
                    eSYw(i, Pxl)
                    i.then_inc(dX[sl].h, 16)
                xlv = dX[sl].inc(32)
                E("sync", xload)

                mdeps = [(dWo.h, wlv), (dOr.h, olv)]
                if tt >= 2:
                    mdeps.append((vO.h, sev_l3[tt - 2]))
                Pm3 = pT_(*mdeps)

                def smm3(eng, tt=tt, Pm3=Pm3):
                    eTpre(eng, Pm3)
                    bs = (tt % 2) * 4
                    for fc in range(KC):
                        for oc in range(4):
                            i = eng.matmul(PS[bs + oc][:],
                                           orc[:, fc, tt * 128:(tt + 1) * 128],
                                           wos[:, fc, oc * 512:(oc + 1) * 512],
                                           start=(fc == 0), stop=(fc == KC - 1),
                                           skip_group_check=True)
                            eTw(i, Pm3)
                    i.then_inc(tP.h, 1)
                smv = tP.inc()
                E("tensor", smm3)

                gdeps = [(dX[sl].h, xlv)]
                if tt >= 2:
                    gdeps.append((vO.h, sev_l3[tt - 2]))
                Pgs = pS(*gdeps)
                sgv = sO.n
                sg_l3.append(sgv)

                def gsil(eng, sl=sl, Pgs=Pgs):
                    eS(eng, Pgs, lambda: eng.activation(
                        gt3[:, sl, :], yt3[:, sl, :], AF.Silu))
                E("scalar", gsil)

                cdeps = [(tP.h, smv), (dX[sl].h, xlv), (sO.h, sgv)]
                if tt >= 2:
                    cdeps.append((dU[tt % 2].h, outst_l[tt - 2]))
                Pc = [pV(*cdeps)] + [pV() for _ in range(5)]
                sevv = vO.n

                def sevc(eng, tt=tt, sl=sl, Pc=tuple(Pc)):
                    bs = (tt % 2) * 4
                    for oc in range(4):
                        eV(eng, Pc[oc], lambda oc=oc: eng.tensor_tensor(
                            st3[:, sl, oc * 512:(oc + 1) * 512], PS[bs + oc][:],
                            bos[:, oc * 512:(oc + 1) * 512], op=ALU.add))
                    eV(eng, Pc[4], lambda: eng.tensor_tensor(
                        st3[:, sl, :], st3[:, sl, :], gt3[:, sl, :], op=ALU.mult))
                    eV(eng, Pc[5], lambda: eng.tensor_tensor(
                        st3[:, sl, :], st3[:, sl, :], xo3[:, sl, :], op=ALU.add))
                sev_l3.append(sevv)
                E("vector", sevc)

                Pou = pGD((vO.h, sevv))

                def ostore(eng, tt=tt, sl=sl, Pou=Pou):
                    eGDpre(eng, Pou)
                    i = eng.dma_start(out=out[tt * 128:(tt + 1) * 128, :],
                                      in_=st3[:, sl, :])
                    eGDw(i, Pou)
                    i.then_inc(dU[tt % 2].h, 16)
                ov = dU[tt % 2].inc(16)
                outst_l.append(ov)
                E("gpsimd", ostore)

            Pf = pG((dU[0].h, dU[0].n), (dU[1].h, dU[1].n))

            def finout(eng, Pf=Pf):
                eG(eng, Pf, lambda: eng.memset(finscr[0:1, 0:1], 0.0))
            E("gpsimd", finout)
            run_block()

    return nc


def kernel(L0_attn_x, L0_attn_Wqkv, L0_attn_bqkv, L0_attn_Wo, L0_attn_bo,
           L0_ffn_gate_w, L0_ffn_w_gate_up, L0_ffn_w_down,
           L0_ffn_shared_gate_up, L0_ffn_shared_down):
    import ml_dtypes
    x = np.ascontiguousarray(np.asarray(L0_attn_x, np.float32))
    Wqkv = np.asarray(L0_attn_Wqkv, np.float32)
    bqkv = np.asarray(L0_attn_bqkv, np.float32)
    Wo = np.ascontiguousarray(np.asarray(L0_attn_Wo, np.float32))
    bo = np.asarray(L0_attn_bo, np.float32)
    gate_w = np.asarray(L0_ffn_gate_w, np.float32)
    w_gu = np.asarray(L0_ffn_w_gate_up, np.float32)
    w_dn = np.ascontiguousarray(np.asarray(L0_ffn_w_down, np.float32))
    s_gu = np.asarray(L0_ffn_shared_gate_up, np.float32)
    s_dn = np.ascontiguousarray(np.asarray(L0_ffn_shared_down, np.float32))

    slopes = np.array([1.0 / 2 ** (k / 2) for k in range(NH)], np.float32)
    jj = np.arange(S, dtype=np.float32)

    wgu_p = np.ascontiguousarray(
        w_gu.reshape(16, H, 2, 11, 128).transpose(0, 1, 3, 2, 4).reshape(16, H, 11, 256))
    sgu_p = np.ascontiguousarray(
        s_gu.reshape(H, 2, 22, 128).transpose(0, 2, 1, 3).reshape(H, 22, 256))
    gwt = np.ascontiguousarray(gate_w.T)
    bob = np.ascontiguousarray(np.broadcast_to(bo[None, :], (128, H)))
    ii = np.arange(128, dtype=np.float32)
    mtri = np.ascontiguousarray(
        np.where(np.arange(128)[None, :] <= ii[:, None], np.float32(0.0),
                 np.float32(NEG)))
    iden = np.eye(128, dtype=np.float32).astype(ml_dtypes.bfloat16)

    in_maps = []
    for c in range(8):
        b, hg = c // 4, c % 4
        heads = slice(hg * 512, (hg + 1) * 512)
        wq_slice = np.ascontiguousarray(np.concatenate(
            [Wqkv[:, heads], Wqkv[:, H:][:, heads], Wqkv[:, 2 * H:][:, heads]], axis=1))
        bq_s = np.ascontiguousarray(bqkv[:H][heads].reshape(4, 128).T)
        bk_s = np.ascontiguousarray(bqkv[H:2 * H][heads].reshape(4, 128).T)
        bv_s = np.ascontiguousarray(
            np.broadcast_to(bqkv[2 * H:][heads][None, :], (128, 512)))
        jr = np.ascontiguousarray(np.broadcast_to(
            (slopes[4 * hg:4 * hg + 4, None] * jj[None, :])[:, None, :],
            (4, 128, S)))
        iglob = (np.arange(32, dtype=np.float32)[None, :] * 128.0 +
                 np.arange(128, dtype=np.float32)[:, None])
        ar = np.ascontiguousarray(
            (-slopes[4 * hg:4 * hg + 4, None, None] * iglob[None, :, :]
             - np.float32(20.0)).astype(np.float32))
        in_maps.append({
            "xb": x[b],
            "xown": np.ascontiguousarray(x[b, hg * TOK:(hg + 1) * TOK]),
            "wqkv": wq_slice,
            "bq": bq_s, "bk": bk_s, "bv": bv_s,
            "jrow": jr, "arow": ar, "mtri": mtri, "iden": iden,
            "wo": Wo, "bob": bob,
            "gwt": gwt,
            "wgu": wgu_p, "wdn": w_dn,
            "sgu": sgu_p, "sdn": s_dn,
        })

    nc = build_nc()
    res = run_bass_kernel_spmd(nc, in_maps, core_ids=list(range(8)))

    full = np.empty((B, S, H), np.float32)
    for c in range(8):
        b, hg = c // 4, c % 4
        full[b, hg * TOK:(hg + 1) * TOK] = res.results[c]["out"]
    return full
